# version 10
# baseline (speedup 1.0000x reference)
import numpy as np
from contextlib import ExitStack

import concourse.bass as bass
import concourse.mybir as mybir
from concourse.bass_utils import run_bass_kernel_spmd

F32 = mybir.dt.float32
BF16 = mybir.dt.bfloat16
I32 = mybir.dt.int32
AF = mybir.ActivationFunctionType
ALU = mybir.AluOpType
AXX = mybir.AxisListType.X

D = 4096
DC = 32
S = 4096
TT = 8192
NT = 8
HCH_N = 22
HSN = HCH_N * 128
SC0 = float(1.0 / np.sqrt(np.float32(128.0)))
SC1 = 1.0 / 32.0
EPS = 1e-5
NEG = -1e9

NF4 = [-1.0, -0.6961928009986877, -0.5250730514526367, -0.39491748809814453,
       -0.28444138169288635, -0.18477343022823334, -0.09105003625154495, 0.0,
       0.07958029955625534, 0.16093020141124725, 0.24611230194568634,
       0.33791524171829224, 0.44070982933044434, 0.5626170039176941,
       0.7229568362236023, 1.0]

OWN_START = [0, 1024, 2048, 3072, 4096, 5120, 6144, 7168]
MYSEQ = [0, 0, 0, 0, 1, 1, 1, 1]
RS1_GROUPS = [[0, 1, 2, 3], [4, 5, 6, 7]]
AG_GROUPS = [[0, 1, 2, 3], [4, 5, 6, 7]]

ENGS = ("t", "v", "s", "g", "y")

_STOP_AFTER = None
_DEBUG_OUT = set()
_P12_PARTS = 7
_LAST_RESULTS = None


class Bld:
    def __init__(self, nc, es):
        self.nc = nc
        self.es = es
        self.ops = {e: [] for e in ENGS}
        self.semh = {}
        self.cnt = {}
        self.last_wait = {e: {} for e in ENGS}

    def sem(self, name):
        if name not in self.semh:
            self.semh[name] = self.es.enter_context(self.nc.semaphore(name))
            self.cnt[name] = 0
        return self.semh[name]

    def wait(self, eng, tok):
        if tok is None:
            return
        sname, val = tok
        if val <= 0 or self.last_wait[eng].get(sname, 0) >= val:
            return
        self.last_wait[eng][sname] = val
        h = self.sem(sname)
        self.ops[eng].append(lambda E, h=h, v=val: E.wait_ge(h, v))

    def step(self, eng, sname, build, deps=(), dma=False):
        for d in deps:
            self.wait(eng, d)
        self.sem(sname)
        amt = 16 if dma else 1
        self.cnt[sname] += amt
        v = self.cnt[sname]
        h = self.semh[sname]
        self.ops[eng].append(lambda E, h=h, a=amt, bf=build: bf(E).then_inc(h, a))
        return (sname, v)

    def run(self, eng, build, deps=()):
        for d in deps:
            self.wait(eng, d)
        self.ops[eng].append(build)

    def barrier(self):
        toks = [(s, c) for s, c in self.cnt.items() if c > 0]
        for e in ENGS:
            for t in toks:
                self.wait(e, t)


def build_program():
    nc = bass.Bass("TRN2", target_bir_lowering=False, debug=False)
    es = ExitStack()
    b = Bld(nc, es)

    def din(name, shape, dt=F32):
        return nc.dram_tensor(name, list(shape), dt, kind="ExternalInput").ap()

    xall = din("xall", [S, D])
    xown = din("xown", [1024, D])
    wq = din("wq", [D, 1024]); wk = din("wk", [D, 128]); wv = din("wv", [D, 128])
    wo = din("wo", [1024, D])
    cup = din("cup", [HSN, D], I32); aup = din("aup", [HSN, 64])
    bup = din("bup", [HCH_N, 128])
    laup = din("laup", [D, 16]); lbup = din("lbup", [16, HSN])
    cdn = din("cdn", [D, HSN], I32); adn = din("adn", [D, HSN // 64])
    ldna = din("ldna", [HSN, 16]); ldnb = din("ldnb", [16, D])
    bdn8 = din("bdn8", [1, 4096])
    wqkv = din("wqkv", [D, 3072]); bqkv = din("bqkv", [24, 128])
    wo1 = din("wo1", [1024, D]); bo8 = din("bo8", [1, 4096])
    gw = din("gw", [8, D])
    la1 = din("la1", [512, D])
    lb1 = din("lb1", [8 * D, 64])
    ident_in = din("identI", [128, 128])
    ones_in = din("onesI", [1, 128])
    rmat_in = din("rmatI", [8, 512])

    yout = nc.dram_tensor("y", [1024, D], F32, kind="ExternalOutput").ap()

    dbg_copies = []

    def dint(name, shape, dt=F32):
        ap = nc.dram_tensor(name, list(shape), dt, kind="Internal").ap()
        if name in _DEBUG_OUT:
            dap = nc.dram_tensor(name + "_dbg", list(shape), dt,
                                 kind="ExternalOutput").ap()
            dbg_copies.append((ap, dap, list(shape)))
        return ap

    WUPT = dint("WUPT", [D, HSN])
    WDNT = dint("WDNT", [HSN, D])
    BF = BF16
    WUPTH = dint("WUPTH", [D, HSN], BF); WUPTL = dint("WUPTL", [D, HSN], BF)
    WDNTH = dint("WDNTH", [HSN, D], BF); WDNTL = dint("WDNTL", [HSN, D], BF)
    WQH = dint("WQH", [D, 1024], BF); WQL = dint("WQL", [D, 1024], BF)
    WKH = dint("WKH", [D, 128], BF); WKL = dint("WKL", [D, 128], BF)
    WVH = dint("WVH", [D, 128], BF); WVL = dint("WVL", [D, 128], BF)
    WOH = dint("WOH", [1024, D], BF); WOL = dint("WOL", [1024, D], BF)
    WQKVH = dint("WQKVH", [D, 3072], BF); WQKVL = dint("WQKVL", [D, 3072], BF)
    LAUPH = dint("LAUPH", [D, 16], BF); LAUPL = dint("LAUPL", [D, 16], BF)
    AATH = dint("AATH", [D, 512], BF); AATL = dint("AATL", [D, 512], BF)
    QSH = dint("QSH", [1024, S], BF); QSL = dint("QSL", [1024, S], BF)
    AOSH = dint("AOSH", [1024, S], BF); AOSL = dint("AOSL", [1024, S], BF)
    X2SH = dint("X2SH", [HSN, S], BF); X2SL = dint("X2SL", [HSN, S], BF)
    QS = dint("QS", [1024, S])
    AOS = dint("AOS", [1024, S])
    APS = dint("APS", [S, D])
    X2S = dint("X2S", [HSN, S])
    PARTQ = [dint(f"PARTQ{i}", [S, 1024]) for i in range(4)]
    R2OUTQ = [dint(f"R2OUTQ{i}", [1024, 1024]) for i in range(4)]
    X1S = dint("X1S", [1024, D])
    QS1 = dint("QS1", [1024, 1024])
    AGIN = dint("AGIN", [2048, 1024])
    AGOUT = dint("AGOUT", [8192, 1024])
    AAT = dint("AAT", [D, 512])
    BTS = dint("BTS", [512, D])

    def sbt(stk, name, shape, dt=F32, side=None):
        return stk.enter_context(nc.sbuf_tensor(name, list(shape), dt, side=side))

    ident = sbt(es, "ident", [128, 128])
    identB = sbt(es, "identB", [128, 128], BF16)
    trimask = sbt(es, "trimask", [128, 128])
    ones = sbt(es, "ones", [1, 128])
    s_bup = sbt(es, "s_bup", [128, HCH_N])
    s_lbup = sbt(es, "s_lbup", [16, HSN])

    s_bqkv = sbt(es, "s_bqkv", [128, 24])

    s_gwtH = sbt(es, "s_gwtH", [128, 256], BF16)
    s_gwtL = sbt(es, "s_gwtL", [128, 256], BF16)
    keep01 = sbt(es, "keep01", [128, 128])
    sstat = sbt(es, "sstat", [128, 16])
    PS = es.enter_context(nc.psum_tensor("PS", [128, 4096], F32))

    bank_tok = [None] * 8

    def psb(k, w=512):
        return PS[:, 512 * k: 512 * k + w]

    def load(slot, dst, src, deps=(), slow=False):
        return b.step("y", slot, lambda E, d=dst, s=src, sl=slow: E.dma_start(
            out=d, in_=s, allow_slow_non_contiguous=sl), deps=deps, dma=True)

    def store(slot, dst, src, deps=()):
        return b.step("g", slot, lambda E, d=dst, s=src: E.dma_start(out=d, in_=s),
                      deps=deps, dma=True)

    def mm_group(bank, emitters, deps):
        dd = [d for d in deps if d is not None]
        if bank_tok[bank] is not None:
            dd.append(bank_tok[bank])
            bank_tok[bank] = None
        nmm = len(emitters)
        tok = None
        for i, em in enumerate(emitters):
            st, sp = (i == 0), (i == nmm - 1)
            if sp:
                tok = b.step("t", "t", lambda E, em=em, st=st: em(E, st, True),
                             deps=(dd if st else ()))
            else:
                b.run("t", lambda E, em=em, st=st: em(E, st, False),
                      deps=(dd if st else ()))
        return tok

    def transpose_group(bank, blocks, deps):
        ems = []
        for (srcap, ooff) in blocks:
            isbf = (srcap.dtype == BF16)
            def em(E, st, sp, s=srcap, o=ooff, bk=bank, isbf=isbf):
                pn = s.shape[0]
                if isbf:
                    base = PS[:, 512 * bk:512 * bk + 256].bitcast(BF16)
                    return E.matmul(base[0:s.shape[1], o:o + pn], s,
                                    identB[0:pn, 0:pn],
                                    is_transpose=True, start=st, stop=sp)
                return E.matmul(PS[0:s.shape[1], 512 * bk + o: 512 * bk + o + pn],
                                s, ident[0:pn, 0:pn],
                                is_transpose=True, start=st, stop=sp)
            ems.append(em)
        return mm_group(bank, ems, deps)

    t_id = load("d_c", ident[:], ident_in[:, :])
    load("d_c", ones[:], ones_in[:, :])
    t_bup = load("d_c", s_bup[:].rearrange("p (c o) -> p c o", o=1),
                 bup[:, :].rearrange("c (p o) -> p c o", o=1), slow=True)
    t_lbup = load("d_c", s_lbup[:], lbup[:, :])

    t_bqkv = load("d_c", s_bqkv[:].rearrange("p (c o) -> p c o", o=1),
                  bqkv[:, :].rearrange("c (p o) -> p c o", o=1), slow=True)

    t_idb = b.step("v", "v", lambda E: E.tensor_copy(identB[:], ident[:]), deps=[t_id])
    t_io = b.step("g", "g", lambda E: E.iota(keep01[:], pattern=[[-1, 128]], base=0,
                                             channel_multiplier=1,
                                             allow_small_or_imprecise_dtypes=True))
    t_tm0 = b.step("v", "v", lambda E: E.tensor_scalar(trimask[:], keep01[:], 0.0, NEG,
                                                       ALU.is_ge, ALU.mult), deps=[t_io])
    b.step("v", "v", lambda E: E.tensor_scalar(keep01[:], keep01[:], 0.0, 1.0,
                                               ALU.is_lt, ALU.mult), deps=[t_tm0])
    b.barrier()

    es_pw = ExitStack()
    PW32 = sbt(es_pw, "pw32", [128, 2 * 4096])
    PWH = sbt(es_pw, "pwh", [128, 4096], BF16)
    PWL = sbt(es_pw, "pwl", [128, 4096], BF16)
    pw_free = [None, None]
    pwh_free = None
    pwl_free = None

    def presplit(src_ap, hi_ap, lo_ap, nrows, ncols):
        nonlocal_state = None
        for r0 in range(0, nrows, 128):
            sl = (r0 // 128) % 2
            stg = PW32[:, sl * 4096: sl * 4096 + ncols]
            ld = load(f"d_pw{sl}", stg, src_ap[r0:r0 + 128, :], deps=[pw_free[sl]])
            hi = b.step("s", "s", lambda E, st=stg, n=ncols: E.copy(PWH[:, 0:n], st),
                        deps=[ld, b._pwh_free])
            lo = b.step("g", "g", lambda E, st=stg, n=ncols: E.tensor_tensor(
                PWL[:, 0:n], st, PWH[:, 0:n], ALU.subtract), deps=[hi, b._pwl_free])
            pw_free[sl] = lo
            b._pwh_free = store("d_pwo", hi_ap[r0:r0 + 128, :], PWH[:, 0:ncols], deps=[hi])
            b._pwl_free = store("d_pwo", lo_ap[r0:r0 + 128, :], PWL[:, 0:ncols], deps=[lo])

    b._pwh_free = None
    b._pwl_free = None
    presplit(wq, WQH, WQL, D, 1024)
    presplit(wk, WKH, WKL, D, 128)
    presplit(wv, WVH, WVL, D, 128)
    presplit(wo, WOH, WOL, 1024, D)
    presplit(wqkv, WQKVH, WQKVL, D, 3072)
    presplit(laup, LAUPH, LAUPL, D, 16)


    es_pd = ExitStack()
    A1 = sbt(es_pd, "pdA1", [128, 16384])
    EV = sbt(es_pd, "pdEV", [128, 4096])
    EVH = sbt(es_pd, "pdEVH", [128, 1024], BF16)
    EVL = sbt(es_pd, "pdEVL", [128, 1024], BF16)
    AMB = sbt(es_pd, "pdAMB", [128, 64])
    AMBG = sbt(es_pd, "pdAMBG", [128, 64])
    DQAG = sbt(es_pd, "pdDQAG", [128, 4096])
    DQTG = sbt(es_pd, "pdDQTG", [128, 4096])
    cbuf = [A1[:, 0:4096].bitcast(I32), A1[:, 4096:8192].bitcast(I32)]
    dqa = A1[:, 8192:12288]
    dqt = A1[:, 12288:16384]
    cslot_free = [None, None]
    am_free = None
    dqa_free = None
    ev_free = [None, None]

    def dequant_on(eng, acc, tmp, amb2, cd, ncols, nblk, dep_ld, dep_am, dep_free):
        tok = b.step(eng, eng, lambda E, c=cd, n=ncols, a=acc: E.tensor_scalar(
            a[:, 0:n], c, 0.0, NF4[0], ALU.is_equal, ALU.mult),
            deps=[dep_ld, dep_free])
        for k in range(1, 16):
            tk = b.step(eng, eng, lambda E, c=cd, n=ncols, t=tmp, kk=float(k), vv=NF4[k]:
                        E.tensor_scalar(t[:, 0:n], c, kk, vv, ALU.is_equal, ALU.mult),
                        deps=[tok])
            tok = b.step(eng, eng, lambda E, n=ncols, a=acc, t=tmp: E.tensor_tensor(
                a[:, 0:n], a[:, 0:n], t[:, 0:n], ALU.add), deps=[tk])
        tok = b.step(eng, eng, lambda E, n=ncols, nb=nblk, a=acc, m=amb2: E.tensor_tensor(
            a[:, 0:n].rearrange("p (blk e) -> p blk e", e=64),
            a[:, 0:n].rearrange("p (blk e) -> p blk e", e=64),
            m[:, 0:nb].unsqueeze(2).broadcast_to((128, nb, 64)), ALU.mult),
            deps=[tok, dep_am])
        return tok

    def dequant(cd, ncols, nblk, dep_ld, dep_am, dep_free):
        return dequant_on("v", dqa, dqt, AMB, cd, ncols, nblk, dep_ld, dep_am, dep_free)

    def pd_sweep(nsrc, src_cols, nblocks, blkw, dst_fn, dep_src_fn, am_src_fn):
        pass

    amg_free = None
    dqg_free = None
    for hc in range(HCH_N):
        sl = hc % 2
        use_g = False
        ldk = load(f"d_cd{sl}", cbuf[sl][:, :], cup[128 * hc:128 * (hc + 1), :],
                   deps=[cslot_free[sl]])
        if use_g:
            lda = load("d_amg", AMBG[:, 0:64], aup[128 * hc:128 * (hc + 1), :],
                       deps=[amg_free])
            dq = dequant_on("g", DQAG, DQTG, AMBG, cbuf[sl][:, :], 4096, 64,
                            ldk, lda, dqg_free)
            amg_free = dq
            dqg_free = None
        else:
            lda = load("d_am", AMB[:, 0:64], aup[128 * hc:128 * (hc + 1), :],
                       deps=[am_free])
            dq = dequant(cbuf[sl][:, :], 4096, 64, ldk, lda, dqa_free)
            am_free = dq
        cslot_free[sl] = dq
        accb = DQAG if use_g else dqa
        last_tp = None
        for g in range(8):
            bank = g % 4
            blocks = [(accb[:, (4 * g + i) * 128:(4 * g + i + 1) * 128], i * 128)
                      for i in range(4)]
            tp = transpose_group(bank, blocks, deps=[dq])
            sh = EVH[:, (g % 2) * 512:(g % 2) * 512 + 512]
            slo = EVL[:, (g % 2) * 512:(g % 2) * 512 + 512]
            ehk = b.step("s", "s", lambda E, bk=bank, st=sh: E.copy(st, psb(bk)),
                         deps=[tp, ev_free[g % 2]])
            elk = b.step("v", "v", lambda E, bk=bank, st=slo, hh=sh: E.tensor_tensor(
                st, psb(bk), hh, ALU.subtract), deps=[ehk])
            bank_tok[bank] = elk
            dsth = WUPTH.rearrange("(c p) h -> p c h", p=128)[:, 4 * g:4 * g + 4,
                                                              128 * hc:128 * (hc + 1)]
            dstl = WUPTL.rearrange("(c p) h -> p c h", p=128)[:, 4 * g:4 * g + 4,
                                                              128 * hc:128 * (hc + 1)]
            store(f"d_w{g % 2}", dsth, sh.rearrange("p (c f) -> p c f", f=128), deps=[ehk])
            ev_free[g % 2] = store(f"d_w{g % 2}", dstl,
                                   slo.rearrange("p (c f) -> p c f", f=128), deps=[elk])
            last_tp = tp
        if use_g:
            dqg_free = last_tp
        else:
            dqa_free = last_tp

    JCD = sbt(es_pd, "pdJCD", [128, 2816], I32)
    JAM = sbt(es_pd, "pdJAM", [128, 44])
    JEVH = sbt(es_pd, "pdJEVH", [128, 1024], BF16)
    JEVL = sbt(es_pd, "pdJEVL", [128, 1024], BF16)
    jstate = {"cd": [None, None], "am": [None, None],
              "acc": [dqa_free, dqg_free], "ev": [None, None]}

    def pd_down_job(jid, eng):
        dc, half = jid // 2, jid % 2
        hlo_c = 0 if half == 0 else 11
        nhc = 11
        hw = nhc * 128
        blo = hlo_c * 2
        nblk = nhc * 2
        gi = 0 if eng == "v" else 1
        cdj = JCD[:, gi * 1408: gi * 1408 + hw]
        amj = JAM[:, gi * 22: gi * 22 + nblk]
        acc = dqa if eng == "v" else DQAG
        tmp = dqt if eng == "v" else DQTG
        ldk = load(f"d_jcd{gi}", cdj,
                   cdn[128 * dc:128 * (dc + 1), hlo_c * 128: hlo_c * 128 + hw],
                   deps=[jstate["cd"][gi]])
        lda = load(f"d_jam{gi}", amj, adn[128 * dc:128 * (dc + 1), blo:blo + nblk],
                   deps=[jstate["am"][gi]])
        dq = dequant_on(eng, acc, tmp, JAM if False else amj.unsqueeze(0).squeeze(0), hw, nblk, ldk, lda, jstate["acc"][gi]) if False else None
        tok = b.step(eng, eng, lambda E, c=cdj, n=hw, a=acc: E.tensor_scalar(
            a[:, 0:n], c, 0.0, NF4[0], ALU.is_equal, ALU.mult),
            deps=[ldk, jstate["acc"][gi]])
        for k in range(1, 16):
            tk = b.step(eng, eng, lambda E, c=cdj, n=hw, t=tmp, kk=float(k), vv=NF4[k]:
                        E.tensor_scalar(t[:, 0:n], c, kk, vv, ALU.is_equal, ALU.mult),
                        deps=[tok])
            tok = b.step(eng, eng, lambda E, n=hw, a=acc, t=tmp: E.tensor_tensor(
                a[:, 0:n], a[:, 0:n], t[:, 0:n], ALU.add), deps=[tk])
        dq = b.step(eng, eng, lambda E, n=hw, nb=nblk, a=acc, m=amj: E.tensor_tensor(
            a[:, 0:n].rearrange("p (blk e) -> p blk e", e=64),
            a[:, 0:n].rearrange("p (blk e) -> p blk e", e=64),
            m.unsqueeze(2).broadcast_to((128, nb, 64)), ALU.mult),
            deps=[tok, lda])
        jstate["cd"][gi] = dq
        jstate["am"][gi] = dq
        last_tp = None
        for g in range(3):
            bank = (4 + 2 * gi) % 8 + (g % 2)
            ilo = 4 * g
            ihi = min(4 * g + 4, nhc)
            if ihi <= ilo:
                break
            blocks = [(acc[:, (ilo + i) * 128:(ilo + i + 1) * 128], i * 128)
                      for i in range(ihi - ilo)]
            tp = transpose_group(bank, blocks, deps=[dq])
            w = (ihi - ilo) * 128
            sh = JEVH[:, gi * 512:gi * 512 + w]
            slo2 = JEVL[:, gi * 512:gi * 512 + w]
            ehk = b.step("s", "s", lambda E, bk=bank, st=sh, w=w: E.copy(
                st, PS[:, 512 * bk:512 * bk + w]), deps=[tp, jstate["ev"][gi]])
            elk = b.step("v", "v", lambda E, bk=bank, st=slo2, hh=sh, w=w: E.tensor_tensor(
                st, PS[:, 512 * bk:512 * bk + w], hh, ALU.subtract), deps=[ehk])
            bank_tok[bank] = elk
            dsth = WDNTH.rearrange("(c p) d -> p c d", p=128)[:, hlo_c + ilo: hlo_c + ihi,
                                                              128 * dc:128 * (dc + 1)]
            dstl = WDNTL.rearrange("(c p) d -> p c d", p=128)[:, hlo_c + ilo: hlo_c + ihi,
                                                              128 * dc:128 * (dc + 1)]
            store(f"d_jw{gi}", dsth, sh.rearrange("p (c f) -> p c f", f=128), deps=[ehk])
            jstate["ev"][gi] = store(f"d_jw{gi}", dstl,
                                     slo2.rearrange("p (c f) -> p c f", f=128), deps=[elk])
            last_tp = tp
        jstate["acc"][gi] = last_tp

    for jid in range(64):
        pd_down_job(jid, "v")


    for erc in range(4):
        sl = erc % 2
        src = cbuf[sl][:, 0:4096].bitcast(F32)
        ldk = load(f"d_cd{sl}", src, la1[128 * erc:128 * (erc + 1), :],
                   deps=[cslot_free[sl]])
        for g in range(8):
            bank = g % 4
            blocks = [(src[:, (4 * g + i) * 128:(4 * g + i + 1) * 128], i * 128)
                      for i in range(4)]
            tp = transpose_group(bank, blocks, deps=[ldk])
            sh = EVH[:, (g % 2) * 512:(g % 2) * 512 + 512]
            slo = EVL[:, (g % 2) * 512:(g % 2) * 512 + 512]
            ehk = b.step("s", "s", lambda E, bk=bank, st=sh: E.copy(st, psb(bk)),
                         deps=[tp, ev_free[g % 2]])
            elk = b.step("v", "v", lambda E, bk=bank, st=slo, hh=sh: E.tensor_tensor(
                st, psb(bk), hh, ALU.subtract), deps=[ehk])
            bank_tok[bank] = elk
            dsth = AATH.rearrange("(c p) e -> p c e", p=128)[:, 4 * g:4 * g + 4,
                                                             128 * erc:128 * (erc + 1)]
            dstl = AATL.rearrange("(c p) e -> p c e", p=128)[:, 4 * g:4 * g + 4,
                                                             128 * erc:128 * (erc + 1)]
            store(f"d_w{g % 2}", dsth, sh.rearrange("p (c f) -> p c f", f=128), deps=[ehk])
            ev_free[g % 2] = store(f"d_w{g % 2}", dstl,
                                   slo.rearrange("p (c f) -> p c f", f=128), deps=[elk])
            cslot_free[sl] = tp

    for e in range(8):
        sl = e % 2
        src = cbuf[sl][:, 0:2048].bitcast(F32)
        ldk = load(f"d_cd{sl}", src.rearrange("p (c r) -> p c r", r=64),
                   lb1[e * D:(e + 1) * D, :].rearrange("(c p) r -> p c r", p=128),
                   deps=[cslot_free[sl]])
        for g in range(8):
            bank = g % 4
            blocks = [(src[:, (4 * g + i) * 64:(4 * g + i + 1) * 64], i * 128)
                      for i in range(4)]
            tp = transpose_group(bank, blocks, deps=[ldk])
            stg = EV[:, (g % 2) * 2048:(g % 2) * 2048 + 512]
            evk = b.step("s", "s", lambda E, bk=bank, st=stg: E.copy(
                st[0:64, :], PS[0:64, 512 * bk:512 * bk + 512]),
                deps=[tp, ev_free[g % 2]])
            bank_tok[bank] = evk
            dst = BTS[64 * e:64 * e + 64, :].rearrange("r (c f) -> r c f", f=128)[:, 4 * g:4 * g + 4, :]
            ev_free[g % 2] = store(f"d_w{g % 2}", dst,
                                   stg[0:64, :].rearrange("p (c f) -> p c f", f=128),
                                   deps=[evk])
            cslot_free[sl] = tp

    src = cbuf[0][:, 0:4096].bitcast(F32)
    ldk = load("d_cd0", src[0:8, :], gw[:, :], deps=[cslot_free[0]])
    for g in range(8):
        bank = g % 4
        blocks = [(src[0:8, (4 * g + i) * 128:(4 * g + i + 1) * 128], i * 8)
                  for i in range(4)]
        tp = transpose_group(bank, blocks, deps=[ldk])
        ehk = b.step("s", "s", lambda E, bk=bank, g=g: E.copy(
            s_gwtH[:, 32 * g:32 * g + 32], PS[:, 512 * bk:512 * bk + 32]), deps=[tp])
        evk = b.step("v", "v", lambda E, bk=bank, g=g: E.tensor_tensor(
            s_gwtL[:, 32 * g:32 * g + 32], PS[:, 512 * bk:512 * bk + 32],
            s_gwtH[:, 32 * g:32 * g + 32], ALU.subtract), deps=[ehk])
        bank_tok[bank] = evk

    b.barrier()
    es_pd.close()
    es_pw.close()
    for i in range(8):
        bank_tok[i] = None

    if _STOP_AFTER == "pd":
        return _finish(nc, es, b, dbg_copies)

    es_kv = ExitStack()
    es_l1 = ExitStack()
    prev_sq = [None]

    def ln_subtile(XT, XSQ, dep_ld):
        t1 = b.step("v", "v", lambda E, X=XT: E.reduce_sum(sstat[:, 0:1], X[:, 0:D], AXX),
                    deps=[dep_ld])
        t2 = b.step("v", "v", lambda E: E.tensor_scalar(
            sstat[:, 1:2], sstat[:, 0:1], -1.0 / D, None, ALU.mult), deps=[t1])
        t3 = b.step("s", "s", lambda E, X=XT, Q=XSQ: E.activation(
            Q[:, 0:D], X[:, 0:D], AF.Square, bias=sstat[:, 1:2], scale=1.0,
            accum_out=sstat[:, 2:3]), deps=[t2, dep_ld, prev_sq[0]])
        prev_sq[0] = t3
        t4 = b.step("v", "v", lambda E: E.tensor_scalar(
            sstat[:, 3:4], sstat[:, 2:3], 1.0 / D, EPS, ALU.mult, ALU.add), deps=[t3])
        t5 = b.step("s", "s", lambda E: E.activation(sstat[:, 4:5], sstat[:, 3:4],
                                                     AF.Sqrt), deps=[t4])
        t6 = b.step("v", "v", lambda E: E.reciprocal(sstat[:, 5:6], sstat[:, 4:5]),
                    deps=[t5])
        t7 = b.step("v", "v", lambda E: E.tensor_tensor(
            sstat[:, 6:7], sstat[:, 1:2], sstat[:, 5:6], ALU.mult), deps=[t6])
        t8 = b.step("v", "v", lambda E, X=XT: E.tensor_scalar(
            X[:, 0:D], X[:, 0:D], sstat[:, 5:6], sstat[:, 6:7], ALU.mult, ALU.add),
            deps=[t7])
        return t8

    es_p1 = ExitStack()
    XNH = sbt(es_p1, "p1XNH", [128, 16384], BF16)
    XNL = sbt(es_p1, "p1XNL", [128, 16384], BF16)
    SLAB = sbt(es_p1, "p1SLAB", [128, 16384], BF16)
    XT = sbt(es_p1, "p1XT", [128, 4096])
    XSQ = sbt(es_p1, "p1XSQ", [128, 4096])
    XH = sbt(es_p1, "p1XH", [128, 4096], BF16)
    XL = sbt(es_p1, "p1XL", [128, 4096], BF16)
    STG = sbt(es_p1, "p1STG", [128, 2048])
    STGB = sbt(es_p1, "p1STGB", [128, 4096], BF16)
    vstg = STG[:, 0:512]
    x2stg = [STG[:, 512:1024], STG[:, 1024:1536]]
    uub = STG[0:16, 1536:2048]
    qstgh = [STGB[:, 0:512], STGB[:, 512:1024]]
    qstgl = [STGB[:, 1024:1536], STGB[:, 1536:2048]]
    KSB = sbt(es_kv, "KSBH", [128, 4096], BF16, side="right")
    KSBL = sbt(es_kv, "KSBL", [128, 4096], BF16, side="right")
    VTM = sbt(es_kv, "VTM", [128, 4096], side="right")
    slab_free = [None, None]
    stg_free = {}
    last_xt_user = [None]
    last_xh_user = [None]
    last_xnf_user = [None]
    p1_bank = [0]

    def psb16(k):
        return PS[:, 512 * k:512 * k + 256].bitcast(BF16)

    ready_h = [None]
    ready_l = [None]

    def make_xnf_tile(src_rows_fn, t0):
        ready = None
        for sub in range(4):
            ldx = load("d_xt", XT[:], src_rows_fn(t0 + sub * 128),
                       deps=[last_xt_user[0]])
            lnk = ln_subtile(XT, XSQ, ldx)
            hk = b.step("s", "s", lambda E: E.copy(XH[:], XT[:]), deps=[lnk, last_xh_user[0]])
            lk = b.step("v", "v", lambda E: E.tensor_tensor(XL[:], XT[:], XH[:],
                                                            ALU.subtract), deps=[hk])
            last_xt_user[0] = lk
            for comp in range(2):
                XSRC = XH if comp == 0 else XL
                XDST = XNH if comp == 0 else XNL
                ctok = hk if comp == 0 else lk
                for g in range(8):
                    bank = 4 + ((2 * g + comp) % 4)
                    blocks = [(XSRC[:, (4 * g + i) * 128:(4 * g + i + 1) * 128], i * 128)
                              for i in range(4)]
                    tp = transpose_group(bank, blocks, deps=[ctok, last_xnf_user[0]])
                    dst = XDST.rearrange("p (c t) -> p c t", t=512)[:, 4 * g:4 * g + 4,
                                                                   sub * 128:(sub + 1) * 128]
                    eng = "s" if comp == 0 else "v"
                    if comp == 0:
                        evk = b.step("s", "s", lambda E, bk=bank, d=dst: E.copy(
                            d, psb16(bk).rearrange("p (c f) -> p c f", f=128)), deps=[tp])
                    else:
                        evk = b.step("v", "v", lambda E, bk=bank, d=dst: E.tensor_copy(
                            d, psb16(bk).rearrange("p (c f) -> p c f", f=128)), deps=[tp])
                    bank_tok[bank] = evk
                    if comp == 0:
                        ready_h[0] = evk
                    else:
                        ready_l[0] = evk
                    last_xh_user[0] = tp
        return (ready_h[0], ready_l[0])

    def fclist(ti):
        return ([("AUP", 0)] + [("Q", j) for j in range(8)] + [("K", 0), ("V", 0)]
                + [("UP", j) for j in range(HCH_N)])

    for ti in range(NT):
        xnf_ready = make_xnf_tile(lambda r: xall[r:r + 128, :], ti * 512)
        uup_evac = None
        mm_tok = None
        for fci, (kind, idx) in enumerate(fclist(ti)):
            sl = fci % 2
            slab = SLAB[:, sl * 8192:(sl + 1) * 8192]
            if kind == "Q":
                srch = WQH[:, 128 * idx:128 * (idx + 1)]
                srcl = WQL[:, 128 * idx:128 * (idx + 1)]; w = 128
            elif kind == "K":
                srch = WKH[:, :]; srcl = WKL[:, :]; w = 128
            elif kind == "V":
                srch = WVH[:, :]; srcl = WVL[:, :]; w = 128
            elif kind == "AUP":
                srch = LAUPH[:, :]; srcl = LAUPL[:, :]; w = 16
            else:
                srch = WUPTH[:, 128 * idx:128 * (idx + 1)]
                srcl = WUPTL[:, 128 * idx:128 * (idx + 1)]; w = 128
            load(f"d_sl{sl}", slab[:, 0:32 * w].rearrange("p (c f) -> p c f", f=w),
                 srch.rearrange("(c p) f -> p c f", p=128), deps=[slab_free[sl]])
            slt = load(f"d_sl{sl}",
                       slab[:, 32 * w:64 * w].rearrange("p (c f) -> p c f", f=w),
                       srcl.rearrange("(c p) f -> p c f", p=128))
            bank = p1_bank[0]
            p1_bank[0] = (p1_bank[0] + 1) % 4
            m = 16 if kind == "AUP" else 128
            ems = []
            for dc in range(DC):
                for ph, xsrc in ((0, XNH), (1, XNL), (2, XNH)):
                    woff = (0 if ph < 2 else 32 * w) + dc * w
                    def em(E, st, sp, woff=woff, xs=xsrc, dc=dc, sl2=slab, w=w,
                           bk=bank, m=m):
                        return E.matmul(PS[0:m, 512 * bk:512 * bk + 512],
                                        sl2[:, woff:woff + w],
                                        xs[:, dc * 512:(dc + 1) * 512], start=st, stop=sp)
                    ems.append(em)
            dd = [slt] + list(xnf_ready)
            if kind == "AUP":
                dd.append(last_xnf_user[0])
            if kind == "UP":
                def emL(E, st, sp, bk=bank, idx=idx):
                    return E.matmul(PS[0:128, 512 * bk:512 * bk + 512],
                                    s_lbup[:, 128 * idx:128 * (idx + 1)],
                                    uub, start=st, stop=sp)
                ems.append(emL)
                dd += [uup_evac, t_lbup]
            mm_tok = mm_group(bank, ems, dd)
            slab_free[sl] = mm_tok
            if kind == "AUP":
                uup_evac = b.step("v", "v", lambda E, bk=bank:
                                  E.tensor_copy(uub, PS[0:16, 512 * bk:512 * bk + 512]),
                                  deps=[mm_tok])
                bank_tok[bank] = uup_evac
            elif kind == "K":
                ehk = b.step("s", "s", lambda E, bk=bank, ti=ti: E.copy(
                    KSB[:, ti * 512:(ti + 1) * 512], psb(bk)), deps=[mm_tok])
                elk = b.step("v", "v", lambda E, bk=bank, ti=ti: E.tensor_tensor(
                    KSBL[:, ti * 512:(ti + 1) * 512], psb(bk),
                    KSB[:, ti * 512:(ti + 1) * 512], ALU.subtract), deps=[ehk])
                bank_tok[bank] = elk
            elif kind == "V":
                evk = b.step("s", "s", lambda E, bk=bank: E.copy(vstg[:], psb(bk)),
                             deps=[mm_tok, stg_free.get("v")])
                bank_tok[bank] = evk
                vb = 4 + (fci % 4)
                blocks = [(vstg[:, i * 128:(i + 1) * 128], i * 128) for i in range(4)]
                tp = transpose_group(vb, blocks, deps=[evk])
                stg_free["v"] = tp
                vev = b.step("v", "v", lambda E, bk=vb, ti=ti: E.tensor_copy(
                    VTM[:, ti * 512:(ti + 1) * 512], psb(bk)), deps=[tp])
                bank_tok[vb] = vev
            elif kind == "Q":
                qh = qstgh[fci % 2]
                ql = qstgl[fci % 2]
                ehk = b.step("s", "s", lambda E, bk=bank, st=qh: E.copy(st, psb(bk)),
                             deps=[mm_tok, stg_free.get(f"qh{fci % 2}")])
                elk = b.step("v", "v", lambda E, bk=bank, sl3=ql, hh=qh: E.tensor_tensor(
                    sl3, psb(bk), hh, ALU.subtract),
                    deps=[ehk, stg_free.get(f"ql{fci % 2}")])
                bank_tok[bank] = elk
                stg_free[f"qh{fci % 2}"] = store(
                    "d_qs", QSH[128 * idx:128 * (idx + 1), ti * 512:(ti + 1) * 512],
                    qh, deps=[ehk])
                stg_free[f"ql{fci % 2}"] = store(
                    "d_qs", QSL[128 * idx:128 * (idx + 1), ti * 512:(ti + 1) * 512],
                    ql, deps=[elk])
            else:
                xs = x2stg[fci % 2]
                xh2 = STGB[:, 2048 + (fci % 2) * 512: 2048 + (fci % 2) * 512 + 512]
                xl2 = STGB[:, 3072 + (fci % 2) * 512: 3072 + (fci % 2) * 512 + 512]
                r32 = b.step("s", "s", lambda E, bk=bank, st=xs, idx=idx: E.activation(
                    st, psb(bk), AF.Relu, bias=s_bup[:, idx:idx + 1], scale=1.0),
                    deps=[mm_tok, t_bup, stg_free.get(f"x{fci % 2}")])
                hk = b.step("s", "s", lambda E, st=xs, d=xh2: E.copy(d, st),
                            deps=[r32, stg_free.get(f"xh{fci % 2}")])
                lk = b.step("v", "v", lambda E, st=xs, d=xl2, hh=xh2: E.tensor_tensor(
                    d, st, hh, ALU.subtract), deps=[hk, stg_free.get(f"xl{fci % 2}")])
                bank_tok[bank] = r32
                stg_free[f"x{fci % 2}"] = lk
                stg_free[f"xh{fci % 2}"] = store(
                    "d_x2", X2SH[128 * idx:128 * (idx + 1), ti * 512:(ti + 1) * 512],
                    xh2, deps=[hk])
                stg_free[f"xl{fci % 2}"] = store(
                    "d_x2", X2SL[128 * idx:128 * (idx + 1), ti * 512:(ti + 1) * 512],
                    xl2, deps=[lk])
        last_xnf_user[0] = mm_tok

    b.barrier()
    es_p1.close()
    for i in range(8):
        bank_tok[i] = None

    if _STOP_AFTER == "p1":
        return _finish(nc, es, b, dbg_copies)

    es_p2 = ExitStack()
    A1 = sbt(es_p2, "p2A1", [128, 16384])
    A2 = sbt(es_p2, "p2A2", [128, 16384])
    SM = sbt(es_p2, "p2SM", [128, 1024])
    SMB = sbt(es_p2, "p2SMB", [128, 2048], BF16)
    sc = A1[:, 0:16384]
    at = A2[:, 0:16384]
    qtih = [SMB[:, 0:512], SMB[:, 1024:1536]]
    qtil = [SMB[:, 512:1024], SMB[:, 1536:2048]]
    SMB2 = sbt(es_p2, "p2SMB2", [128, 1024], BF16)
    aostgh = SMB2[:, 0:512]
    aostgl = SMB2[:, 512:1024]

    qld_free = [None, None]
    ao_free = None
    sc_free = None
    at_free = None
    for qb in range(8):
        kb0 = 0 if qb == 7 else qb
        nkb = 8 - kb0
        nkc = 4 * nkb
        kmax = 512 * nkb
        for h in range(8):
            qsl2 = h % 2
            load(f"d_q{qsl2}", qtih[qsl2][:], QSH[128 * h:128 * (h + 1), qb * 512:(qb + 1) * 512],
                 deps=[qld_free[qsl2]])
            qld = load(f"d_q{qsl2}", qtil[qsl2][:], QSL[128 * h:128 * (h + 1), qb * 512:(qb + 1) * 512])
            exp_toks = []
            for sub in range(4):
                scs = sc[:, sub * 4096: sub * 4096 + kmax]
                ev_last = None
                for kb in range(nkb):
                    bank = sub % 2
                    ems0 = []
                    for qa, ka in ((qtih[qsl2], KSB), (qtih[qsl2], KSBL), (qtil[qsl2], KSB)):
                        def em(E, st, sp, kb=kb, sub=sub, bk=bank, kb0=kb0, qa=qa, ka=ka):
                            return E.matmul(psb(bk), qa[:, sub * 128:(sub + 1) * 128],
                                            ka[:, (kb0 + kb) * 512:(kb0 + kb + 1) * 512],
                                            start=st, stop=sp)
                        ems0.append(em)
                    mmt = mm_group(bank, ems0, [qld, sc_free])
                    ev_last = b.step("s", "s",
                                     lambda E, bk=bank, d=scs[:, kb * 512:(kb + 1) * 512]:
                                     E.copy(d, psb(bk)), deps=[mmt])
                    bank_tok[bank] = ev_last
                dcol = qb * 512 + sub * 128 - 512 * kb0
                tok = ev_last
                if dcol > 0:
                    tok = b.step("v", "v", lambda E, d=scs[:, 0:dcol]:
                                 E.memset(d, NEG), deps=[tok])
                tok = b.step("v", "v", lambda E, d=scs[:, dcol:dcol + 128]:
                             E.tensor_tensor(d, d, keep01[:], ALU.mult), deps=[tok])
                tok = b.step("v", "v", lambda E, d=scs[:, dcol:dcol + 128]:
                             E.tensor_tensor(d, d, trimask[:], ALU.add), deps=[tok])
                mx = b.step("v", "v", lambda E, s=scs, km=kmax: E.reduce_max(
                    sstat[:, 8:9], s[:, 0:km], AXX), deps=[tok])
                nb2 = b.step("v", "v", lambda E: E.tensor_scalar(
                    sstat[:, 9:10], sstat[:, 8:9], -SC0, None, ALU.mult), deps=[mx])
                ex = b.step("s", "s", lambda E, s=scs, km=kmax: E.activation(
                    s[:, 0:km], s[:, 0:km], AF.Exp, bias=sstat[:, 9:10], scale=SC0,
                    accum_out=sstat[:, 10:11]), deps=[nb2])
                rc = b.step("v", "v", lambda E: E.reciprocal(
                    sstat[:, 11:12], sstat[:, 10:11]), deps=[ex])
                nm = b.step("v", "v", lambda E, s=scs, km=kmax: E.tensor_scalar(
                    s[:, 0:km], s[:, 0:km], sstat[:, 11:12], None, ALU.mult), deps=[rc])
                exp_toks.append(nm)
            at_ready = None
            last_tp = None
            for kc in range(nkc):
                bank = 2 + (kc % 2)
                blocks = [(sc[:, sub * 4096 + kc * 128: sub * 4096 + (kc + 1) * 128],
                           sub * 128) for sub in range(4)]
                tp = transpose_group(bank, blocks, deps=exp_toks + [at_free])
                evk = b.step("s", "s", lambda E, bk=bank,
                             d=at[:, kc * 512:(kc + 1) * 512]:
                             E.copy(d, psb(bk)), deps=[tp])
                bank_tok[bank] = evk
                at_ready = evk
                last_tp = tp
            sc_free = last_tp
            bank = 6
            ems = []
            for kc in range(nkc):
                def emv(E, st, sp, kc=kc, kb0=kb0):
                    return E.matmul(psb(6), VTM[:, (4 * kb0 + kc) * 128:(4 * kb0 + kc + 1) * 128],
                                    at[:, kc * 512:(kc + 1) * 512], start=st, stop=sp)
                ems.append(emv)
            avt = mm_group(bank, ems, [at_ready])
            at_free = avt
            ehk = b.step("s", "s", lambda E: E.copy(aostgh[:], psb(6)),
                         deps=[avt, ao_free])
            elk = b.step("v", "v", lambda E: E.tensor_tensor(
                aostgl[:], psb(6), aostgh[:], ALU.subtract), deps=[ehk])
            bank_tok[bank] = elk
            store("d_ao", AOSH[128 * h:128 * (h + 1), qb * 512:(qb + 1) * 512],
                  aostgh[:], deps=[ehk])
            ao_free = store("d_ao", AOSL[128 * h:128 * (h + 1), qb * 512:(qb + 1) * 512],
                            aostgl[:], deps=[elk])
            qld_free[qsl2] = avt

    b.barrier()
    es_p2.close()
    es_kv.close()
    for i in range(8):
        bank_tok[i] = None

    if _STOP_AFTER == "p2":
        return _finish(nc, es, b, dbg_copies)

    es_p3 = ExitStack()
    A1 = sbt(es_p3, "p3A1", [128, 32768], BF16)
    AOT = sbt(es_p3, "p3AOT", [128, 4096], BF16)
    EV3 = sbt(es_p3, "p3EV", [128, 4096])
    slab_tok = None
    slab_users3 = None
    ao_free3 = [None, None]
    ev_free3 = [None, None]
    for dh in range(2):
        load("d_wo", A1[:, 0:16384].rearrange("p (c f) -> p c f", f=2048),
             WOH.rearrange("(c p) d -> p c d", p=128)[:, :, dh * 2048:(dh + 1) * 2048],
             deps=[slab_tok, slab_users3])
        slab_tok = load("d_wo", A1[:, 16384:32768].rearrange("p (c f) -> p c f", f=2048),
                        WOL.rearrange("(c p) d -> p c d", p=128)[:, :, dh * 2048:(dh + 1) * 2048])
        for tc in range(32):
            sl = tc % 2
            aoth = AOT[:, sl * 1024:(sl + 1) * 1024]
            aotl = AOT[:, 2048 + sl * 1024: 2048 + (sl + 1) * 1024]
            load(f"d_aot{sl}", aoth.rearrange("p (c f) -> p c f", f=128),
                 AOSH.rearrange("(c p) t -> p c t", p=128)[:, :, tc * 128:(tc + 1) * 128],
                 deps=[ao_free3[sl]])
            aold = load(f"d_aot{sl}", aotl.rearrange("p (c f) -> p c f", f=128),
                        AOSL.rearrange("(c p) t -> p c t", p=128)[:, :, tc * 128:(tc + 1) * 128])
            b0 = 4 * (tc % 2)
            gtoks = []
            for nb in range(4):
                ems = []
                for j in range(8):
                    for aop, woff in ((aoth, 0), (aoth, 16384), (aotl, 0)):
                        def em(E, st, sp, nb=nb, j=j, b0=b0, a=aop, wo2=woff):
                            return E.matmul(PS[:, (b0 + nb) * 512:(b0 + nb) * 512 + 512],
                                            a[:, j * 128:(j + 1) * 128],
                                            A1[:, wo2 + j * 2048 + nb * 512: wo2 + j * 2048 + nb * 512 + 512],
                                            start=st, stop=sp)
                        ems.append(em)
                gt = mm_group(b0 + nb, ems, [aold, slab_tok])
                gtoks.append(gt)
            ao_free3[sl] = gtoks[-1]
            stg = EV3[:, sl * 2048:(sl + 1) * 2048]
            evk = b.step("v", "v", lambda E, b0=b0, st=stg: E.tensor_copy(
                st, PS[:, b0 * 512:b0 * 512 + 2048]), deps=gtoks + [ev_free3[sl]])
            for nb in range(4):
                bank_tok[b0 + nb] = evk
            ev_free3[sl] = store("d_aps", APS[tc * 128:(tc + 1) * 128,
                                              dh * 2048:(dh + 1) * 2048], stg, deps=[evk])
            slab_users3 = gtoks[-1]
    b.barrier()
    es_p3.close()
    for i in range(8):
        bank_tok[i] = None

    if _STOP_AFTER == "p3":
        return _finish(nc, es, b, dbg_copies)

    es_p5 = ExitStack()
    WSL = sbt(es_p5, "p5WSL", [128, 2 * HCH_N * 1024], BF16)
    X2T = sbt(es_p5, "p5X2T", [128, 2 * HCH_N * 512], BF16)
    ATL = sbt(es_p5, "p5ATL", [128, 2048])
    EV5 = sbt(es_p5, "p5EV", [128, 2048])
    UDB = sbt(es_p5, "p5UDB", [128, 512])
    s_ldna32 = sbt(es_p5, "p5ldna32", [128, HCH_N * 16])
    s_ldnaH = sbt(es_p5, "p5ldnaH", [128, HCH_N * 16], BF16)
    s_ldnaL = sbt(es_p5, "p5ldnaL", [128, HCH_N * 16], BF16)
    s_ldnb = sbt(es_p5, "p5ldnb", [16, D])
    s_bdn = sbt(es_p5, "p5bdn", [1, D])
    t_ldnb = load("d_c", s_ldnb[:], ldnb[:, :])
    t_bdn = load("d_c", s_bdn[:], bdn8[:, :])
    t_la32 = load("d_c", s_ldna32[:].rearrange("p (c r) -> p c r", r=16),
                  ldna[:, :].rearrange("(c p) r -> p c r", p=128))
    t_lah = b.step("s", "s", lambda E: E.copy(s_ldnaH[:], s_ldna32[:]), deps=[t_la32])
    t_lal = b.step("v", "v", lambda E: E.tensor_tensor(
        s_ldnaL[:], s_ldna32[:], s_ldnaH[:], ALU.subtract), deps=[t_lah])
    NH2 = HCH_N * 1024
    NX2 = HCH_N * 512
    wsl_tok = None
    x2_free = None
    atl_free = [None, None]
    ev_free5 = [None, None]
    ud_free = [None]
    cc_toks = []
    for dq4 in range(4):
        load("d_wsl", WSL[:, 0:NH2].rearrange("p (c f) -> p c f", f=1024),
             WDNTH.rearrange("(c p) d -> p c d", p=128)[:, :, dq4 * 1024:(dq4 + 1) * 1024],
             deps=[wsl_tok, x2_free])
        wsl_tok = load("d_wsl", WSL[:, NH2:2 * NH2].rearrange("p (c f) -> p c f", f=1024),
                       WDNTL.rearrange("(c p) d -> p c d", p=128)[:, :, dq4 * 1024:(dq4 + 1) * 1024])
        for ti in range(NT):
            load("d_x2t", X2T[:, 0:NX2].rearrange("p (c f) -> p c f", f=512),
                 X2SH.rearrange("(c p) t -> p c t", p=128)[:, :, ti * 512:(ti + 1) * 512],
                 deps=[x2_free])
            x2ld = load("d_x2t", X2T[:, NX2:2 * NX2].rearrange("p (c f) -> p c f", f=512),
                        X2SL.rearrange("(c p) t -> p c t", p=128)[:, :, ti * 512:(ti + 1) * 512])
            ems = []
            for hc in range(HCH_N):
                for aop, xoff in ((s_ldnaH, 0), (s_ldnaH, NX2), (s_ldnaL, 0)):
                    def em(E, st, sp, hc=hc, a=aop, xo=xoff):
                        return E.matmul(PS[0:16, 7 * 512:7 * 512 + 512],
                                        a[:, hc * 16:(hc + 1) * 16],
                                        X2T[:, xo + hc * 512: xo + (hc + 1) * 512],
                                        start=st, stop=sp)
                    ems.append(em)
            udt = mm_group(7, ems, [x2ld, t_lah, t_lal, ud_free[0]])
            ud_done_t = b.step("v", "v", lambda E: E.tensor_copy(
                UDB[0:16, :], PS[0:16, 7 * 512:7 * 512 + 512]), deps=[udt])
            bank_tok[7] = ud_done_t
            last_g = None
            for sub in range(4):
                b0 = 0 if (sub % 2 == 0) else 2
                gtoks = []
                for nb in range(2):
                    bk = b0 + nb
                    ems = []
                    for hc in range(HCH_N):
                        for xoff, woff in ((0, 0), (NX2, 0), (0, NH2)):
                            def em(E, st, sp, hc=hc, nb=nb, sub=sub, bk=bk, xo=xoff, wo2=woff):
                                return E.matmul(psb(bk),
                                                X2T[:, xo + hc * 512 + sub * 128: xo + hc * 512 + sub * 128 + 128],
                                                WSL[:, wo2 + hc * 1024 + nb * 512: wo2 + hc * 1024 + nb * 512 + 512],
                                                start=st, stop=sp)
                            ems.append(em)
                    def emlora(E, st, sp, nb=nb, sub=sub, dq4=dq4, bk=bk):
                        return E.matmul(psb(bk),
                                        UDB[0:16, sub * 128: sub * 128 + 128],
                                        s_ldnb[:, dq4 * 1024 + nb * 512: dq4 * 1024 + nb * 512 + 512],
                                        start=st, stop=sp)
                    ems.append(emlora)
                    def embd(E, st, sp, nb=nb, dq4=dq4, bk=bk):
                        return E.matmul(psb(bk), ones[0:1, 0:128],
                                        s_bdn[0:1, dq4 * 1024 + nb * 512: dq4 * 1024 + nb * 512 + 512],
                                        start=st, stop=sp)
                    ems.append(embd)
                    gt = mm_group(bk, ems, [x2ld, wsl_tok, ud_done_t, t_ldnb, t_bdn])
                    gtoks.append(gt)
                slq = sub % 2
                stg = EV5[:, slq * 1024:(slq + 1) * 1024]
                trow = ti * 512 + sub * 128
                dd = gtoks + [ev_free5[slq]]
                atl = ATL[:, slq * 1024:(slq + 1) * 1024]
                atld = load(f"d_atl{slq}", atl, APS[trow:trow + 128,
                                                    dq4 * 1024:(dq4 + 1) * 1024],
                            deps=[atl_free[slq]])
                e1 = b.step("v", "v", lambda E, st=stg, b0=b0, a=atl: E.tensor_tensor(
                    st, PS[:, b0 * 512:b0 * 512 + 1024], a, ALU.add),
                    deps=dd + [atld])
                atl_free[slq] = e1
                for bk in (b0, b0 + 1):
                    bank_tok[bk] = e1
                ev_free5[slq] = store("d_part", PARTQ[dq4][trow:trow + 128, :],
                                      stg, deps=[e1])
                last_g = gtoks[-1]
            x2_free = last_g
            ud_free[0] = last_g
        b.wait("g", ("d_part", b.cnt["d_part"]))
        cc_toks.append(b.step("g", "cc", lambda E, q=dq4: E.collective_compute(
            "ReduceScatter", ALU.add, replica_groups=RS1_GROUPS,
            ins=[PARTQ[q][:, :]], outs=[R2OUTQ[q][:, :]])))

    b.barrier()
    es_p5.close()
    for i in range(8):
        bank_tok[i] = None

    if _STOP_AFTER == "p5":
        return _finish(nc, es, b, dbg_copies)

    b.barrier()

    if _STOP_AFTER == "l0":
        return _finish(nc, es, b, dbg_copies)

    es_p8 = ExitStack()
    XNH8 = sbt(es_p8, "p8XNH", [128, 16384], BF16)
    XNL8 = sbt(es_p8, "p8XNL", [128, 16384], BF16)
    SLAB8 = sbt(es_p8, "p8SLAB", [128, 16384], BF16)
    XT = sbt(es_p8, "p8XT", [128, 4096])
    XSQ = sbt(es_p8, "p8XSQ", [128, 4096])
    XH8 = sbt(es_p8, "p8XH", [128, 4096], BF16)
    XL8 = sbt(es_p8, "p8XL", [128, 4096], BF16)
    STG8 = sbt(es_p8, "p8STG", [128, 3072])
    uall = sbt(es_l1, "uall", [128, 4096], side="right")
    glog = sbt(es_l1, "glog", [8, 1024], side="right")
    vstg = STG8[:, 0:512]
    qstg8 = [STG8[:, 512:1024], STG8[:, 1024:1536]]
    kstg8 = [STG8[:, 1536:2048], STG8[:, 2048:2560]]
    vstg2 = STG8[:, 2560:3072]
    slab_free = [None, None]
    stg_free = {}
    last_xt_user[0] = None
    last_xh_user[0] = None
    last_xnf_user[0] = None

    def x1_sub(trow):
        ldo = load("d_xt", XT[:], xown[trow:trow + 128, :], deps=[last_xt_user[0]])
        ldr = None
        for q in range(4):
            ldr = load("d_xr", XSQ[:, q * 1024:(q + 1) * 1024],
                       R2OUTQ[q][trow:trow + 128, :],
                       deps=([prev_sq[0]] if q == 0 else ()))
        ad = b.step("v", "v", lambda E: E.tensor_tensor(XT[:], XT[:], XSQ[:], ALU.add),
                    deps=[ldo, ldr])
        st = store("d_x1", X1S[trow:trow + 128, :], XT[:], deps=[ad])
        return ln_subtile(XT, XSQ, st)

    ready8 = [None, None]
    for ti1 in range(2):
        for sub in range(4):
            trow = ti1 * 512 + sub * 128
            lnk = x1_sub(trow)
            hk = b.step("s", "s", lambda E: E.copy(XH8[:], XT[:]),
                        deps=[lnk, last_xh_user[0]])
            lk = b.step("v", "v", lambda E: E.tensor_tensor(XL8[:], XT[:], XH8[:],
                                                            ALU.subtract), deps=[hk])
            last_xt_user[0] = lk
            for comp in range(2):
                XSRC = XH8 if comp == 0 else XL8
                XDST = XNH8 if comp == 0 else XNL8
                ctok = hk if comp == 0 else lk
                for g in range(8):
                    bank = 4 + ((2 * g + comp) % 4)
                    blocks = [(XSRC[:, (4 * g + i) * 128:(4 * g + i + 1) * 128], i * 128)
                              for i in range(4)]
                    tp = transpose_group(bank, blocks, deps=[ctok, last_xnf_user[0]])
                    dst = XDST.rearrange("p (c t) -> p c t", t=512)[:, 4 * g:4 * g + 4,
                                                                   sub * 128:(sub + 1) * 128]
                    eng = "s" if comp == 0 else "v"
                    if comp == 0:
                        evk = b.step("s", "s", lambda E, bk=bank, d=dst: E.copy(
                            d, psb16(bk).rearrange("p (c f) -> p c f", f=128)), deps=[tp])
                    else:
                        evk = b.step("v", "v", lambda E, bk=bank, d=dst: E.tensor_copy(
                            d, psb16(bk).rearrange("p (c f) -> p c f", f=128)), deps=[tp])
                    bank_tok[bank] = evk
                    ready8[comp] = evk
                    last_xh_user[0] = tp
        fcs = [("QKV", j) for j in range(24)] + [("G", 0)] + [("MA", j) for j in range(4)]
        mm_tok = None
        for fci, (kind, idx) in enumerate(fcs):
            bank = fci % 4
            m = 8 if kind == "G" else 128
            if kind in ("QKV", "MA"):
                sl = fci % 2
                slab = SLAB8[:, sl * 8192:(sl + 1) * 8192]
                if kind == "QKV":
                    sh2 = WQKVH[:, 128 * idx:128 * (idx + 1)]
                    sl2 = WQKVL[:, 128 * idx:128 * (idx + 1)]
                else:
                    sh2 = AATH[:, 128 * idx:128 * (idx + 1)]
                    sl2 = AATL[:, 128 * idx:128 * (idx + 1)]
                load(f"d_sl{sl}", slab[:, 0:4096].rearrange("p (c f) -> p c f", f=128),
                     sh2.rearrange("(c p) f -> p c f", p=128), deps=[slab_free[sl]])
                slt = load(f"d_sl{sl}",
                           slab[:, 4096:8192].rearrange("p (c f) -> p c f", f=128),
                           sl2.rearrange("(c p) f -> p c f", p=128))
                def lhs_fn(dc, ph, slab=slab):
                    return slab[:, (0 if ph < 2 else 4096) + dc * 128:
                                (0 if ph < 2 else 4096) + dc * 128 + 128]
                dd = [slt] + ready8
            else:
                def lhs_fn(dc, ph):
                    g2 = s_gwtH if ph < 2 else s_gwtL
                    return g2[:, 8 * dc:8 * dc + 8]
                dd = list(ready8)
                slt = None
            ems = []
            for dc in range(DC):
                for ph, xsrc in ((0, XNH8), (1, XNL8), (2, XNH8)):
                    def em(E, st, sp, dc=dc, ph=ph, f=lhs_fn, bk=bank, m=m, xs=xsrc):
                        return E.matmul(PS[0:m, 512 * bk:512 * bk + 512], f(dc, ph),
                                        xs[:, dc * 512:(dc + 1) * 512], start=st, stop=sp)
                    ems.append(em)
            mm_tok = mm_group(bank, ems, dd)
            if kind in ("QKV", "MA"):
                slab_free[sl] = mm_tok
            if kind == "QKV" and idx < 8:
                qs = qstg8[fci % 2]
                evk = b.step("v", "v", lambda E, bk=bank, st=qs, idx=idx: E.tensor_scalar(
                    st, psb(bk), s_bqkv[:, idx:idx + 1], None, ALU.add),
                    deps=[mm_tok, t_bqkv, stg_free.get(f"q8{fci % 2}")])
                bank_tok[bank] = evk
                stg_free[f"q8{fci % 2}"] = store(
                    "d_qs1", QS1[128 * idx:128 * (idx + 1), ti1 * 512:(ti1 + 1) * 512],
                    qs, deps=[evk])
            elif kind == "QKV" and idx < 16:
                ks = kstg8[fci % 2]
                evk = b.step("v", "v", lambda E, bk=bank, st=ks, idx=idx: E.tensor_scalar(
                    st, psb(bk), s_bqkv[:, idx:idx + 1], None, ALU.add),
                    deps=[mm_tok, t_bqkv, stg_free.get(f"k8{fci % 2}")])
                bank_tok[bank] = evk
                stg_free[f"k8{fci % 2}"] = store(
                    "d_ag", AGIN[128 * (idx - 8):128 * (idx - 7),
                                 ti1 * 512:(ti1 + 1) * 512], ks, deps=[evk])
            elif kind == "QKV":
                evk = b.step("v", "v", lambda E, bk=bank, idx=idx: E.tensor_scalar(
                    vstg[:], psb(bk), s_bqkv[:, idx:idx + 1], None, ALU.add),
                    deps=[mm_tok, t_bqkv, stg_free.get("v8")])
                bank_tok[bank] = evk
                vb = 4 + (fci % 4)
                blocks = [(vstg[:, i * 128:(i + 1) * 128], i * 128) for i in range(4)]
                tp = transpose_group(vb, blocks, deps=[evk])
                stg_free["v8"] = tp
                vev = b.step("v", "v", lambda E, bk=vb, sv=vstg2: E.tensor_copy(
                    sv, psb(bk)), deps=[tp, stg_free.get("v8st")])
                bank_tok[vb] = vev
                dst = AGIN[1024 + ti1 * 512: 1024 + (ti1 + 1) * 512,
                           128 * (idx - 16):128 * (idx - 15)]
                stg_free["v8st"] = store(
                    "d_ag", dst.rearrange("(c p) f -> p c f", p=128),
                    vstg2.rearrange("p (c f) -> p c f", f=128), deps=[vev])
            elif kind == "G":
                evk = b.step("v", "v", lambda E, bk=bank, ti1=ti1: E.tensor_copy(
                    glog[0:8, ti1 * 512:(ti1 + 1) * 512], PS[0:8, 512 * bk:512 * bk + 512]),
                    deps=[mm_tok])
                bank_tok[bank] = evk
            else:
                evk = b.step("v", "v", lambda E, bk=bank, idx=idx, ti1=ti1: E.tensor_copy(
                    uall[:, idx * 1024 + ti1 * 512: idx * 1024 + (ti1 + 1) * 512],
                    psb(bk)), deps=[mm_tok])
                bank_tok[bank] = evk
        last_xnf_user[0] = mm_tok

    b.barrier()
    es_p8.close()
    for i in range(8):
        bank_tok[i] = None

    cc3 = b.step("g", "cc", lambda E: E.collective_compute(
        "AllGather", ALU.bypass, replica_groups=AG_GROUPS,
        ins=[AGIN[:, :]], outs=[AGOUT[:, :]]))
    b.barrier()

    if _STOP_AFTER == "p9":
        return _finish(nc, es, b, dbg_copies)

    es_p10 = ExitStack()
    SC1B = sbt(es_p10, "pASC", [128, 8192])
    AT1 = sbt(es_p10, "pAAT", [128, 8192])
    QT1 = sbt(es_p10, "pAQT", [128, 2048])
    KB1 = sbt(es_p10, "pAKB", [128, 4096])
    VB1 = sbt(es_p10, "pAVB", [128, 4096])
    af = sbt(es_l1, "af", [128, 8192], side="right")
    qfree = None
    kfree = None
    vfree = None
    scfree = None
    atfree = None
    for qb1 in range(4):
        qld = load("d_q1", QT1[:, :].rearrange("p (c f) -> p c f", f=256),
                   QS1.rearrange("(c p) t -> p c t", p=128)[:, :, qb1 * 256:(qb1 + 1) * 256],
                   deps=[qfree])
        exp_toks = []
        ev_last = None
        for kb in range(8):
            r, hf = kb // 2, kb % 2
            kld = load("d_k1", KB1[:, :].rearrange("p (c f) -> p c f", f=512),
                       AGOUT[r * 2048: r * 2048 + 1024, :].rearrange(
                           "(c p) t -> p c t", p=128)[:, :, hf * 512:(hf + 1) * 512],
                       deps=[kfree])
            for sub in range(2):
                bank = sub
                ems = []
                for dkc in range(8):
                    def em(E, st, sp, dkc=dkc, sub=sub, bk=bank):
                        return E.matmul(psb(bk),
                                        QT1[:, dkc * 256 + sub * 128: dkc * 256 + sub * 128 + 128],
                                        KB1[:, dkc * 512:(dkc + 1) * 512], start=st, stop=sp)
                    ems.append(em)
                mmt = mm_group(bank, ems, [qld, kld, scfree])
                ev_last = b.step("v", "v", lambda E, bk=bank,
                                 d=SC1B[:, sub * 4096 + kb * 512: sub * 4096 + (kb + 1) * 512]:
                                 E.tensor_copy(d, psb(bk)), deps=[mmt])
                bank_tok[bank] = ev_last
            kfree = mmt
        for sub in range(2):
            scs = SC1B[:, sub * 4096: sub * 4096 + 4096]
            mx = b.step("v", "v", lambda E, s=scs: E.reduce_max(
                sstat[:, 8:9], s, AXX), deps=[ev_last])
            nb2 = b.step("v", "v", lambda E: E.tensor_scalar(
                sstat[:, 9:10], sstat[:, 8:9], -SC1, None, ALU.mult), deps=[mx])
            ex = b.step("s", "s", lambda E, s=scs: E.activation(
                s, s, AF.Exp, bias=sstat[:, 9:10], scale=SC1,
                accum_out=sstat[:, 10:11]), deps=[nb2])
            rc = b.step("v", "v", lambda E: E.reciprocal(
                sstat[:, 11:12], sstat[:, 10:11]), deps=[ex])
            nm = b.step("v", "v", lambda E, s=scs: E.tensor_scalar(
                s, s, sstat[:, 11:12], None, ALU.mult), deps=[rc])
            exp_toks.append(nm)
        at_ready = None
        last_tp = None
        for kc in range(32):
            bank = 2 + (kc % 2)
            blocks = [(SC1B[:, sub * 4096 + kc * 128: sub * 4096 + (kc + 1) * 128],
                       sub * 128) for sub in range(2)]
            tp = transpose_group(bank, blocks, deps=exp_toks + [atfree])
            evk = b.step("v", "v", lambda E, bk=bank,
                         d=AT1[:, kc * 256:(kc + 1) * 256]:
                         E.tensor_copy(d, PS[:, 512 * bk:512 * bk + 256]), deps=[tp])
            bank_tok[bank] = evk
            at_ready = evk
            last_tp = tp
        scfree = last_tp
        first = [True] * 8
        avtoks = [None] * 8
        for kb in range(8):
            r, hf = kb // 2, kb % 2
            vld = load("d_v1", VB1[:, :].rearrange("p (c f) -> p c f", f=1024),
                       AGOUT[r * 2048 + 1024 + hf * 512: r * 2048 + 1024 + (hf + 1) * 512, :]
                       .rearrange("(c p) f -> p c f", p=128),
                       deps=[vfree])
            for dvc in range(8):
                for kcl in range(4):
                    st = first[dvc]
                    first[dvc] = False
                    sp = (kb == 7 and kcl == 3)
                    last_of_kb = (dvc == 7 and kcl == 3)
                    dd = [vld, at_ready]
                    if st and bank_tok[dvc] is not None:
                        dd.append(bank_tok[dvc])
                        bank_tok[dvc] = None
                    def em(E, dvc=dvc, kcl=kcl, kb=kb, st=st, sp=sp):
                        return E.matmul(PS[:, dvc * 512:dvc * 512 + 256],
                                        VB1[:, kcl * 1024 + dvc * 128: kcl * 1024 + dvc * 128 + 128],
                                        AT1[:, (4 * kb + kcl) * 256:(4 * kb + kcl + 1) * 256],
                                        start=st, stop=sp)
                    if sp or last_of_kb:
                        tokkb = b.step("t", "t", em, deps=dd)
                        if sp:
                            avtoks[dvc] = tokkb
                        vfree = tokkb
                    else:
                        b.run("t", em, deps=dd)
        for dvc in range(8):
            if avtoks[dvc] is None:
                avtoks[dvc] = vfree
        atfree = vfree
        qfree = vfree
        for dvc in range(8):
            evk = b.step("v", "v", lambda E, dvc=dvc, qb1=qb1: E.tensor_copy(
                af[:, dvc * 1024 + qb1 * 256: dvc * 1024 + (qb1 + 1) * 256],
                PS[:, dvc * 512:dvc * 512 + 256]), deps=[avtoks[dvc]])
            bank_tok[dvc] = evk

    b.barrier()
    es_p10.close()
    for i in range(8):
        bank_tok[i] = None

    if _STOP_AFTER == "p10":
        return _finish(nc, es, b, dbg_copies)

    es_p11 = ExitStack()
    rmat = sbt(es_p11, "rmat", [8, 512])
    t_rm = load("d_c", rmat[:], rmat_in[:, :])
    wtm = sbt(es_p11, "wtm", [128, 64])
    wtm2 = sbt(es_p11, "wtm2", [128, 64])
    PB = sbt(es_p11, "p11PB", [128, 40])
    wfm = sbt(es_p11, "wfm", [8, 1024])
    wexp = sbt(es_l1, "wexp", [128, 4096], side="right")
    evk = None
    for g2 in range(2):
        bank = g2 % 2
        blocks = [(glog[0:8, (4 * g2 + i) * 128:(4 * g2 + i + 1) * 128], i * 8)
                  for i in range(4)]
        tp = transpose_group(bank, blocks, deps=[])
        evk = b.step("v", "v", lambda E, bk=bank, g2=g2: E.tensor_copy(
            wtm[:, 32 * g2:32 * g2 + 32], PS[:, 512 * bk:512 * bk + 32]), deps=[tp])
        bank_tok[bank] = evk
    wlast = None
    for tc in range(8):
        lg = wtm[:, 8 * tc:8 * tc + 8]
        t1 = b.step("v", "v", lambda E, lg=lg: E.reduce_max(sstat[:, 8:9], lg, AXX),
                    deps=[evk])
        t2 = b.step("v", "v", lambda E, lg=lg: E.tensor_scalar(
            PB[:, 0:8], lg, sstat[:, 8:9], None, ALU.is_equal), deps=[t1])
        t3 = b.step("v", "v", lambda E, lg=lg: E.scalar_tensor_tensor(
            PB[:, 8:16], PB[:, 0:8], -1e30, lg, ALU.mult, ALU.add), deps=[t2])
        t4 = b.step("v", "v", lambda E: E.reduce_max(sstat[:, 9:10], PB[:, 8:16], AXX),
                    deps=[t3])
        t5 = b.step("v", "v", lambda E: E.tensor_scalar(
            sstat[:, 10:11], sstat[:, 8:9], -1.0, None, ALU.mult), deps=[t4])
        t6 = b.step("s", "s", lambda E, lg=lg: E.activation(
            PB[:, 16:24], lg, AF.Exp, bias=sstat[:, 10:11], scale=1.0), deps=[t5])
        t7 = b.step("s", "s", lambda E: E.activation(
            sstat[:, 11:12], sstat[:, 9:10], AF.Exp, bias=sstat[:, 10:11], scale=1.0),
            deps=[t6])
        t8 = b.step("v", "v", lambda E: E.tensor_scalar(
            sstat[:, 12:13], sstat[:, 11:12], 1.0, None, ALU.add), deps=[t7])
        t9 = b.step("v", "v", lambda E: E.reciprocal(sstat[:, 13:14], sstat[:, 12:13]),
                    deps=[t8])
        ta = b.step("v", "v", lambda E, lg=lg: E.tensor_scalar(
            PB[:, 24:32], lg, sstat[:, 9:10], None, ALU.is_ge), deps=[t9])
        tb = b.step("v", "v", lambda E: E.tensor_tensor(
            PB[:, 32:40], PB[:, 16:24], PB[:, 24:32], ALU.mult), deps=[ta, t6])
        wlast = b.step("v", "v", lambda E, tc=tc: E.tensor_scalar(
            wtm2[:, 8 * tc:8 * tc + 8], PB[:, 32:40], sstat[:, 13:14], None, ALU.mult),
            deps=[tb])
    for g2 in range(2):
        bank = 2 + g2
        blocks = [(wtm2[:, (4 * g2 + i) * 8:(4 * g2 + i + 1) * 8], i * 128)
                  for i in range(4)]
        tp = transpose_group(bank, blocks, deps=[wlast])
        evk = b.step("v", "v", lambda E, bk=bank, g2=g2: E.tensor_copy(
            wfm[0:8, 512 * g2:512 * g2 + 512], PS[0:8, 512 * bk:512 * bk + 512]),
            deps=[tp])
        bank_tok[bank] = evk
    wu_toks = []
    for erc in range(4):
        for ti1 in range(2):
            bank = 4 + ((2 * erc + ti1) % 4)
            def em(E, st, sp, erc=erc, ti1=ti1, bk=bank):
                return E.matmul(psb(bk), rmat[0:8, erc * 128:(erc + 1) * 128],
                                wfm[0:8, ti1 * 512:(ti1 + 1) * 512], start=st, stop=sp)
            mmt = mm_group(bank, [em], [evk])
            wek = b.step("v", "v", lambda E, bk=bank, erc=erc, ti1=ti1: E.tensor_copy(
                wexp[:, erc * 1024 + ti1 * 512: erc * 1024 + (ti1 + 1) * 512], psb(bk)),
                deps=[mmt])
            bank_tok[bank] = wek
            wu_toks.append(b.step("v", "v", lambda E, erc=erc, ti1=ti1: E.tensor_tensor(
                uall[:, erc * 1024 + ti1 * 512: erc * 1024 + (ti1 + 1) * 512],
                uall[:, erc * 1024 + ti1 * 512: erc * 1024 + (ti1 + 1) * 512],
                wexp[:, erc * 1024 + ti1 * 512: erc * 1024 + (ti1 + 1) * 512],
                ALU.mult), deps=[wek]))

    b.barrier()
    es_p11.close()
    for i in range(8):
        bank_tok[i] = None

    if _STOP_AFTER == "p11":
        return _finish(nc, es, b, dbg_copies)

    es_p12 = ExitStack()
    W1S = sbt(es_p12, "p12W1", [128, 8192])
    BTSL = sbt(es_p12, "p12BT", [128, 4096])
    X1T = sbt(es_p12, "p12X1", [128, 1024])
    EVF = sbt(es_p12, "p12EV", [128, 2048])
    s_bo = sbt(es_p12, "p12bo", [1, D])
    t_bo = load("d_c", s_bo[:], bo8[:, :])
    w1_tok = None
    bt_tok = None
    x1_free = [None]
    evf_free = [None, None]
    out_toks = []
    p12_last = [None]
    for dq in range(4):
        w1_tok = load("d_w1s", W1S[:, :].rearrange("p (c f) -> p c f", f=1024),
                      wo1.rearrange("(c p) d -> p c d", p=128)[:, :, dq * 1024:(dq + 1) * 1024],
                      deps=[w1_tok, p12_last[0]])
        bt_tok = load("d_bts", BTSL[:, :].rearrange("p (c f) -> p c f", f=1024),
                      BTS.rearrange("(c p) d -> p c d", p=128)[:, :, dq * 1024:(dq + 1) * 1024],
                      deps=[bt_tok, p12_last[0]])
        for tc in range(8):
            x1t = X1T[:, 0:1024]
            x1ld = load("d_x1t", x1t, X1S[tc * 128:(tc + 1) * 128,
                                          dq * 1024:(dq + 1) * 1024],
                        deps=[x1_free[0]])
            b0 = 2 * (tc % 4)
            gtoks = []
            for nb in range(2):
                bk = b0 + nb
                ems = []
                if _P12_PARTS & 1:
                    for dvc in range(8):
                        def em(E, st, sp, dvc=dvc, nb=nb, tc=tc, bk=bk):
                            return E.matmul(psb(bk),
                                            af[:, dvc * 1024 + tc * 128: dvc * 1024 + tc * 128 + 128],
                                            W1S[:, dvc * 1024 + nb * 512: dvc * 1024 + nb * 512 + 512],
                                            start=st, stop=sp)
                        ems.append(em)
                if _P12_PARTS & 2:
                    for erc in range(4):
                        def em2(E, st, sp, erc=erc, nb=nb, tc=tc, bk=bk):
                            return E.matmul(psb(bk),
                                            uall[:, erc * 1024 + tc * 128: erc * 1024 + tc * 128 + 128],
                                            BTSL[:, erc * 1024 + nb * 512: erc * 1024 + nb * 512 + 512],
                                            start=st, stop=sp)
                        ems.append(em2)
                if _P12_PARTS & 4:
                    def em3(E, st, sp, nb=nb, dq=dq, bk=bk):
                        return E.matmul(psb(bk), ones[0:1, 0:128],
                                        s_bo[0:1, dq * 1024 + nb * 512: dq * 1024 + nb * 512 + 512],
                                        start=st, stop=sp)
                    ems.append(em3)
                gt = mm_group(bk, ems, [w1_tok, bt_tok, t_bo] + wu_toks)
                gtoks.append(gt)
            sl = tc % 2
            stg = EVF[:, sl * 1024:(sl + 1) * 1024]
            evk = b.step("v", "v", lambda E, st=stg, b0=b0, xx=x1t: E.tensor_tensor(
                st, PS[:, b0 * 512: b0 * 512 + 1024], xx, ALU.add),
                deps=gtoks + [x1ld, evf_free[sl]])
            for nb in range(2):
                bank_tok[b0 + nb] = evk
            x1_free[0] = evk
            out_toks.append(store("d_out", yout[tc * 128:(tc + 1) * 128,
                                                dq * 1024:(dq + 1) * 1024], stg,
                                  deps=[evk]))
            evf_free[sl] = out_toks[-1]
            p12_last[0] = gtoks[-1]
    for t in out_toks:
        b.wait("y", t)
        b.wait("g", t)
    b.barrier()
    es_p12.close()
    es_l1.close()
    return _finish(nc, es, b, dbg_copies)


def _finish(nc, es, b, dbg_copies=()):
    b.barrier()
    for (src_ap, dst_ap, shape) in dbg_copies:
        nrow = shape[0]
        step = max(1, min(nrow, (4 << 20) // max(1, int(np.prod(shape[1:])) * 4)))
        for r0 in range(0, nrow, step):
            r1 = min(nrow, r0 + step)
            b.step("g", "d_dbg", lambda E, s=src_ap[r0:r1], d=dst_ap[r0:r1]:
                   E.dma_start(out=d, in_=s), dma=True)
    b.barrier()
    with nc.Block() as block:
        @block.tensor
        def _(E):
            for f in b.ops["t"]:
                f(E)

        @block.vector
        def _(E):
            for f in b.ops["v"]:
                f(E)

        @block.scalar
        def _(E):
            for f in b.ops["s"]:
                f(E)

        @block.gpsimd
        def _(E):
            for f in b.ops["g"]:
                f(E)

        @block.sync
        def _(E):
            for f in b.ops["y"]:
                f(E)
    es.close()
    return nc


def kernel(**inputs):
    x = np.ascontiguousarray(np.asarray(inputs["L0_attn_x"], dtype=np.float32)).reshape(TT, D)
    xs = [x[0:S], x[S:TT]]
    xall_per_seq = [np.ascontiguousarray(np.concatenate([xs[b_], xs[1 - b_]], axis=0))
                    for b_ in range(2)]
    wq_full = np.asarray(inputs["L0_attn_Wq"], dtype=np.float32)
    wk_full = np.asarray(inputs["L0_attn_Wk"], dtype=np.float32)
    wv_full = np.asarray(inputs["L0_attn_Wv"], dtype=np.float32)
    wo_full = np.asarray(inputs["L0_attn_Wo"], dtype=np.float32)
    cu = np.asarray(inputs["L0_ffn_w_up_codes"], dtype=np.int32).reshape(11008, D)
    au = np.asarray(inputs["L0_ffn_w_up_absmax"], dtype=np.float32).reshape(11008, 64)
    bu = np.asarray(inputs["L0_ffn_b_up"], dtype=np.float32)
    lau = np.asarray(inputs["L0_ffn_w_up_lora_a"], dtype=np.float32)
    lbu = np.asarray(inputs["L0_ffn_w_up_lora_b"], dtype=np.float32)
    cd = np.asarray(inputs["L0_ffn_w_down_codes"], dtype=np.int32).reshape(D, 11008)
    ad = np.asarray(inputs["L0_ffn_w_down_absmax"], dtype=np.float32).reshape(D, 172)
    bd = np.asarray(inputs["L0_ffn_b_down"], dtype=np.float32)
    lda_ = np.asarray(inputs["L0_ffn_w_down_lora_a"], dtype=np.float32)
    ldb_ = np.asarray(inputs["L0_ffn_w_down_lora_b"], dtype=np.float32)
    wqkv_ = np.asarray(inputs["L1_attn_Wqkv"], dtype=np.float32)
    bqkv_ = np.asarray(inputs["L1_attn_bqkv"], dtype=np.float32).reshape(24, 128)
    wo1_ = np.asarray(inputs["L1_attn_Wo"], dtype=np.float32)
    bo_ = np.asarray(inputs["L1_attn_bo"], dtype=np.float32).reshape(1, 4096)
    gw_ = np.asarray(inputs["L1_ffn_gate_w"], dtype=np.float32)
    la1_ = np.asarray(inputs["L1_ffn_lora_A"], dtype=np.float32).reshape(512, D)
    lb1_ = np.asarray(inputs["L1_ffn_lora_B"], dtype=np.float32).reshape(8 * D, 64)

    ident = np.eye(128, dtype=np.float32)
    ones_ = np.ones((1, 128), dtype=np.float32)
    rmat_ = np.zeros((8, 512), dtype=np.float32)
    for e in range(8):
        rmat_[e, 64 * e:64 * (e + 1)] = 1.0

    chunks = []
    for c in range(8):
        g = c % 4
        if g < 3:
            chunks.append(list(range(22 * g, 22 * g + 22)))
        else:
            chunks.append(list(range(66, 86)) + [None, None])

    def rows_blk(arr, hlist, blk=128):
        parts = []
        for h in hlist:
            if h is None:
                parts.append(np.zeros((blk,) + arr.shape[1:], dtype=arr.dtype))
            else:
                parts.append(arr[blk * h: blk * (h + 1)])
        return np.ascontiguousarray(np.concatenate(parts, axis=0))

    def cols_blk(arr, hlist, blk):
        parts = []
        for h in hlist:
            if h is None:
                parts.append(np.zeros(arr.shape[:1] + (blk,), dtype=arr.dtype))
            else:
                parts.append(arr[:, blk * h: blk * (h + 1)])
        return np.ascontiguousarray(np.concatenate(parts, axis=1))

    in_maps = []
    for c in range(8):
        kvh = c % 4
        hl = chunks[c]
        bup_rows = []
        for h in hl:
            bup_rows.append(np.zeros(128, dtype=np.float32) if h is None
                            else bu[128 * h:128 * (h + 1)])
        bup_c = np.ascontiguousarray(np.stack(bup_rows, axis=0))
        m = {
            "xall": np.ascontiguousarray(xs[MYSEQ[c]]),
            "xown": np.ascontiguousarray(x[OWN_START[c]:OWN_START[c] + 1024]),
            "wq": np.ascontiguousarray(wq_full[:, 1024 * kvh:1024 * (kvh + 1)]),
            "wk": np.ascontiguousarray(wk_full[:, 128 * kvh:128 * (kvh + 1)]),
            "wv": np.ascontiguousarray(wv_full[:, 128 * kvh:128 * (kvh + 1)]),
            "wo": np.ascontiguousarray(wo_full[1024 * kvh:1024 * (kvh + 1), :]),
            "cup": rows_blk(cu, hl),
            "aup": rows_blk(au, hl),
            "bup": bup_c,
            "laup": lau,
            "lbup": cols_blk(lbu, hl, 128),
            "cdn": cols_blk(cd, hl, 128),
            "adn": cols_blk(ad, hl, 2),
            "ldna": rows_blk(lda_, hl),
            "ldnb": ldb_,
            "bdn8": (bd.reshape(1, 4096) if c % 4 == 0
                     else np.zeros((1, 4096), dtype=np.float32)),
            "wqkv": wqkv_,
            "bqkv": bqkv_,
            "wo1": wo1_,
            "bo8": bo_,
            "gw": gw_,
            "la1": la1_,
            "lb1": lb1_,
            "identI": ident,
            "onesI": ones_,
            "rmatI": rmat_,
        }
        in_maps.append(m)

    nc = build_program()
    res = run_bass_kernel_spmd(nc, in_maps, core_ids=list(range(8)))
    global _LAST_RESULTS
    _LAST_RESULTS = res.results
    out = np.empty((TT, D), dtype=np.float32)
    for c in range(8):
        out[OWN_START[c]:OWN_START[c] + 1024] = np.asarray(res.results[c]["y"],
                                                           dtype=np.float32)
    return out.reshape(2, S, D)


# version 12
# speedup vs baseline: 1.0166x; 1.0166x over previous
import numpy as np
from contextlib import ExitStack

import concourse.bass as bass
import concourse.mybir as mybir
from concourse.bass_utils import run_bass_kernel_spmd

F32 = mybir.dt.float32
BF16 = mybir.dt.bfloat16
I32 = mybir.dt.int32
AF = mybir.ActivationFunctionType
ALU = mybir.AluOpType
AXX = mybir.AxisListType.X

D = 4096
DC = 32
S = 4096
TT = 8192
NT = 8
HCH_N = 22
HSN = HCH_N * 128
SC0 = float(1.0 / np.sqrt(np.float32(128.0)))
SC1 = 1.0 / 32.0
EPS = 1e-5
NEG = -1e9

NF4 = [-1.0, -0.6961928009986877, -0.5250730514526367, -0.39491748809814453,
       -0.28444138169288635, -0.18477343022823334, -0.09105003625154495, 0.0,
       0.07958029955625534, 0.16093020141124725, 0.24611230194568634,
       0.33791524171829224, 0.44070982933044434, 0.5626170039176941,
       0.7229568362236023, 1.0]

OWN_START = [0, 1024, 2048, 3072, 4096, 5120, 6144, 7168]
MYSEQ = [0, 0, 0, 0, 1, 1, 1, 1]
RS1_GROUPS = [[0, 1, 2, 3], [4, 5, 6, 7]]
AG_GROUPS = [[0, 1, 2, 3], [4, 5, 6, 7]]

ENGS = ("t", "v", "s", "g", "y")

_STOP_AFTER = None
_DEBUG_OUT = set()
_P12_PARTS = 7
_LAST_RESULTS = None


class Bld:
    def __init__(self, nc, es):
        self.nc = nc
        self.es = es
        self.ops = {e: [] for e in ENGS}
        self.semh = {}
        self.cnt = {}
        self.last_wait = {e: {} for e in ENGS}

    def sem(self, name):
        if name not in self.semh:
            self.semh[name] = self.es.enter_context(self.nc.semaphore(name))
            self.cnt[name] = 0
        return self.semh[name]

    def wait(self, eng, tok):
        if tok is None:
            return
        sname, val = tok
        if val <= 0 or self.last_wait[eng].get(sname, 0) >= val:
            return
        self.last_wait[eng][sname] = val
        h = self.sem(sname)
        self.ops[eng].append(lambda E, h=h, v=val: E.wait_ge(h, v))

    def step(self, eng, sname, build, deps=(), dma=False):
        for d in deps:
            self.wait(eng, d)
        self.sem(sname)
        amt = 16 if dma else 1
        self.cnt[sname] += amt
        v = self.cnt[sname]
        h = self.semh[sname]
        self.ops[eng].append(lambda E, h=h, a=amt, bf=build: bf(E).then_inc(h, a))
        return (sname, v)

    def run(self, eng, build, deps=()):
        for d in deps:
            self.wait(eng, d)
        self.ops[eng].append(build)

    def barrier(self):
        toks = [(s, c) for s, c in self.cnt.items() if c > 0]
        for e in ENGS:
            for t in toks:
                self.wait(e, t)


def build_program():
    nc = bass.Bass("TRN2", target_bir_lowering=False, debug=False)
    es = ExitStack()
    b = Bld(nc, es)

    def din(name, shape, dt=F32):
        return nc.dram_tensor(name, list(shape), dt, kind="ExternalInput").ap()

    xall = din("xall", [S, D])
    xown = din("xown", [1024, D])
    wq = din("wq", [D, 1024]); wk = din("wk", [D, 128]); wv = din("wv", [D, 128])
    wo = din("wo", [1024, D])
    cup = din("cup", [HSN, D], I32); aup = din("aup", [HSN, 64])
    bup = din("bup", [HCH_N, 128])
    laup = din("laup", [D, 16]); lbup = din("lbup", [16, HSN])
    cdn = din("cdn", [D, HSN], I32); adn = din("adn", [D, HSN // 64])
    ldna = din("ldna", [HSN, 16]); ldnb = din("ldnb", [16, D])
    bdn8 = din("bdn8", [1, 4096])
    wqkv = din("wqkv", [D, 3072]); bqkv = din("bqkv", [24, 128])
    wo1 = din("wo1", [1024, D]); bo8 = din("bo8", [1, 4096])
    gw = din("gw", [8, D])
    la1 = din("la1", [512, D])
    lb1 = din("lb1", [8 * D, 64])
    ident_in = din("identI", [128, 128])
    ones_in = din("onesI", [1, 128])
    rmat_in = din("rmatI", [8, 512])

    yout = nc.dram_tensor("y", [1024, D], F32, kind="ExternalOutput").ap()

    dbg_copies = []

    def dint(name, shape, dt=F32):
        ap = nc.dram_tensor(name, list(shape), dt, kind="Internal").ap()
        if name in _DEBUG_OUT:
            dap = nc.dram_tensor(name + "_dbg", list(shape), dt,
                                 kind="ExternalOutput").ap()
            dbg_copies.append((ap, dap, list(shape)))
        return ap

    WUPT = dint("WUPT", [D, HSN])
    WDNT = dint("WDNT", [HSN, D])
    BF = BF16
    WUPTH = dint("WUPTH", [D, HSN], BF); WUPTL = dint("WUPTL", [D, HSN], BF)
    WDNTH = dint("WDNTH", [HSN, D], BF); WDNTL = dint("WDNTL", [HSN, D], BF)
    WQH = dint("WQH", [D, 1024], BF); WQL = dint("WQL", [D, 1024], BF)
    WKH = dint("WKH", [D, 128], BF); WKL = dint("WKL", [D, 128], BF)
    WVH = dint("WVH", [D, 128], BF); WVL = dint("WVL", [D, 128], BF)
    WOH = dint("WOH", [1024, D], BF); WOL = dint("WOL", [1024, D], BF)
    WQKVH = dint("WQKVH", [D, 3072], BF); WQKVL = dint("WQKVL", [D, 3072], BF)
    LAUPH = dint("LAUPH", [D, 16], BF); LAUPL = dint("LAUPL", [D, 16], BF)
    AATH = dint("AATH", [D, 512], BF); AATL = dint("AATL", [D, 512], BF)
    QSH = dint("QSH", [1024, S], BF); QSL = dint("QSL", [1024, S], BF)
    AOSH = dint("AOSH", [1024, S], BF); AOSL = dint("AOSL", [1024, S], BF)
    X2SH = dint("X2SH", [HSN, S], BF); X2SL = dint("X2SL", [HSN, S], BF)
    QS = dint("QS", [1024, S])
    AOS = dint("AOS", [1024, S])
    APS = dint("APS", [S, D])
    X2S = dint("X2S", [HSN, S])
    PARTQ = [dint(f"PARTQ{i}", [S, 1024]) for i in range(4)]
    R2OUTQ = [dint(f"R2OUTQ{i}", [1024, 1024]) for i in range(4)]
    X1S = dint("X1S", [1024, D])
    QS1 = dint("QS1", [1024, 1024])
    AGIN = dint("AGIN", [2048, 1024])
    AGOUT = dint("AGOUT", [8192, 1024])
    AAT = dint("AAT", [D, 512])
    BTS = dint("BTS", [512, D])

    def sbt(stk, name, shape, dt=F32, side=None):
        return stk.enter_context(nc.sbuf_tensor(name, list(shape), dt, side=side))

    ident = sbt(es, "ident", [128, 128])
    identB = sbt(es, "identB", [128, 128], BF16)
    trimask = sbt(es, "trimask", [128, 128])
    ones = sbt(es, "ones", [1, 128])
    s_bup = sbt(es, "s_bup", [128, HCH_N])
    s_lbup = sbt(es, "s_lbup", [16, HSN])

    s_bqkv = sbt(es, "s_bqkv", [128, 24])

    s_gwtH = sbt(es, "s_gwtH", [128, 256], BF16)
    s_gwtL = sbt(es, "s_gwtL", [128, 256], BF16)
    keep01 = sbt(es, "keep01", [128, 128])
    sstat = sbt(es, "sstat", [128, 16])
    PS = es.enter_context(nc.psum_tensor("PS", [128, 4096], F32))

    bank_tok = [None] * 8

    def psb(k, w=512):
        return PS[:, 512 * k: 512 * k + w]

    def load(slot, dst, src, deps=(), slow=False):
        return b.step("y", slot, lambda E, d=dst, s=src, sl=slow: E.dma_start(
            out=d, in_=s, allow_slow_non_contiguous=sl), deps=deps, dma=True)

    def store(slot, dst, src, deps=()):
        return b.step("g", slot, lambda E, d=dst, s=src: E.dma_start(out=d, in_=s),
                      deps=deps, dma=True)

    def mm_group(bank, emitters, deps):
        dd = [d for d in deps if d is not None]
        if bank_tok[bank] is not None:
            dd.append(bank_tok[bank])
            bank_tok[bank] = None
        nmm = len(emitters)
        tok = None
        for i, em in enumerate(emitters):
            st, sp = (i == 0), (i == nmm - 1)
            if sp:
                tok = b.step("t", "t", lambda E, em=em, st=st: em(E, st, True),
                             deps=(dd if st else ()))
            else:
                b.run("t", lambda E, em=em, st=st: em(E, st, False),
                      deps=(dd if st else ()))
        return tok

    def transpose_group(bank, blocks, deps):
        ems = []
        for (srcap, ooff) in blocks:
            isbf = (srcap.dtype == BF16)
            def em(E, st, sp, s=srcap, o=ooff, bk=bank, isbf=isbf):
                pn = s.shape[0]
                if isbf:
                    base = PS[:, 512 * bk:512 * bk + 256].bitcast(BF16)
                    return E.matmul(base[0:s.shape[1], o:o + pn], s,
                                    identB[0:pn, 0:pn],
                                    is_transpose=True, start=st, stop=sp)
                return E.matmul(PS[0:s.shape[1], 512 * bk + o: 512 * bk + o + pn],
                                s, ident[0:pn, 0:pn],
                                is_transpose=True, start=st, stop=sp)
            ems.append(em)
        return mm_group(bank, ems, deps)

    t_id = load("d_c", ident[:], ident_in[:, :])
    load("d_c", ones[:], ones_in[:, :])
    t_bup = load("d_c", s_bup[:].rearrange("p (c o) -> p c o", o=1),
                 bup[:, :].rearrange("c (p o) -> p c o", o=1), slow=True)
    t_lbup = load("d_c", s_lbup[:], lbup[:, :])

    t_bqkv = load("d_c", s_bqkv[:].rearrange("p (c o) -> p c o", o=1),
                  bqkv[:, :].rearrange("c (p o) -> p c o", o=1), slow=True)

    t_idb = b.step("v", "v", lambda E: E.tensor_copy(identB[:], ident[:]), deps=[t_id])
    t_io = b.step("g", "g", lambda E: E.iota(keep01[:], pattern=[[-1, 128]], base=0,
                                             channel_multiplier=1,
                                             allow_small_or_imprecise_dtypes=True))
    t_tm0 = b.step("v", "v", lambda E: E.tensor_scalar(trimask[:], keep01[:], 0.0, NEG,
                                                       ALU.is_ge, ALU.mult), deps=[t_io])
    b.step("v", "v", lambda E: E.tensor_scalar(keep01[:], keep01[:], 0.0, 1.0,
                                               ALU.is_lt, ALU.mult), deps=[t_tm0])
    b.barrier()

    es_pw = ExitStack()
    PW32 = sbt(es_pw, "pw32", [128, 2 * 4096])
    PWH = sbt(es_pw, "pwh", [128, 4096], BF16)
    PWL = sbt(es_pw, "pwl", [128, 4096], BF16)
    pw_free = [None, None]
    pwh_free = None
    pwl_free = None

    def presplit(src_ap, hi_ap, lo_ap, nrows, ncols):
        nonlocal_state = None
        for r0 in range(0, nrows, 128):
            sl = (r0 // 128) % 2
            stg = PW32[:, sl * 4096: sl * 4096 + ncols]
            ld = load(f"d_pw{sl}", stg, src_ap[r0:r0 + 128, :], deps=[pw_free[sl]])
            hi = b.step("s", "s", lambda E, st=stg, n=ncols: E.copy(PWH[:, 0:n], st),
                        deps=[ld, b._pwh_free])
            lo = b.step("g", "g", lambda E, st=stg, n=ncols: E.tensor_tensor(
                PWL[:, 0:n], st, PWH[:, 0:n], ALU.subtract), deps=[hi, b._pwl_free])
            pw_free[sl] = lo
            b._pwh_free = store("d_pwo", hi_ap[r0:r0 + 128, :], PWH[:, 0:ncols], deps=[hi])
            b._pwl_free = store("d_pwo", lo_ap[r0:r0 + 128, :], PWL[:, 0:ncols], deps=[lo])

    b._pwh_free = None
    b._pwl_free = None
    presplit(wq, WQH, WQL, D, 1024)
    presplit(wk, WKH, WKL, D, 128)
    presplit(wv, WVH, WVL, D, 128)
    presplit(wo, WOH, WOL, 1024, D)
    presplit(wqkv, WQKVH, WQKVL, D, 3072)
    presplit(laup, LAUPH, LAUPL, D, 16)


    es_pd = ExitStack()
    A1 = sbt(es_pd, "pdA1", [128, 16384])
    EV = sbt(es_pd, "pdEV", [128, 4096])
    EVH = sbt(es_pd, "pdEVH", [128, 1024], BF16)
    EVL = sbt(es_pd, "pdEVL", [128, 1024], BF16)
    AMB = sbt(es_pd, "pdAMB", [128, 64])
    AMBG = sbt(es_pd, "pdAMBG", [128, 64])
    DQAG = sbt(es_pd, "pdDQAG", [128, 4096])
    DQTG = sbt(es_pd, "pdDQTG", [128, 4096])
    cbuf = [A1[:, 0:4096].bitcast(I32), A1[:, 4096:8192].bitcast(I32)]
    dqa = A1[:, 8192:12288]
    dqt = A1[:, 12288:16384]
    cslot_free = [None, None]
    am_free = None
    dqa_free = None
    ev_free = [None, None]

    def dequant_on(eng, acc, tmp, amb2, cd, ncols, nblk, dep_ld, dep_am, dep_free):
        tok = b.step(eng, eng, lambda E, c=cd, n=ncols, a=acc: E.tensor_scalar(
            a[:, 0:n], c, 0.0, NF4[0], ALU.is_equal, ALU.mult),
            deps=[dep_ld, dep_free])
        for k in range(1, 16):
            tk = b.step(eng, eng, lambda E, c=cd, n=ncols, t=tmp, kk=float(k), vv=NF4[k]:
                        E.tensor_scalar(t[:, 0:n], c, kk, vv, ALU.is_equal, ALU.mult),
                        deps=[tok])
            tok = b.step(eng, eng, lambda E, n=ncols, a=acc, t=tmp: E.tensor_tensor(
                a[:, 0:n], a[:, 0:n], t[:, 0:n], ALU.add), deps=[tk])
        tok = b.step(eng, eng, lambda E, n=ncols, nb=nblk, a=acc, m=amb2: E.tensor_tensor(
            a[:, 0:n].rearrange("p (blk e) -> p blk e", e=64),
            a[:, 0:n].rearrange("p (blk e) -> p blk e", e=64),
            m[:, 0:nb].unsqueeze(2).broadcast_to((128, nb, 64)), ALU.mult),
            deps=[tok, dep_am])
        return tok

    def dequant(cd, ncols, nblk, dep_ld, dep_am, dep_free):
        return dequant_on("v", dqa, dqt, AMB, cd, ncols, nblk, dep_ld, dep_am, dep_free)

    def pd_sweep(nsrc, src_cols, nblocks, blkw, dst_fn, dep_src_fn, am_src_fn):
        pass

    amg_free = None
    dqg_free = None
    for hc in range(HCH_N):
        sl = hc % 2
        use_g = False
        ldk = load(f"d_cd{sl}", cbuf[sl][:, :], cup[128 * hc:128 * (hc + 1), :],
                   deps=[cslot_free[sl]])
        if use_g:
            lda = load("d_amg", AMBG[:, 0:64], aup[128 * hc:128 * (hc + 1), :],
                       deps=[amg_free])
            dq = dequant_on("g", DQAG, DQTG, AMBG, cbuf[sl][:, :], 4096, 64,
                            ldk, lda, dqg_free)
            amg_free = dq
            dqg_free = None
        else:
            lda = load("d_am", AMB[:, 0:64], aup[128 * hc:128 * (hc + 1), :],
                       deps=[am_free])
            dq = dequant(cbuf[sl][:, :], 4096, 64, ldk, lda, dqa_free)
            am_free = dq
        cslot_free[sl] = dq
        accb = DQAG if use_g else dqa
        last_tp = None
        for g in range(8):
            bank = g % 4
            blocks = [(accb[:, (4 * g + i) * 128:(4 * g + i + 1) * 128], i * 128)
                      for i in range(4)]
            tp = transpose_group(bank, blocks, deps=[dq])
            sh = EVH[:, (g % 2) * 512:(g % 2) * 512 + 512]
            slo = EVL[:, (g % 2) * 512:(g % 2) * 512 + 512]
            ehk = b.step("s", "s", lambda E, bk=bank, st=sh: E.copy(st, psb(bk)),
                         deps=[tp, ev_free[g % 2]])
            elk = b.step("v", "v", lambda E, bk=bank, st=slo, hh=sh: E.tensor_tensor(
                st, psb(bk), hh, ALU.subtract), deps=[ehk])
            bank_tok[bank] = elk
            dsth = WUPTH.rearrange("(c p) h -> p c h", p=128)[:, 4 * g:4 * g + 4,
                                                              128 * hc:128 * (hc + 1)]
            dstl = WUPTL.rearrange("(c p) h -> p c h", p=128)[:, 4 * g:4 * g + 4,
                                                              128 * hc:128 * (hc + 1)]
            store(f"d_w{g % 2}", dsth, sh.rearrange("p (c f) -> p c f", f=128), deps=[ehk])
            ev_free[g % 2] = store(f"d_w{g % 2}", dstl,
                                   slo.rearrange("p (c f) -> p c f", f=128), deps=[elk])
            last_tp = tp
        if use_g:
            dqg_free = last_tp
        else:
            dqa_free = last_tp

    JCD = sbt(es_pd, "pdJCD", [128, 2816], I32)
    JAM = sbt(es_pd, "pdJAM", [128, 44])
    JEVH = sbt(es_pd, "pdJEVH", [128, 1024], BF16)
    JEVL = sbt(es_pd, "pdJEVL", [128, 1024], BF16)
    jstate = {"cd": [None, None], "am": [None, None],
              "acc": [dqa_free, dqg_free], "ev": [None, None]}

    def pd_down_job(jid, eng):
        dc, half = jid // 2, jid % 2
        hlo_c = 0 if half == 0 else 11
        nhc = 11
        hw = nhc * 128
        blo = hlo_c * 2
        nblk = nhc * 2
        gi = 0 if eng == "v" else 1
        cdj = JCD[:, gi * 1408: gi * 1408 + hw]
        amj = JAM[:, gi * 22: gi * 22 + nblk]
        acc = dqa if eng == "v" else DQAG
        tmp = dqt if eng == "v" else DQTG
        ldk = load(f"d_jcd{gi}", cdj,
                   cdn[128 * dc:128 * (dc + 1), hlo_c * 128: hlo_c * 128 + hw],
                   deps=[jstate["cd"][gi]])
        lda = load(f"d_jam{gi}", amj, adn[128 * dc:128 * (dc + 1), blo:blo + nblk],
                   deps=[jstate["am"][gi]])
        dq = dequant_on(eng, acc, tmp, JAM if False else amj.unsqueeze(0).squeeze(0), hw, nblk, ldk, lda, jstate["acc"][gi]) if False else None
        tok = b.step(eng, eng, lambda E, c=cdj, n=hw, a=acc: E.tensor_scalar(
            a[:, 0:n], c, 0.0, NF4[0], ALU.is_equal, ALU.mult),
            deps=[ldk, jstate["acc"][gi]])
        for k in range(1, 16):
            tk = b.step(eng, eng, lambda E, c=cdj, n=hw, t=tmp, kk=float(k), vv=NF4[k]:
                        E.tensor_scalar(t[:, 0:n], c, kk, vv, ALU.is_equal, ALU.mult),
                        deps=[tok])
            tok = b.step(eng, eng, lambda E, n=hw, a=acc, t=tmp: E.tensor_tensor(
                a[:, 0:n], a[:, 0:n], t[:, 0:n], ALU.add), deps=[tk])
        dq = b.step(eng, eng, lambda E, n=hw, nb=nblk, a=acc, m=amj: E.tensor_tensor(
            a[:, 0:n].rearrange("p (blk e) -> p blk e", e=64),
            a[:, 0:n].rearrange("p (blk e) -> p blk e", e=64),
            m.unsqueeze(2).broadcast_to((128, nb, 64)), ALU.mult),
            deps=[tok, lda])
        jstate["cd"][gi] = dq
        jstate["am"][gi] = dq
        last_tp = None
        for g in range(3):
            bank = (4 + 2 * gi) % 8 + (g % 2)
            ilo = 4 * g
            ihi = min(4 * g + 4, nhc)
            if ihi <= ilo:
                break
            blocks = [(acc[:, (ilo + i) * 128:(ilo + i + 1) * 128], i * 128)
                      for i in range(ihi - ilo)]
            tp = transpose_group(bank, blocks, deps=[dq])
            w = (ihi - ilo) * 128
            sh = JEVH[:, gi * 512:gi * 512 + w]
            slo2 = JEVL[:, gi * 512:gi * 512 + w]
            ehk = b.step("s", "s", lambda E, bk=bank, st=sh, w=w: E.copy(
                st, PS[:, 512 * bk:512 * bk + w]), deps=[tp, jstate["ev"][gi]])
            elk = b.step("v", "v", lambda E, bk=bank, st=slo2, hh=sh, w=w: E.tensor_tensor(
                st, PS[:, 512 * bk:512 * bk + w], hh, ALU.subtract), deps=[ehk])
            bank_tok[bank] = elk
            dsth = WDNTH.rearrange("(c p) d -> p c d", p=128)[:, hlo_c + ilo: hlo_c + ihi,
                                                              128 * dc:128 * (dc + 1)]
            dstl = WDNTL.rearrange("(c p) d -> p c d", p=128)[:, hlo_c + ilo: hlo_c + ihi,
                                                              128 * dc:128 * (dc + 1)]
            store(f"d_jw{gi}", dsth, sh.rearrange("p (c f) -> p c f", f=128), deps=[ehk])
            jstate["ev"][gi] = store(f"d_jw{gi}", dstl,
                                     slo2.rearrange("p (c f) -> p c f", f=128), deps=[elk])
            last_tp = tp
        jstate["acc"][gi] = last_tp

    for jid in range(64):
        pd_down_job(jid, "v")


    for erc in range(4):
        sl = erc % 2
        src = cbuf[sl][:, 0:4096].bitcast(F32)
        ldk = load(f"d_cd{sl}", src, la1[128 * erc:128 * (erc + 1), :],
                   deps=[cslot_free[sl]])
        for g in range(8):
            bank = g % 4
            blocks = [(src[:, (4 * g + i) * 128:(4 * g + i + 1) * 128], i * 128)
                      for i in range(4)]
            tp = transpose_group(bank, blocks, deps=[ldk])
            sh = EVH[:, (g % 2) * 512:(g % 2) * 512 + 512]
            slo = EVL[:, (g % 2) * 512:(g % 2) * 512 + 512]
            ehk = b.step("s", "s", lambda E, bk=bank, st=sh: E.copy(st, psb(bk)),
                         deps=[tp, ev_free[g % 2]])
            elk = b.step("v", "v", lambda E, bk=bank, st=slo, hh=sh: E.tensor_tensor(
                st, psb(bk), hh, ALU.subtract), deps=[ehk])
            bank_tok[bank] = elk
            dsth = AATH.rearrange("(c p) e -> p c e", p=128)[:, 4 * g:4 * g + 4,
                                                             128 * erc:128 * (erc + 1)]
            dstl = AATL.rearrange("(c p) e -> p c e", p=128)[:, 4 * g:4 * g + 4,
                                                             128 * erc:128 * (erc + 1)]
            store(f"d_w{g % 2}", dsth, sh.rearrange("p (c f) -> p c f", f=128), deps=[ehk])
            ev_free[g % 2] = store(f"d_w{g % 2}", dstl,
                                   slo.rearrange("p (c f) -> p c f", f=128), deps=[elk])
            cslot_free[sl] = tp

    for e in range(8):
        sl = e % 2
        src = cbuf[sl][:, 0:2048].bitcast(F32)
        ldk = load(f"d_cd{sl}", src.rearrange("p (c r) -> p c r", r=64),
                   lb1[e * D:(e + 1) * D, :].rearrange("(c p) r -> p c r", p=128),
                   deps=[cslot_free[sl]])
        for g in range(8):
            bank = g % 4
            blocks = [(src[:, (4 * g + i) * 64:(4 * g + i + 1) * 64], i * 128)
                      for i in range(4)]
            tp = transpose_group(bank, blocks, deps=[ldk])
            stg = EV[:, (g % 2) * 2048:(g % 2) * 2048 + 512]
            evk = b.step("s", "s", lambda E, bk=bank, st=stg: E.copy(
                st[0:64, :], PS[0:64, 512 * bk:512 * bk + 512]),
                deps=[tp, ev_free[g % 2]])
            bank_tok[bank] = evk
            dst = BTS[64 * e:64 * e + 64, :].rearrange("r (c f) -> r c f", f=128)[:, 4 * g:4 * g + 4, :]
            ev_free[g % 2] = store(f"d_w{g % 2}", dst,
                                   stg[0:64, :].rearrange("p (c f) -> p c f", f=128),
                                   deps=[evk])
            cslot_free[sl] = tp

    src = cbuf[0][:, 0:4096].bitcast(F32)
    ldk = load("d_cd0", src[0:8, :], gw[:, :], deps=[cslot_free[0]])
    for g in range(8):
        bank = g % 4
        blocks = [(src[0:8, (4 * g + i) * 128:(4 * g + i + 1) * 128], i * 8)
                  for i in range(4)]
        tp = transpose_group(bank, blocks, deps=[ldk])
        ehk = b.step("s", "s", lambda E, bk=bank, g=g: E.copy(
            s_gwtH[:, 32 * g:32 * g + 32], PS[:, 512 * bk:512 * bk + 32]), deps=[tp])
        evk = b.step("v", "v", lambda E, bk=bank, g=g: E.tensor_tensor(
            s_gwtL[:, 32 * g:32 * g + 32], PS[:, 512 * bk:512 * bk + 32],
            s_gwtH[:, 32 * g:32 * g + 32], ALU.subtract), deps=[ehk])
        bank_tok[bank] = evk

    b.barrier()
    es_pd.close()
    es_pw.close()
    for i in range(8):
        bank_tok[i] = None

    if _STOP_AFTER == "pd":
        return _finish(nc, es, b, dbg_copies)

    es_kv = ExitStack()
    es_l1 = ExitStack()
    prev_sq = [None]

    def ln_subtile(XT, XSQ, dep_ld):
        t1 = b.step("v", "v", lambda E, X=XT: E.reduce_sum(sstat[:, 0:1], X[:, 0:D], AXX),
                    deps=[dep_ld])
        t2 = b.step("v", "v", lambda E: E.tensor_scalar(
            sstat[:, 1:2], sstat[:, 0:1], -1.0 / D, None, ALU.mult), deps=[t1])
        t3 = b.step("s", "s", lambda E, X=XT, Q=XSQ: E.activation(
            Q[:, 0:D], X[:, 0:D], AF.Square, bias=sstat[:, 1:2], scale=1.0,
            accum_out=sstat[:, 2:3]), deps=[t2, dep_ld, prev_sq[0]])
        prev_sq[0] = t3
        t4 = b.step("v", "v", lambda E: E.tensor_scalar(
            sstat[:, 3:4], sstat[:, 2:3], 1.0 / D, EPS, ALU.mult, ALU.add), deps=[t3])
        t5 = b.step("s", "s", lambda E: E.activation(sstat[:, 4:5], sstat[:, 3:4],
                                                     AF.Sqrt), deps=[t4])
        t6 = b.step("v", "v", lambda E: E.reciprocal(sstat[:, 5:6], sstat[:, 4:5]),
                    deps=[t5])
        t7 = b.step("v", "v", lambda E: E.tensor_tensor(
            sstat[:, 6:7], sstat[:, 1:2], sstat[:, 5:6], ALU.mult), deps=[t6])
        t8 = b.step("v", "v", lambda E, X=XT: E.tensor_scalar(
            X[:, 0:D], X[:, 0:D], sstat[:, 5:6], sstat[:, 6:7], ALU.mult, ALU.add),
            deps=[t7])
        return t8

    es_p1 = ExitStack()
    XNH = sbt(es_p1, "p1XNH", [128, 16384], BF16)
    XNL = sbt(es_p1, "p1XNL", [128, 16384], BF16)
    SLAB = sbt(es_p1, "p1SLAB", [128, 16384], BF16)
    XT = sbt(es_p1, "p1XT", [128, 4096])
    XSQ = sbt(es_p1, "p1XSQ", [128, 4096])
    XH = sbt(es_p1, "p1XH", [128, 4096], BF16)
    XL = sbt(es_p1, "p1XL", [128, 4096], BF16)
    STG = sbt(es_p1, "p1STG", [128, 2048])
    STGB = sbt(es_p1, "p1STGB", [128, 4096], BF16)
    vstg = STG[:, 0:512]
    x2stg = [STG[:, 512:1024], STG[:, 1024:1536]]
    uub = STG[0:16, 1536:2048]
    qstgh = [STGB[:, 0:512], STGB[:, 512:1024]]
    qstgl = [STGB[:, 1024:1536], STGB[:, 1536:2048]]
    KSB = sbt(es_kv, "KSBH", [128, 4096], BF16, side="right")
    KSBL = sbt(es_kv, "KSBL", [128, 4096], BF16, side="right")
    VTM = sbt(es_kv, "VTM", [128, 4096], side="right")
    slab_free = [None, None]
    stg_free = {}
    last_xt_user = [None]
    last_xh_user = [None]
    last_xnf_user = [None]
    p1_bank = [0]

    def psb16(k):
        return PS[:, 512 * k:512 * k + 256].bitcast(BF16)

    ready_h = [None]
    ready_l = [None]

    def make_xnf_tile(src_rows_fn, t0):
        ready = None
        for sub in range(4):
            ldx = load("d_xt", XT[:], src_rows_fn(t0 + sub * 128),
                       deps=[last_xt_user[0]])
            lnk = ln_subtile(XT, XSQ, ldx)
            hk = b.step("s", "s", lambda E: E.copy(XH[:], XT[:]), deps=[lnk, last_xh_user[0]])
            lk = b.step("v", "v", lambda E: E.tensor_tensor(XL[:], XT[:], XH[:],
                                                            ALU.subtract), deps=[hk])
            last_xt_user[0] = lk
            for comp in range(2):
                XSRC = XH if comp == 0 else XL
                XDST = XNH if comp == 0 else XNL
                ctok = hk if comp == 0 else lk
                for g in range(8):
                    bank = 4 + ((2 * g + comp) % 4)
                    blocks = [(XSRC[:, (4 * g + i) * 128:(4 * g + i + 1) * 128], i * 128)
                              for i in range(4)]
                    tp = transpose_group(bank, blocks, deps=[ctok, last_xnf_user[0]])
                    dst = XDST.rearrange("p (c t) -> p c t", t=512)[:, 4 * g:4 * g + 4,
                                                                   sub * 128:(sub + 1) * 128]
                    eng = "s" if comp == 0 else "v"
                    if comp == 0:
                        evk = b.step("s", "s", lambda E, bk=bank, d=dst: E.copy(
                            d, psb16(bk).rearrange("p (c f) -> p c f", f=128)), deps=[tp])
                    else:
                        evk = b.step("v", "v", lambda E, bk=bank, d=dst: E.tensor_copy(
                            d, psb16(bk).rearrange("p (c f) -> p c f", f=128)), deps=[tp])
                    bank_tok[bank] = evk
                    if comp == 0:
                        ready_h[0] = evk
                    else:
                        ready_l[0] = evk
                    last_xh_user[0] = tp
        return (ready_h[0], ready_l[0])

    def fclist(ti):
        return ([("AUP", 0)] + [("Q", j) for j in range(8)] + [("K", 0), ("V", 0)]
                + [("UP", j) for j in range(HCH_N)])

    for ti in range(NT):
        xnf_ready = make_xnf_tile(lambda r: xall[r:r + 128, :], ti * 512)
        uup_evac = None
        mm_tok = None
        for fci, (kind, idx) in enumerate(fclist(ti)):
            sl = fci % 2
            slab = SLAB[:, sl * 8192:(sl + 1) * 8192]
            if kind == "Q":
                srch = WQH[:, 128 * idx:128 * (idx + 1)]
                srcl = WQL[:, 128 * idx:128 * (idx + 1)]; w = 128
            elif kind == "K":
                srch = WKH[:, :]; srcl = WKL[:, :]; w = 128
            elif kind == "V":
                srch = WVH[:, :]; srcl = WVL[:, :]; w = 128
            elif kind == "AUP":
                srch = LAUPH[:, :]; srcl = LAUPL[:, :]; w = 16
            else:
                srch = WUPTH[:, 128 * idx:128 * (idx + 1)]
                srcl = WUPTL[:, 128 * idx:128 * (idx + 1)]; w = 128
            load(f"d_sl{sl}", slab[:, 0:32 * w].rearrange("p (c f) -> p c f", f=w),
                 srch.rearrange("(c p) f -> p c f", p=128), deps=[slab_free[sl]])
            slt = load(f"d_sl{sl}",
                       slab[:, 32 * w:64 * w].rearrange("p (c f) -> p c f", f=w),
                       srcl.rearrange("(c p) f -> p c f", p=128))
            bank = p1_bank[0]
            p1_bank[0] = (p1_bank[0] + 1) % 4
            m = 16 if kind == "AUP" else 128
            ems = []
            for dc in range(DC):
                for ph, xsrc in ((0, XNH), (1, XNL), (2, XNH)):
                    woff = (0 if ph < 2 else 32 * w) + dc * w
                    def em(E, st, sp, woff=woff, xs=xsrc, dc=dc, sl2=slab, w=w,
                           bk=bank, m=m):
                        return E.matmul(PS[0:m, 512 * bk:512 * bk + 512],
                                        sl2[:, woff:woff + w],
                                        xs[:, dc * 512:(dc + 1) * 512], start=st, stop=sp)
                    ems.append(em)
            dd = [slt] + list(xnf_ready)
            if kind == "AUP":
                dd.append(last_xnf_user[0])
            if kind == "UP":
                def emL(E, st, sp, bk=bank, idx=idx):
                    return E.matmul(PS[0:128, 512 * bk:512 * bk + 512],
                                    s_lbup[:, 128 * idx:128 * (idx + 1)],
                                    uub, start=st, stop=sp)
                ems.append(emL)
                dd += [uup_evac, t_lbup]
            mm_tok = mm_group(bank, ems, dd)
            slab_free[sl] = mm_tok
            if kind == "AUP":
                uup_evac = b.step("v", "v", lambda E, bk=bank:
                                  E.tensor_copy(uub, PS[0:16, 512 * bk:512 * bk + 512]),
                                  deps=[mm_tok])
                bank_tok[bank] = uup_evac
            elif kind == "K":
                ehk = b.step("s", "s", lambda E, bk=bank, ti=ti: E.copy(
                    KSB[:, ti * 512:(ti + 1) * 512], psb(bk)), deps=[mm_tok])
                elk = b.step("v", "v", lambda E, bk=bank, ti=ti: E.tensor_tensor(
                    KSBL[:, ti * 512:(ti + 1) * 512], psb(bk),
                    KSB[:, ti * 512:(ti + 1) * 512], ALU.subtract), deps=[ehk])
                bank_tok[bank] = elk
            elif kind == "V":
                evk = b.step("s", "s", lambda E, bk=bank: E.copy(vstg[:], psb(bk)),
                             deps=[mm_tok, stg_free.get("v")])
                bank_tok[bank] = evk
                vb = 4 + (fci % 4)
                blocks = [(vstg[:, i * 128:(i + 1) * 128], i * 128) for i in range(4)]
                tp = transpose_group(vb, blocks, deps=[evk])
                stg_free["v"] = tp
                vev = b.step("v", "v", lambda E, bk=vb, ti=ti: E.tensor_copy(
                    VTM[:, ti * 512:(ti + 1) * 512], psb(bk)), deps=[tp])
                bank_tok[vb] = vev
            elif kind == "Q":
                qh = qstgh[fci % 2]
                ql = qstgl[fci % 2]
                ehk = b.step("s", "s", lambda E, bk=bank, st=qh: E.copy(st, psb(bk)),
                             deps=[mm_tok, stg_free.get(f"qh{fci % 2}")])
                elk = b.step("v", "v", lambda E, bk=bank, sl3=ql, hh=qh: E.tensor_tensor(
                    sl3, psb(bk), hh, ALU.subtract),
                    deps=[ehk, stg_free.get(f"ql{fci % 2}")])
                bank_tok[bank] = elk
                stg_free[f"qh{fci % 2}"] = store(
                    "d_qs", QSH[128 * idx:128 * (idx + 1), ti * 512:(ti + 1) * 512],
                    qh, deps=[ehk])
                stg_free[f"ql{fci % 2}"] = store(
                    "d_qs", QSL[128 * idx:128 * (idx + 1), ti * 512:(ti + 1) * 512],
                    ql, deps=[elk])
            else:
                xs = x2stg[fci % 2]
                xh2 = STGB[:, 2048 + (fci % 2) * 512: 2048 + (fci % 2) * 512 + 512]
                xl2 = STGB[:, 3072 + (fci % 2) * 512: 3072 + (fci % 2) * 512 + 512]
                r32 = b.step("s", "s", lambda E, bk=bank, st=xs, idx=idx: E.activation(
                    st, psb(bk), AF.Relu, bias=s_bup[:, idx:idx + 1], scale=1.0),
                    deps=[mm_tok, t_bup, stg_free.get(f"x{fci % 2}")])
                hk = b.step("s", "s", lambda E, st=xs, d=xh2: E.copy(d, st),
                            deps=[r32, stg_free.get(f"xh{fci % 2}")])
                lk = b.step("v", "v", lambda E, st=xs, d=xl2, hh=xh2: E.tensor_tensor(
                    d, st, hh, ALU.subtract), deps=[hk, stg_free.get(f"xl{fci % 2}")])
                bank_tok[bank] = r32
                stg_free[f"x{fci % 2}"] = lk
                stg_free[f"xh{fci % 2}"] = store(
                    "d_x2", X2SH[128 * idx:128 * (idx + 1), ti * 512:(ti + 1) * 512],
                    xh2, deps=[hk])
                stg_free[f"xl{fci % 2}"] = store(
                    "d_x2", X2SL[128 * idx:128 * (idx + 1), ti * 512:(ti + 1) * 512],
                    xl2, deps=[lk])
        last_xnf_user[0] = mm_tok

    b.barrier()
    es_p1.close()
    for i in range(8):
        bank_tok[i] = None

    if _STOP_AFTER == "p1":
        return _finish(nc, es, b, dbg_copies)

    es_p2 = ExitStack()
    A1 = sbt(es_p2, "p2A1", [128, 16384])
    A2 = sbt(es_p2, "p2A2", [128, 16384])
    SM = sbt(es_p2, "p2SM", [128, 1024])
    SMB = sbt(es_p2, "p2SMB", [128, 2048], BF16)
    sc = A1[:, 0:16384]
    at = A2[:, 0:16384]
    qtih = [SMB[:, 0:512], SMB[:, 1024:1536]]
    qtil = [SMB[:, 512:1024], SMB[:, 1536:2048]]
    SMB2 = sbt(es_p2, "p2SMB2", [128, 1024], BF16)
    aostgh = SMB2[:, 0:512]
    aostgl = SMB2[:, 512:1024]

    AT2 = sbt(es_p2, "p2AT2", [128, 1024])
    qld_free = [None, None]
    ao_free = None
    sc_free = None
    pr_free = None
    bank6_free = None
    for qb in range(8):
        kb0 = 0 if qb == 7 else qb
        nkb = 8 - kb0
        nkc = 4 * nkb
        kmax = 512 * nkb
        for h in range(8):
            qsl2 = h % 2
            load(f"d_q{qsl2}", qtih[qsl2][:], QSH[128 * h:128 * (h + 1), qb * 512:(qb + 1) * 512],
                 deps=[qld_free[qsl2]])
            qld = load(f"d_q{qsl2}", qtil[qsl2][:], QSL[128 * h:128 * (h + 1), qb * 512:(qb + 1) * 512])
            norm_toks = []
            last_mmt = None
            for sub in range(4):
                scs = sc[:, sub * 4096: sub * 4096 + kmax]
                prs = at[:, sub * 4096: sub * 4096 + kmax]
                ev_last = None
                for kb in range(nkb):
                    bank = sub % 2
                    ems0 = []
                    for qa, ka in ((qtih[qsl2], KSB), (qtih[qsl2], KSBL), (qtil[qsl2], KSB)):
                        def em(E, st, sp, kb=kb, sub=sub, bk=bank, kb0=kb0, qa=qa, ka=ka):
                            return E.matmul(psb(bk), qa[:, sub * 128:(sub + 1) * 128],
                                            ka[:, (kb0 + kb) * 512:(kb0 + kb + 1) * 512],
                                            start=st, stop=sp)
                        ems0.append(em)
                    mmt = mm_group(bank, ems0, [qld, sc_free])
                    last_mmt = mmt
                    ev_last = b.step("s", "s",
                                     lambda E, bk=bank, d=scs[:, kb * 512:(kb + 1) * 512]:
                                     E.copy(d, psb(bk)), deps=[mmt])
                    bank_tok[bank] = ev_last
                dcol = qb * 512 + sub * 128 - 512 * kb0
                tok = ev_last
                if dcol > 0:
                    tok = b.step("v", "v", lambda E, d=scs[:, 0:dcol]:
                                 E.memset(d, NEG), deps=[tok])
                tok = b.step("v", "v", lambda E, d=scs[:, dcol:dcol + 128]:
                             E.tensor_tensor(d, d, keep01[:], ALU.mult), deps=[tok])
                tok = b.step("v", "v", lambda E, d=scs[:, dcol:dcol + 128]:
                             E.tensor_tensor(d, d, trimask[:], ALU.add), deps=[tok])
                mx = b.step("v", "v", lambda E, s=scs, km=kmax: E.reduce_max(
                    sstat[:, 8:9], s[:, 0:km], AXX), deps=[tok])
                nb2 = b.step("v", "v", lambda E: E.tensor_scalar(
                    sstat[:, 9:10], sstat[:, 8:9], -SC0, None, ALU.mult), deps=[mx])
                ex = b.step("s", "s", lambda E, s=scs, km=kmax: E.activation(
                    s[:, 0:km], s[:, 0:km], AF.Exp, bias=sstat[:, 9:10], scale=SC0,
                    accum_out=sstat[:, 10:11]), deps=[nb2])
                rc = b.step("v", "v", lambda E: E.reciprocal(
                    sstat[:, 11:12], sstat[:, 10:11]), deps=[ex])
                nm = b.step("v", "v", lambda E, s=scs, p=prs, km=kmax: E.tensor_scalar(
                    p[:, 0:km], s[:, 0:km], sstat[:, 11:12], None, ALU.mult),
                    deps=[rc, pr_free])
                norm_toks.append(nm)
            qld_free[qsl2] = last_mmt
            sc_free = norm_toks[-1]
            last_tp = None
            av_tok = [None, None]
            evks = [None] * nkc

            def emit_av(kc):
                sl = kc % 2
                av_tok[sl] = b.step("t", "t", lambda E, kc=kc, kb0=kb0, sl=sl, last=(kc == nkc - 1):
                                    E.matmul(psb(6),
                                             VTM[:, (4 * kb0 + kc) * 128:(4 * kb0 + kc + 1) * 128],
                                             AT2[:, sl * 512:(sl + 1) * 512],
                                             start=(kc == 0), stop=last),
                                    deps=([evks[kc], bank6_free] if kc == 0 else [evks[kc]]))

            for kc in range(nkc):
                bank = 2 + (kc % 2)
                blocks = [(at[:, sub * 4096 + kc * 128: sub * 4096 + (kc + 1) * 128],
                           sub * 128) for sub in range(4)]
                tp = transpose_group(bank, blocks, deps=norm_toks)
                sl = kc % 2
                evks[kc] = b.step("v", "v", lambda E, bk=bank, sl=sl:
                                  E.tensor_copy(AT2[:, sl * 512:(sl + 1) * 512], psb(bk)),
                                  deps=[tp, av_tok[sl]])
                bank_tok[bank] = evks[kc]
                if kc > 0:
                    emit_av(kc - 1)
                last_tp = tp
            emit_av(nkc - 1)
            pr_free = last_tp
            avt = av_tok[(nkc - 1) % 2]
            ehk = b.step("s", "s", lambda E: E.copy(aostgh[:], psb(6)),
                         deps=[avt, ao_free])
            elk = b.step("v", "v", lambda E: E.tensor_tensor(
                aostgl[:], psb(6), aostgh[:], ALU.subtract), deps=[ehk])
            bank6_free = elk
            store("d_ao", AOSH[128 * h:128 * (h + 1), qb * 512:(qb + 1) * 512],
                  aostgh[:], deps=[ehk])
            ao_free = store("d_ao", AOSL[128 * h:128 * (h + 1), qb * 512:(qb + 1) * 512],
                            aostgl[:], deps=[elk])

    b.barrier()
    es_p2.close()
    es_kv.close()
    for i in range(8):
        bank_tok[i] = None

    if _STOP_AFTER == "p2":
        return _finish(nc, es, b, dbg_copies)

    es_p3 = ExitStack()
    A1 = sbt(es_p3, "p3A1", [128, 32768], BF16)
    AOT = sbt(es_p3, "p3AOT", [128, 4096], BF16)
    EV3 = sbt(es_p3, "p3EV", [128, 4096])
    slab_tok = None
    slab_users3 = None
    ao_free3 = [None, None]
    ev_free3 = [None, None]
    for dh in range(2):
        load("d_wo", A1[:, 0:16384].rearrange("p (c f) -> p c f", f=2048),
             WOH.rearrange("(c p) d -> p c d", p=128)[:, :, dh * 2048:(dh + 1) * 2048],
             deps=[slab_tok, slab_users3])
        slab_tok = load("d_wo", A1[:, 16384:32768].rearrange("p (c f) -> p c f", f=2048),
                        WOL.rearrange("(c p) d -> p c d", p=128)[:, :, dh * 2048:(dh + 1) * 2048])
        for tc in range(32):
            sl = tc % 2
            aoth = AOT[:, sl * 1024:(sl + 1) * 1024]
            aotl = AOT[:, 2048 + sl * 1024: 2048 + (sl + 1) * 1024]
            load(f"d_aot{sl}", aoth.rearrange("p (c f) -> p c f", f=128),
                 AOSH.rearrange("(c p) t -> p c t", p=128)[:, :, tc * 128:(tc + 1) * 128],
                 deps=[ao_free3[sl]])
            aold = load(f"d_aot{sl}", aotl.rearrange("p (c f) -> p c f", f=128),
                        AOSL.rearrange("(c p) t -> p c t", p=128)[:, :, tc * 128:(tc + 1) * 128])
            b0 = 4 * (tc % 2)
            gtoks = []
            for nb in range(4):
                ems = []
                for j in range(8):
                    for aop, woff in ((aoth, 0), (aoth, 16384), (aotl, 0)):
                        def em(E, st, sp, nb=nb, j=j, b0=b0, a=aop, wo2=woff):
                            return E.matmul(PS[:, (b0 + nb) * 512:(b0 + nb) * 512 + 512],
                                            a[:, j * 128:(j + 1) * 128],
                                            A1[:, wo2 + j * 2048 + nb * 512: wo2 + j * 2048 + nb * 512 + 512],
                                            start=st, stop=sp)
                        ems.append(em)
                gt = mm_group(b0 + nb, ems, [aold, slab_tok])
                gtoks.append(gt)
            ao_free3[sl] = gtoks[-1]
            stg = EV3[:, sl * 2048:(sl + 1) * 2048]
            evk = b.step("v", "v", lambda E, b0=b0, st=stg: E.tensor_copy(
                st, PS[:, b0 * 512:b0 * 512 + 2048]), deps=gtoks + [ev_free3[sl]])
            for nb in range(4):
                bank_tok[b0 + nb] = evk
            ev_free3[sl] = store("d_aps", APS[tc * 128:(tc + 1) * 128,
                                              dh * 2048:(dh + 1) * 2048], stg, deps=[evk])
            slab_users3 = gtoks[-1]
    b.barrier()
    es_p3.close()
    for i in range(8):
        bank_tok[i] = None

    if _STOP_AFTER == "p3":
        return _finish(nc, es, b, dbg_copies)

    es_p5 = ExitStack()
    WSL = sbt(es_p5, "p5WSL", [128, 2 * HCH_N * 1024], BF16)
    X2T = sbt(es_p5, "p5X2T", [128, 2 * HCH_N * 512], BF16)
    ATL = sbt(es_p5, "p5ATL", [128, 2048])
    EV5 = sbt(es_p5, "p5EV", [128, 2048])
    UDB = sbt(es_p5, "p5UDB", [128, 512])
    s_ldna32 = sbt(es_p5, "p5ldna32", [128, HCH_N * 16])
    s_ldnaH = sbt(es_p5, "p5ldnaH", [128, HCH_N * 16], BF16)
    s_ldnaL = sbt(es_p5, "p5ldnaL", [128, HCH_N * 16], BF16)
    s_ldnb = sbt(es_p5, "p5ldnb", [16, D])
    s_bdn = sbt(es_p5, "p5bdn", [1, D])
    t_ldnb = load("d_c", s_ldnb[:], ldnb[:, :])
    t_bdn = load("d_c", s_bdn[:], bdn8[:, :])
    t_la32 = load("d_c", s_ldna32[:].rearrange("p (c r) -> p c r", r=16),
                  ldna[:, :].rearrange("(c p) r -> p c r", p=128))
    t_lah = b.step("s", "s", lambda E: E.copy(s_ldnaH[:], s_ldna32[:]), deps=[t_la32])
    t_lal = b.step("v", "v", lambda E: E.tensor_tensor(
        s_ldnaL[:], s_ldna32[:], s_ldnaH[:], ALU.subtract), deps=[t_lah])
    NH2 = HCH_N * 1024
    NX2 = HCH_N * 512
    wsl_tok = None
    x2_free = None
    atl_free = [None, None]
    ev_free5 = [None, None]
    ud_free = [None]
    cc_toks = []
    for dq4 in range(4):
        load("d_wsl", WSL[:, 0:NH2].rearrange("p (c f) -> p c f", f=1024),
             WDNTH.rearrange("(c p) d -> p c d", p=128)[:, :, dq4 * 1024:(dq4 + 1) * 1024],
             deps=[wsl_tok, x2_free])
        wsl_tok = load("d_wsl", WSL[:, NH2:2 * NH2].rearrange("p (c f) -> p c f", f=1024),
                       WDNTL.rearrange("(c p) d -> p c d", p=128)[:, :, dq4 * 1024:(dq4 + 1) * 1024])
        for ti in range(NT):
            load("d_x2t", X2T[:, 0:NX2].rearrange("p (c f) -> p c f", f=512),
                 X2SH.rearrange("(c p) t -> p c t", p=128)[:, :, ti * 512:(ti + 1) * 512],
                 deps=[x2_free])
            x2ld = load("d_x2t", X2T[:, NX2:2 * NX2].rearrange("p (c f) -> p c f", f=512),
                        X2SL.rearrange("(c p) t -> p c t", p=128)[:, :, ti * 512:(ti + 1) * 512])
            ems = []
            for hc in range(HCH_N):
                for aop, xoff in ((s_ldnaH, 0), (s_ldnaH, NX2), (s_ldnaL, 0)):
                    def em(E, st, sp, hc=hc, a=aop, xo=xoff):
                        return E.matmul(PS[0:16, 7 * 512:7 * 512 + 512],
                                        a[:, hc * 16:(hc + 1) * 16],
                                        X2T[:, xo + hc * 512: xo + (hc + 1) * 512],
                                        start=st, stop=sp)
                    ems.append(em)
            udt = mm_group(7, ems, [x2ld, t_lah, t_lal, ud_free[0]])
            ud_done_t = b.step("v", "v", lambda E: E.tensor_copy(
                UDB[0:16, :], PS[0:16, 7 * 512:7 * 512 + 512]), deps=[udt])
            bank_tok[7] = ud_done_t
            last_g = None
            for sub in range(4):
                b0 = 0 if (sub % 2 == 0) else 2
                gtoks = []
                for nb in range(2):
                    bk = b0 + nb
                    ems = []
                    for hc in range(HCH_N):
                        for xoff, woff in ((0, 0), (NX2, 0), (0, NH2)):
                            def em(E, st, sp, hc=hc, nb=nb, sub=sub, bk=bk, xo=xoff, wo2=woff):
                                return E.matmul(psb(bk),
                                                X2T[:, xo + hc * 512 + sub * 128: xo + hc * 512 + sub * 128 + 128],
                                                WSL[:, wo2 + hc * 1024 + nb * 512: wo2 + hc * 1024 + nb * 512 + 512],
                                                start=st, stop=sp)
                            ems.append(em)
                    def emlora(E, st, sp, nb=nb, sub=sub, dq4=dq4, bk=bk):
                        return E.matmul(psb(bk),
                                        UDB[0:16, sub * 128: sub * 128 + 128],
                                        s_ldnb[:, dq4 * 1024 + nb * 512: dq4 * 1024 + nb * 512 + 512],
                                        start=st, stop=sp)
                    ems.append(emlora)
                    def embd(E, st, sp, nb=nb, dq4=dq4, bk=bk):
                        return E.matmul(psb(bk), ones[0:1, 0:128],
                                        s_bdn[0:1, dq4 * 1024 + nb * 512: dq4 * 1024 + nb * 512 + 512],
                                        start=st, stop=sp)
                    ems.append(embd)
                    gt = mm_group(bk, ems, [x2ld, wsl_tok, ud_done_t, t_ldnb, t_bdn])
                    gtoks.append(gt)
                slq = sub % 2
                stg = EV5[:, slq * 1024:(slq + 1) * 1024]
                trow = ti * 512 + sub * 128
                dd = gtoks + [ev_free5[slq]]
                atl = ATL[:, slq * 1024:(slq + 1) * 1024]
                atld = load(f"d_atl{slq}", atl, APS[trow:trow + 128,
                                                    dq4 * 1024:(dq4 + 1) * 1024],
                            deps=[atl_free[slq]])
                e1 = b.step("v", "v", lambda E, st=stg, b0=b0, a=atl: E.tensor_tensor(
                    st, PS[:, b0 * 512:b0 * 512 + 1024], a, ALU.add),
                    deps=dd + [atld])
                atl_free[slq] = e1
                for bk in (b0, b0 + 1):
                    bank_tok[bk] = e1
                ev_free5[slq] = store("d_part", PARTQ[dq4][trow:trow + 128, :],
                                      stg, deps=[e1])
                last_g = gtoks[-1]
            x2_free = last_g
            ud_free[0] = last_g
        b.wait("g", ("d_part", b.cnt["d_part"]))
        cc_toks.append(b.step("g", "cc", lambda E, q=dq4: E.collective_compute(
            "ReduceScatter", ALU.add, replica_groups=RS1_GROUPS,
            ins=[PARTQ[q][:, :]], outs=[R2OUTQ[q][:, :]])))

    b.barrier()
    es_p5.close()
    for i in range(8):
        bank_tok[i] = None

    if _STOP_AFTER == "p5":
        return _finish(nc, es, b, dbg_copies)

    b.barrier()

    if _STOP_AFTER == "l0":
        return _finish(nc, es, b, dbg_copies)

    es_p8 = ExitStack()
    XNH8 = sbt(es_p8, "p8XNH", [128, 16384], BF16)
    XNL8 = sbt(es_p8, "p8XNL", [128, 16384], BF16)
    SLAB8 = sbt(es_p8, "p8SLAB", [128, 16384], BF16)
    XT = sbt(es_p8, "p8XT", [128, 4096])
    XSQ = sbt(es_p8, "p8XSQ", [128, 4096])
    XH8 = sbt(es_p8, "p8XH", [128, 4096], BF16)
    XL8 = sbt(es_p8, "p8XL", [128, 4096], BF16)
    STG8 = sbt(es_p8, "p8STG", [128, 3072])
    uall = sbt(es_l1, "uall", [128, 4096], side="right")
    glog = sbt(es_l1, "glog", [8, 1024], side="right")
    vstg = STG8[:, 0:512]
    qstg8 = [STG8[:, 512:1024], STG8[:, 1024:1536]]
    kstg8 = [STG8[:, 1536:2048], STG8[:, 2048:2560]]
    vstg2 = STG8[:, 2560:3072]
    slab_free = [None, None]
    stg_free = {}
    last_xt_user[0] = None
    last_xh_user[0] = None
    last_xnf_user[0] = None

    def x1_sub(trow):
        ldo = load("d_xt", XT[:], xown[trow:trow + 128, :], deps=[last_xt_user[0]])
        ldr = None
        for q in range(4):
            ldr = load("d_xr", XSQ[:, q * 1024:(q + 1) * 1024],
                       R2OUTQ[q][trow:trow + 128, :],
                       deps=([prev_sq[0]] if q == 0 else ()))
        ad = b.step("v", "v", lambda E: E.tensor_tensor(XT[:], XT[:], XSQ[:], ALU.add),
                    deps=[ldo, ldr])
        st = store("d_x1", X1S[trow:trow + 128, :], XT[:], deps=[ad])
        return ln_subtile(XT, XSQ, st)

    ready8 = [None, None]
    for ti1 in range(2):
        for sub in range(4):
            trow = ti1 * 512 + sub * 128
            lnk = x1_sub(trow)
            hk = b.step("s", "s", lambda E: E.copy(XH8[:], XT[:]),
                        deps=[lnk, last_xh_user[0]])
            lk = b.step("v", "v", lambda E: E.tensor_tensor(XL8[:], XT[:], XH8[:],
                                                            ALU.subtract), deps=[hk])
            last_xt_user[0] = lk
            for comp in range(2):
                XSRC = XH8 if comp == 0 else XL8
                XDST = XNH8 if comp == 0 else XNL8
                ctok = hk if comp == 0 else lk
                for g in range(8):
                    bank = 4 + ((2 * g + comp) % 4)
                    blocks = [(XSRC[:, (4 * g + i) * 128:(4 * g + i + 1) * 128], i * 128)
                              for i in range(4)]
                    tp = transpose_group(bank, blocks, deps=[ctok, last_xnf_user[0]])
                    dst = XDST.rearrange("p (c t) -> p c t", t=512)[:, 4 * g:4 * g + 4,
                                                                   sub * 128:(sub + 1) * 128]
                    eng = "s" if comp == 0 else "v"
                    if comp == 0:
                        evk = b.step("s", "s", lambda E, bk=bank, d=dst: E.copy(
                            d, psb16(bk).rearrange("p (c f) -> p c f", f=128)), deps=[tp])
                    else:
                        evk = b.step("v", "v", lambda E, bk=bank, d=dst: E.tensor_copy(
                            d, psb16(bk).rearrange("p (c f) -> p c f", f=128)), deps=[tp])
                    bank_tok[bank] = evk
                    ready8[comp] = evk
                    last_xh_user[0] = tp
        fcs = [("QKV", j) for j in range(24)] + [("G", 0)] + [("MA", j) for j in range(4)]
        mm_tok = None
        for fci, (kind, idx) in enumerate(fcs):
            bank = fci % 4
            m = 8 if kind == "G" else 128
            if kind in ("QKV", "MA"):
                sl = fci % 2
                slab = SLAB8[:, sl * 8192:(sl + 1) * 8192]
                if kind == "QKV":
                    sh2 = WQKVH[:, 128 * idx:128 * (idx + 1)]
                    sl2 = WQKVL[:, 128 * idx:128 * (idx + 1)]
                else:
                    sh2 = AATH[:, 128 * idx:128 * (idx + 1)]
                    sl2 = AATL[:, 128 * idx:128 * (idx + 1)]
                load(f"d_sl{sl}", slab[:, 0:4096].rearrange("p (c f) -> p c f", f=128),
                     sh2.rearrange("(c p) f -> p c f", p=128), deps=[slab_free[sl]])
                slt = load(f"d_sl{sl}",
                           slab[:, 4096:8192].rearrange("p (c f) -> p c f", f=128),
                           sl2.rearrange("(c p) f -> p c f", p=128))
                def lhs_fn(dc, ph, slab=slab):
                    return slab[:, (0 if ph < 2 else 4096) + dc * 128:
                                (0 if ph < 2 else 4096) + dc * 128 + 128]
                dd = [slt] + ready8
            else:
                def lhs_fn(dc, ph):
                    g2 = s_gwtH if ph < 2 else s_gwtL
                    return g2[:, 8 * dc:8 * dc + 8]
                dd = list(ready8)
                slt = None
            ems = []
            for dc in range(DC):
                for ph, xsrc in ((0, XNH8), (1, XNL8), (2, XNH8)):
                    def em(E, st, sp, dc=dc, ph=ph, f=lhs_fn, bk=bank, m=m, xs=xsrc):
                        return E.matmul(PS[0:m, 512 * bk:512 * bk + 512], f(dc, ph),
                                        xs[:, dc * 512:(dc + 1) * 512], start=st, stop=sp)
                    ems.append(em)
            mm_tok = mm_group(bank, ems, dd)
            if kind in ("QKV", "MA"):
                slab_free[sl] = mm_tok
            if kind == "QKV" and idx < 8:
                qs = qstg8[fci % 2]
                evk = b.step("v", "v", lambda E, bk=bank, st=qs, idx=idx: E.tensor_scalar(
                    st, psb(bk), s_bqkv[:, idx:idx + 1], None, ALU.add),
                    deps=[mm_tok, t_bqkv, stg_free.get(f"q8{fci % 2}")])
                bank_tok[bank] = evk
                stg_free[f"q8{fci % 2}"] = store(
                    "d_qs1", QS1[128 * idx:128 * (idx + 1), ti1 * 512:(ti1 + 1) * 512],
                    qs, deps=[evk])
            elif kind == "QKV" and idx < 16:
                ks = kstg8[fci % 2]
                evk = b.step("v", "v", lambda E, bk=bank, st=ks, idx=idx: E.tensor_scalar(
                    st, psb(bk), s_bqkv[:, idx:idx + 1], None, ALU.add),
                    deps=[mm_tok, t_bqkv, stg_free.get(f"k8{fci % 2}")])
                bank_tok[bank] = evk
                stg_free[f"k8{fci % 2}"] = store(
                    "d_ag", AGIN[128 * (idx - 8):128 * (idx - 7),
                                 ti1 * 512:(ti1 + 1) * 512], ks, deps=[evk])
            elif kind == "QKV":
                evk = b.step("v", "v", lambda E, bk=bank, idx=idx: E.tensor_scalar(
                    vstg[:], psb(bk), s_bqkv[:, idx:idx + 1], None, ALU.add),
                    deps=[mm_tok, t_bqkv, stg_free.get("v8")])
                bank_tok[bank] = evk
                vb = 4 + (fci % 4)
                blocks = [(vstg[:, i * 128:(i + 1) * 128], i * 128) for i in range(4)]
                tp = transpose_group(vb, blocks, deps=[evk])
                stg_free["v8"] = tp
                vev = b.step("v", "v", lambda E, bk=vb, sv=vstg2: E.tensor_copy(
                    sv, psb(bk)), deps=[tp, stg_free.get("v8st")])
                bank_tok[vb] = vev
                dst = AGIN[1024 + ti1 * 512: 1024 + (ti1 + 1) * 512,
                           128 * (idx - 16):128 * (idx - 15)]
                stg_free["v8st"] = store(
                    "d_ag", dst.rearrange("(c p) f -> p c f", p=128),
                    vstg2.rearrange("p (c f) -> p c f", f=128), deps=[vev])
            elif kind == "G":
                evk = b.step("v", "v", lambda E, bk=bank, ti1=ti1: E.tensor_copy(
                    glog[0:8, ti1 * 512:(ti1 + 1) * 512], PS[0:8, 512 * bk:512 * bk + 512]),
                    deps=[mm_tok])
                bank_tok[bank] = evk
            else:
                evk = b.step("v", "v", lambda E, bk=bank, idx=idx, ti1=ti1: E.tensor_copy(
                    uall[:, idx * 1024 + ti1 * 512: idx * 1024 + (ti1 + 1) * 512],
                    psb(bk)), deps=[mm_tok])
                bank_tok[bank] = evk
        last_xnf_user[0] = mm_tok

    b.barrier()
    es_p8.close()
    for i in range(8):
        bank_tok[i] = None

    cc3 = b.step("g", "cc", lambda E: E.collective_compute(
        "AllGather", ALU.bypass, replica_groups=AG_GROUPS,
        ins=[AGIN[:, :]], outs=[AGOUT[:, :]]))
    b.barrier()

    if _STOP_AFTER == "p9":
        return _finish(nc, es, b, dbg_copies)

    es_p10 = ExitStack()
    SC1B = sbt(es_p10, "pASC", [128, 8192])
    AT1 = sbt(es_p10, "pAAT", [128, 8192])
    QT1 = sbt(es_p10, "pAQT", [128, 2048])
    KB1 = sbt(es_p10, "pAKB", [128, 4096])
    VB1 = sbt(es_p10, "pAVB", [128, 4096])
    af = sbt(es_l1, "af", [128, 8192], side="right")
    qfree = None
    kfree = None
    vfree = None
    scfree = None
    atfree = None
    for qb1 in range(4):
        qld = load("d_q1", QT1[:, :].rearrange("p (c f) -> p c f", f=256),
                   QS1.rearrange("(c p) t -> p c t", p=128)[:, :, qb1 * 256:(qb1 + 1) * 256],
                   deps=[qfree])
        exp_toks = []
        ev_last = None
        for kb in range(8):
            r, hf = kb // 2, kb % 2
            kld = load("d_k1", KB1[:, :].rearrange("p (c f) -> p c f", f=512),
                       AGOUT[r * 2048: r * 2048 + 1024, :].rearrange(
                           "(c p) t -> p c t", p=128)[:, :, hf * 512:(hf + 1) * 512],
                       deps=[kfree])
            for sub in range(2):
                bank = sub
                ems = []
                for dkc in range(8):
                    def em(E, st, sp, dkc=dkc, sub=sub, bk=bank):
                        return E.matmul(psb(bk),
                                        QT1[:, dkc * 256 + sub * 128: dkc * 256 + sub * 128 + 128],
                                        KB1[:, dkc * 512:(dkc + 1) * 512], start=st, stop=sp)
                    ems.append(em)
                mmt = mm_group(bank, ems, [qld, kld, scfree])
                ev_last = b.step("v", "v", lambda E, bk=bank,
                                 d=SC1B[:, sub * 4096 + kb * 512: sub * 4096 + (kb + 1) * 512]:
                                 E.tensor_copy(d, psb(bk)), deps=[mmt])
                bank_tok[bank] = ev_last
            kfree = mmt
        for sub in range(2):
            scs = SC1B[:, sub * 4096: sub * 4096 + 4096]
            mx = b.step("v", "v", lambda E, s=scs: E.reduce_max(
                sstat[:, 8:9], s, AXX), deps=[ev_last])
            nb2 = b.step("v", "v", lambda E: E.tensor_scalar(
                sstat[:, 9:10], sstat[:, 8:9], -SC1, None, ALU.mult), deps=[mx])
            ex = b.step("s", "s", lambda E, s=scs: E.activation(
                s, s, AF.Exp, bias=sstat[:, 9:10], scale=SC1,
                accum_out=sstat[:, 10:11]), deps=[nb2])
            rc = b.step("v", "v", lambda E: E.reciprocal(
                sstat[:, 11:12], sstat[:, 10:11]), deps=[ex])
            nm = b.step("v", "v", lambda E, s=scs: E.tensor_scalar(
                s, s, sstat[:, 11:12], None, ALU.mult), deps=[rc])
            exp_toks.append(nm)
        at_ready = None
        last_tp = None
        for kc in range(32):
            bank = 2 + (kc % 2)
            blocks = [(SC1B[:, sub * 4096 + kc * 128: sub * 4096 + (kc + 1) * 128],
                       sub * 128) for sub in range(2)]
            tp = transpose_group(bank, blocks, deps=exp_toks + [atfree])
            evk = b.step("v", "v", lambda E, bk=bank,
                         d=AT1[:, kc * 256:(kc + 1) * 256]:
                         E.tensor_copy(d, PS[:, 512 * bk:512 * bk + 256]), deps=[tp])
            bank_tok[bank] = evk
            at_ready = evk
            last_tp = tp
        scfree = last_tp
        first = [True] * 8
        avtoks = [None] * 8
        for kb in range(8):
            r, hf = kb // 2, kb % 2
            vld = load("d_v1", VB1[:, :].rearrange("p (c f) -> p c f", f=1024),
                       AGOUT[r * 2048 + 1024 + hf * 512: r * 2048 + 1024 + (hf + 1) * 512, :]
                       .rearrange("(c p) f -> p c f", p=128),
                       deps=[vfree])
            for dvc in range(8):
                for kcl in range(4):
                    st = first[dvc]
                    first[dvc] = False
                    sp = (kb == 7 and kcl == 3)
                    last_of_kb = (dvc == 7 and kcl == 3)
                    dd = [vld, at_ready]
                    if st and bank_tok[dvc] is not None:
                        dd.append(bank_tok[dvc])
                        bank_tok[dvc] = None
                    def em(E, dvc=dvc, kcl=kcl, kb=kb, st=st, sp=sp):
                        return E.matmul(PS[:, dvc * 512:dvc * 512 + 256],
                                        VB1[:, kcl * 1024 + dvc * 128: kcl * 1024 + dvc * 128 + 128],
                                        AT1[:, (4 * kb + kcl) * 256:(4 * kb + kcl + 1) * 256],
                                        start=st, stop=sp)
                    if sp or last_of_kb:
                        tokkb = b.step("t", "t", em, deps=dd)
                        if sp:
                            avtoks[dvc] = tokkb
                        vfree = tokkb
                    else:
                        b.run("t", em, deps=dd)
        for dvc in range(8):
            if avtoks[dvc] is None:
                avtoks[dvc] = vfree
        atfree = vfree
        qfree = vfree
        for dvc in range(8):
            evk = b.step("v", "v", lambda E, dvc=dvc, qb1=qb1: E.tensor_copy(
                af[:, dvc * 1024 + qb1 * 256: dvc * 1024 + (qb1 + 1) * 256],
                PS[:, dvc * 512:dvc * 512 + 256]), deps=[avtoks[dvc]])
            bank_tok[dvc] = evk

    b.barrier()
    es_p10.close()
    for i in range(8):
        bank_tok[i] = None

    if _STOP_AFTER == "p10":
        return _finish(nc, es, b, dbg_copies)

    es_p11 = ExitStack()
    rmat = sbt(es_p11, "rmat", [8, 512])
    t_rm = load("d_c", rmat[:], rmat_in[:, :])
    wtm = sbt(es_p11, "wtm", [128, 64])
    wtm2 = sbt(es_p11, "wtm2", [128, 64])
    PB = sbt(es_p11, "p11PB", [128, 40])
    wfm = sbt(es_p11, "wfm", [8, 1024])
    wexp = sbt(es_l1, "wexp", [128, 4096], side="right")
    evk = None
    for g2 in range(2):
        bank = g2 % 2
        blocks = [(glog[0:8, (4 * g2 + i) * 128:(4 * g2 + i + 1) * 128], i * 8)
                  for i in range(4)]
        tp = transpose_group(bank, blocks, deps=[])
        evk = b.step("v", "v", lambda E, bk=bank, g2=g2: E.tensor_copy(
            wtm[:, 32 * g2:32 * g2 + 32], PS[:, 512 * bk:512 * bk + 32]), deps=[tp])
        bank_tok[bank] = evk
    wlast = None
    for tc in range(8):
        lg = wtm[:, 8 * tc:8 * tc + 8]
        t1 = b.step("v", "v", lambda E, lg=lg: E.reduce_max(sstat[:, 8:9], lg, AXX),
                    deps=[evk])
        t2 = b.step("v", "v", lambda E, lg=lg: E.tensor_scalar(
            PB[:, 0:8], lg, sstat[:, 8:9], None, ALU.is_equal), deps=[t1])
        t3 = b.step("v", "v", lambda E, lg=lg: E.scalar_tensor_tensor(
            PB[:, 8:16], PB[:, 0:8], -1e30, lg, ALU.mult, ALU.add), deps=[t2])
        t4 = b.step("v", "v", lambda E: E.reduce_max(sstat[:, 9:10], PB[:, 8:16], AXX),
                    deps=[t3])
        t5 = b.step("v", "v", lambda E: E.tensor_scalar(
            sstat[:, 10:11], sstat[:, 8:9], -1.0, None, ALU.mult), deps=[t4])
        t6 = b.step("s", "s", lambda E, lg=lg: E.activation(
            PB[:, 16:24], lg, AF.Exp, bias=sstat[:, 10:11], scale=1.0), deps=[t5])
        t7 = b.step("s", "s", lambda E: E.activation(
            sstat[:, 11:12], sstat[:, 9:10], AF.Exp, bias=sstat[:, 10:11], scale=1.0),
            deps=[t6])
        t8 = b.step("v", "v", lambda E: E.tensor_scalar(
            sstat[:, 12:13], sstat[:, 11:12], 1.0, None, ALU.add), deps=[t7])
        t9 = b.step("v", "v", lambda E: E.reciprocal(sstat[:, 13:14], sstat[:, 12:13]),
                    deps=[t8])
        ta = b.step("v", "v", lambda E, lg=lg: E.tensor_scalar(
            PB[:, 24:32], lg, sstat[:, 9:10], None, ALU.is_ge), deps=[t9])
        tb = b.step("v", "v", lambda E: E.tensor_tensor(
            PB[:, 32:40], PB[:, 16:24], PB[:, 24:32], ALU.mult), deps=[ta, t6])
        wlast = b.step("v", "v", lambda E, tc=tc: E.tensor_scalar(
            wtm2[:, 8 * tc:8 * tc + 8], PB[:, 32:40], sstat[:, 13:14], None, ALU.mult),
            deps=[tb])
    for g2 in range(2):
        bank = 2 + g2
        blocks = [(wtm2[:, (4 * g2 + i) * 8:(4 * g2 + i + 1) * 8], i * 128)
                  for i in range(4)]
        tp = transpose_group(bank, blocks, deps=[wlast])
        evk = b.step("v", "v", lambda E, bk=bank, g2=g2: E.tensor_copy(
            wfm[0:8, 512 * g2:512 * g2 + 512], PS[0:8, 512 * bk:512 * bk + 512]),
            deps=[tp])
        bank_tok[bank] = evk
    wu_toks = []
    for erc in range(4):
        for ti1 in range(2):
            bank = 4 + ((2 * erc + ti1) % 4)
            def em(E, st, sp, erc=erc, ti1=ti1, bk=bank):
                return E.matmul(psb(bk), rmat[0:8, erc * 128:(erc + 1) * 128],
                                wfm[0:8, ti1 * 512:(ti1 + 1) * 512], start=st, stop=sp)
            mmt = mm_group(bank, [em], [evk])
            wek = b.step("v", "v", lambda E, bk=bank, erc=erc, ti1=ti1: E.tensor_copy(
                wexp[:, erc * 1024 + ti1 * 512: erc * 1024 + (ti1 + 1) * 512], psb(bk)),
                deps=[mmt])
            bank_tok[bank] = wek
            wu_toks.append(b.step("v", "v", lambda E, erc=erc, ti1=ti1: E.tensor_tensor(
                uall[:, erc * 1024 + ti1 * 512: erc * 1024 + (ti1 + 1) * 512],
                uall[:, erc * 1024 + ti1 * 512: erc * 1024 + (ti1 + 1) * 512],
                wexp[:, erc * 1024 + ti1 * 512: erc * 1024 + (ti1 + 1) * 512],
                ALU.mult), deps=[wek]))

    b.barrier()
    es_p11.close()
    for i in range(8):
        bank_tok[i] = None

    if _STOP_AFTER == "p11":
        return _finish(nc, es, b, dbg_copies)

    es_p12 = ExitStack()
    W1S = sbt(es_p12, "p12W1", [128, 8192])
    BTSL = sbt(es_p12, "p12BT", [128, 4096])
    X1T = sbt(es_p12, "p12X1", [128, 1024])
    EVF = sbt(es_p12, "p12EV", [128, 2048])
    s_bo = sbt(es_p12, "p12bo", [1, D])
    t_bo = load("d_c", s_bo[:], bo8[:, :])
    w1_tok = None
    bt_tok = None
    x1_free = [None]
    evf_free = [None, None]
    out_toks = []
    p12_last = [None]
    for dq in range(4):
        w1_tok = load("d_w1s", W1S[:, :].rearrange("p (c f) -> p c f", f=1024),
                      wo1.rearrange("(c p) d -> p c d", p=128)[:, :, dq * 1024:(dq + 1) * 1024],
                      deps=[w1_tok, p12_last[0]])
        bt_tok = load("d_bts", BTSL[:, :].rearrange("p (c f) -> p c f", f=1024),
                      BTS.rearrange("(c p) d -> p c d", p=128)[:, :, dq * 1024:(dq + 1) * 1024],
                      deps=[bt_tok, p12_last[0]])
        for tc in range(8):
            x1t = X1T[:, 0:1024]
            x1ld = load("d_x1t", x1t, X1S[tc * 128:(tc + 1) * 128,
                                          dq * 1024:(dq + 1) * 1024],
                        deps=[x1_free[0]])
            b0 = 2 * (tc % 4)
            gtoks = []
            for nb in range(2):
                bk = b0 + nb
                ems = []
                if _P12_PARTS & 1:
                    for dvc in range(8):
                        def em(E, st, sp, dvc=dvc, nb=nb, tc=tc, bk=bk):
                            return E.matmul(psb(bk),
                                            af[:, dvc * 1024 + tc * 128: dvc * 1024 + tc * 128 + 128],
                                            W1S[:, dvc * 1024 + nb * 512: dvc * 1024 + nb * 512 + 512],
                                            start=st, stop=sp)
                        ems.append(em)
                if _P12_PARTS & 2:
                    for erc in range(4):
                        def em2(E, st, sp, erc=erc, nb=nb, tc=tc, bk=bk):
                            return E.matmul(psb(bk),
                                            uall[:, erc * 1024 + tc * 128: erc * 1024 + tc * 128 + 128],
                                            BTSL[:, erc * 1024 + nb * 512: erc * 1024 + nb * 512 + 512],
                                            start=st, stop=sp)
                        ems.append(em2)
                if _P12_PARTS & 4:
                    def em3(E, st, sp, nb=nb, dq=dq, bk=bk):
                        return E.matmul(psb(bk), ones[0:1, 0:128],
                                        s_bo[0:1, dq * 1024 + nb * 512: dq * 1024 + nb * 512 + 512],
                                        start=st, stop=sp)
                    ems.append(em3)
                gt = mm_group(bk, ems, [w1_tok, bt_tok, t_bo] + wu_toks)
                gtoks.append(gt)
            sl = tc % 2
            stg = EVF[:, sl * 1024:(sl + 1) * 1024]
            evk = b.step("v", "v", lambda E, st=stg, b0=b0, xx=x1t: E.tensor_tensor(
                st, PS[:, b0 * 512: b0 * 512 + 1024], xx, ALU.add),
                deps=gtoks + [x1ld, evf_free[sl]])
            for nb in range(2):
                bank_tok[b0 + nb] = evk
            x1_free[0] = evk
            out_toks.append(store("d_out", yout[tc * 128:(tc + 1) * 128,
                                                dq * 1024:(dq + 1) * 1024], stg,
                                  deps=[evk]))
            evf_free[sl] = out_toks[-1]
            p12_last[0] = gtoks[-1]
    for t in out_toks:
        b.wait("y", t)
        b.wait("g", t)
    b.barrier()
    es_p12.close()
    es_l1.close()
    return _finish(nc, es, b, dbg_copies)


def _finish(nc, es, b, dbg_copies=()):
    b.barrier()
    for (src_ap, dst_ap, shape) in dbg_copies:
        nrow = shape[0]
        step = max(1, min(nrow, (4 << 20) // max(1, int(np.prod(shape[1:])) * 4)))
        for r0 in range(0, nrow, step):
            r1 = min(nrow, r0 + step)
            b.step("g", "d_dbg", lambda E, s=src_ap[r0:r1], d=dst_ap[r0:r1]:
                   E.dma_start(out=d, in_=s), dma=True)
    b.barrier()
    with nc.Block() as block:
        @block.tensor
        def _(E):
            for f in b.ops["t"]:
                f(E)

        @block.vector
        def _(E):
            for f in b.ops["v"]:
                f(E)

        @block.scalar
        def _(E):
            for f in b.ops["s"]:
                f(E)

        @block.gpsimd
        def _(E):
            for f in b.ops["g"]:
                f(E)

        @block.sync
        def _(E):
            for f in b.ops["y"]:
                f(E)
    es.close()
    return nc


def kernel(**inputs):
    x = np.ascontiguousarray(np.asarray(inputs["L0_attn_x"], dtype=np.float32)).reshape(TT, D)
    xs = [x[0:S], x[S:TT]]
    xall_per_seq = [np.ascontiguousarray(np.concatenate([xs[b_], xs[1 - b_]], axis=0))
                    for b_ in range(2)]
    wq_full = np.asarray(inputs["L0_attn_Wq"], dtype=np.float32)
    wk_full = np.asarray(inputs["L0_attn_Wk"], dtype=np.float32)
    wv_full = np.asarray(inputs["L0_attn_Wv"], dtype=np.float32)
    wo_full = np.asarray(inputs["L0_attn_Wo"], dtype=np.float32)
    cu = np.asarray(inputs["L0_ffn_w_up_codes"], dtype=np.int32).reshape(11008, D)
    au = np.asarray(inputs["L0_ffn_w_up_absmax"], dtype=np.float32).reshape(11008, 64)
    bu = np.asarray(inputs["L0_ffn_b_up"], dtype=np.float32)
    lau = np.asarray(inputs["L0_ffn_w_up_lora_a"], dtype=np.float32)
    lbu = np.asarray(inputs["L0_ffn_w_up_lora_b"], dtype=np.float32)
    cd = np.asarray(inputs["L0_ffn_w_down_codes"], dtype=np.int32).reshape(D, 11008)
    ad = np.asarray(inputs["L0_ffn_w_down_absmax"], dtype=np.float32).reshape(D, 172)
    bd = np.asarray(inputs["L0_ffn_b_down"], dtype=np.float32)
    lda_ = np.asarray(inputs["L0_ffn_w_down_lora_a"], dtype=np.float32)
    ldb_ = np.asarray(inputs["L0_ffn_w_down_lora_b"], dtype=np.float32)
    wqkv_ = np.asarray(inputs["L1_attn_Wqkv"], dtype=np.float32)
    bqkv_ = np.asarray(inputs["L1_attn_bqkv"], dtype=np.float32).reshape(24, 128)
    wo1_ = np.asarray(inputs["L1_attn_Wo"], dtype=np.float32)
    bo_ = np.asarray(inputs["L1_attn_bo"], dtype=np.float32).reshape(1, 4096)
    gw_ = np.asarray(inputs["L1_ffn_gate_w"], dtype=np.float32)
    la1_ = np.asarray(inputs["L1_ffn_lora_A"], dtype=np.float32).reshape(512, D)
    lb1_ = np.asarray(inputs["L1_ffn_lora_B"], dtype=np.float32).reshape(8 * D, 64)

    ident = np.eye(128, dtype=np.float32)
    ones_ = np.ones((1, 128), dtype=np.float32)
    rmat_ = np.zeros((8, 512), dtype=np.float32)
    for e in range(8):
        rmat_[e, 64 * e:64 * (e + 1)] = 1.0

    chunks = []
    for c in range(8):
        g = c % 4
        if g < 3:
            chunks.append(list(range(22 * g, 22 * g + 22)))
        else:
            chunks.append(list(range(66, 86)) + [None, None])

    def rows_blk(arr, hlist, blk=128):
        parts = []
        for h in hlist:
            if h is None:
                parts.append(np.zeros((blk,) + arr.shape[1:], dtype=arr.dtype))
            else:
                parts.append(arr[blk * h: blk * (h + 1)])
        return np.ascontiguousarray(np.concatenate(parts, axis=0))

    def cols_blk(arr, hlist, blk):
        parts = []
        for h in hlist:
            if h is None:
                parts.append(np.zeros(arr.shape[:1] + (blk,), dtype=arr.dtype))
            else:
                parts.append(arr[:, blk * h: blk * (h + 1)])
        return np.ascontiguousarray(np.concatenate(parts, axis=1))

    in_maps = []
    for c in range(8):
        kvh = c % 4
        hl = chunks[c]
        bup_rows = []
        for h in hl:
            bup_rows.append(np.zeros(128, dtype=np.float32) if h is None
                            else bu[128 * h:128 * (h + 1)])
        bup_c = np.ascontiguousarray(np.stack(bup_rows, axis=0))
        m = {
            "xall": np.ascontiguousarray(xs[MYSEQ[c]]),
            "xown": np.ascontiguousarray(x[OWN_START[c]:OWN_START[c] + 1024]),
            "wq": np.ascontiguousarray(wq_full[:, 1024 * kvh:1024 * (kvh + 1)]),
            "wk": np.ascontiguousarray(wk_full[:, 128 * kvh:128 * (kvh + 1)]),
            "wv": np.ascontiguousarray(wv_full[:, 128 * kvh:128 * (kvh + 1)]),
            "wo": np.ascontiguousarray(wo_full[1024 * kvh:1024 * (kvh + 1), :]),
            "cup": rows_blk(cu, hl),
            "aup": rows_blk(au, hl),
            "bup": bup_c,
            "laup": lau,
            "lbup": cols_blk(lbu, hl, 128),
            "cdn": cols_blk(cd, hl, 128),
            "adn": cols_blk(ad, hl, 2),
            "ldna": rows_blk(lda_, hl),
            "ldnb": ldb_,
            "bdn8": (bd.reshape(1, 4096) if c % 4 == 0
                     else np.zeros((1, 4096), dtype=np.float32)),
            "wqkv": wqkv_,
            "bqkv": bqkv_,
            "wo1": wo1_,
            "bo8": bo_,
            "gw": gw_,
            "la1": la1_,
            "lb1": lb1_,
            "identI": ident,
            "onesI": ones_,
            "rmatI": rmat_,
        }
        in_maps.append(m)

    nc = build_program()
    res = run_bass_kernel_spmd(nc, in_maps, core_ids=list(range(8)))
    global _LAST_RESULTS
    _LAST_RESULTS = res.results
    out = np.empty((TT, D), dtype=np.float32)
    for c in range(8):
        out[OWN_START[c]:OWN_START[c] + 1024] = np.asarray(res.results[c]["y"],
                                                           dtype=np.float32)
    return out.reshape(2, S, D)


# version 13 (final)
# speedup vs baseline: 1.0427x; 1.0256x over previous
import numpy as np
from contextlib import ExitStack

import concourse.bass as bass
import concourse.mybir as mybir
from concourse.bass_utils import run_bass_kernel_spmd

F32 = mybir.dt.float32
BF16 = mybir.dt.bfloat16
I32 = mybir.dt.int32
AF = mybir.ActivationFunctionType
ALU = mybir.AluOpType
AXX = mybir.AxisListType.X

D = 4096
DC = 32
S = 4096
TT = 8192
NT = 8
HCH_N = 22
HSN = HCH_N * 128
SC0 = float(1.0 / np.sqrt(np.float32(128.0)))
SC1 = 1.0 / 32.0
EPS = 1e-5
NEG = -1e9

NF4 = [-1.0, -0.6961928009986877, -0.5250730514526367, -0.39491748809814453,
       -0.28444138169288635, -0.18477343022823334, -0.09105003625154495, 0.0,
       0.07958029955625534, 0.16093020141124725, 0.24611230194568634,
       0.33791524171829224, 0.44070982933044434, 0.5626170039176941,
       0.7229568362236023, 1.0]

OWN_START = [0, 1024, 2048, 3072, 4096, 5120, 6144, 7168]
MYSEQ = [0, 0, 0, 0, 1, 1, 1, 1]
RS1_GROUPS = [[0, 1, 2, 3], [4, 5, 6, 7]]
AG_GROUPS = [[0, 1, 2, 3], [4, 5, 6, 7]]

ENGS = ("t", "v", "s", "g", "y")

_STOP_AFTER = None
_DEBUG_OUT = set()
_P12_PARTS = 7
_LAST_RESULTS = None


class Bld:
    def __init__(self, nc, es):
        self.nc = nc
        self.es = es
        self.ops = {e: [] for e in ENGS}
        self.semh = {}
        self.cnt = {}
        self.last_wait = {e: {} for e in ENGS}

    def sem(self, name):
        if name not in self.semh:
            self.semh[name] = self.es.enter_context(self.nc.semaphore(name))
            self.cnt[name] = 0
        return self.semh[name]

    def wait(self, eng, tok):
        if tok is None:
            return
        sname, val = tok
        if val <= 0 or self.last_wait[eng].get(sname, 0) >= val:
            return
        self.last_wait[eng][sname] = val
        h = self.sem(sname)
        self.ops[eng].append(lambda E, h=h, v=val: E.wait_ge(h, v))

    def step(self, eng, sname, build, deps=(), dma=False):
        for d in deps:
            self.wait(eng, d)
        self.sem(sname)
        amt = 16 if dma else 1
        self.cnt[sname] += amt
        v = self.cnt[sname]
        h = self.semh[sname]
        self.ops[eng].append(lambda E, h=h, a=amt, bf=build: bf(E).then_inc(h, a))
        return (sname, v)

    def run(self, eng, build, deps=()):
        for d in deps:
            self.wait(eng, d)
        self.ops[eng].append(build)

    def barrier(self):
        toks = [(s, c) for s, c in self.cnt.items() if c > 0]
        for e in ENGS:
            for t in toks:
                self.wait(e, t)


def build_program():
    nc = bass.Bass("TRN2", target_bir_lowering=False, debug=False)
    es = ExitStack()
    b = Bld(nc, es)

    def din(name, shape, dt=F32):
        return nc.dram_tensor(name, list(shape), dt, kind="ExternalInput").ap()

    xall = din("xall", [S, D])
    xown = din("xown", [1024, D])
    wq = din("wq", [D, 1024]); wk = din("wk", [D, 128]); wv = din("wv", [D, 128])
    wo = din("wo", [1024, D])
    cup = din("cup", [HSN, D], I32); aup = din("aup", [HSN, 64])
    bup = din("bup", [HCH_N, 128])
    laup = din("laup", [D, 16]); lbup = din("lbup", [16, HSN])
    cdn = din("cdn", [D, HSN], I32); adn = din("adn", [D, HSN // 64])
    ldna = din("ldna", [HSN, 16]); ldnb = din("ldnb", [16, D])
    bdn8 = din("bdn8", [1, 4096])
    wqkv = din("wqkv", [D, 3072]); bqkv = din("bqkv", [24, 128])
    wo1 = din("wo1", [1024, D]); bo8 = din("bo8", [1, 4096])
    gw = din("gw", [8, D])
    la1 = din("la1", [512, D])
    lb1 = din("lb1", [8 * D, 64])
    ident_in = din("identI", [128, 128])
    ones_in = din("onesI", [1, 128])
    rmat_in = din("rmatI", [8, 512])

    yout = nc.dram_tensor("y", [1024, D], F32, kind="ExternalOutput").ap()

    dbg_copies = []

    def dint(name, shape, dt=F32):
        ap = nc.dram_tensor(name, list(shape), dt, kind="Internal").ap()
        if name in _DEBUG_OUT:
            dap = nc.dram_tensor(name + "_dbg", list(shape), dt,
                                 kind="ExternalOutput").ap()
            dbg_copies.append((ap, dap, list(shape)))
        return ap

    WUPT = dint("WUPT", [D, HSN])
    WDNT = dint("WDNT", [HSN, D])
    BF = BF16
    WUPTH = dint("WUPTH", [D, HSN], BF); WUPTL = dint("WUPTL", [D, HSN], BF)
    WDNTH = dint("WDNTH", [HSN, D], BF); WDNTL = dint("WDNTL", [HSN, D], BF)
    WQH = dint("WQH", [D, 1024], BF); WQL = dint("WQL", [D, 1024], BF)
    WKH = dint("WKH", [D, 128], BF); WKL = dint("WKL", [D, 128], BF)
    WVH = dint("WVH", [D, 128], BF); WVL = dint("WVL", [D, 128], BF)
    WOH = dint("WOH", [1024, D], BF); WOL = dint("WOL", [1024, D], BF)
    WQKVH = dint("WQKVH", [D, 3072], BF); WQKVL = dint("WQKVL", [D, 3072], BF)
    LAUPH = dint("LAUPH", [D, 16], BF); LAUPL = dint("LAUPL", [D, 16], BF)
    AATH = dint("AATH", [D, 512], BF); AATL = dint("AATL", [D, 512], BF)
    QSH = dint("QSH", [1024, S], BF); QSL = dint("QSL", [1024, S], BF)
    AOSH = dint("AOSH", [1024, S], BF); AOSL = dint("AOSL", [1024, S], BF)
    X2SH = dint("X2SH", [HSN, S], BF); X2SL = dint("X2SL", [HSN, S], BF)
    QS = dint("QS", [1024, S])
    AOS = dint("AOS", [1024, S])
    APS = dint("APS", [S, D])
    X2S = dint("X2S", [HSN, S])
    PARTQ = [dint(f"PARTQ{i}", [S, 1024]) for i in range(4)]
    R2OUTQ = [dint(f"R2OUTQ{i}", [1024, 1024]) for i in range(4)]
    X1S = dint("X1S", [1024, D])
    QS1 = dint("QS1", [1024, 1024])
    AGIN = dint("AGIN", [2048, 1024])
    AGOUT = dint("AGOUT", [8192, 1024])
    AAT = dint("AAT", [D, 512])
    BTS = dint("BTS", [512, D])

    def sbt(stk, name, shape, dt=F32, side=None):
        return stk.enter_context(nc.sbuf_tensor(name, list(shape), dt, side=side))

    ident = sbt(es, "ident", [128, 128])
    identB = sbt(es, "identB", [128, 128], BF16)
    trimask = sbt(es, "trimask", [128, 128])
    ones = sbt(es, "ones", [1, 128])
    s_bup = sbt(es, "s_bup", [128, HCH_N])
    s_lbup = sbt(es, "s_lbup", [16, HSN])

    s_bqkv = sbt(es, "s_bqkv", [128, 24])

    s_gwtH = sbt(es, "s_gwtH", [128, 256], BF16)
    s_gwtL = sbt(es, "s_gwtL", [128, 256], BF16)
    keep01 = sbt(es, "keep01", [128, 128])
    sstat = sbt(es, "sstat", [128, 16])
    PS = es.enter_context(nc.psum_tensor("PS", [128, 4096], F32))

    bank_tok = [None] * 8

    def psb(k, w=512):
        return PS[:, 512 * k: 512 * k + w]

    def load(slot, dst, src, deps=(), slow=False):
        return b.step("y", slot, lambda E, d=dst, s=src, sl=slow: E.dma_start(
            out=d, in_=s, allow_slow_non_contiguous=sl), deps=deps, dma=True)

    def store(slot, dst, src, deps=()):
        return b.step("g", slot, lambda E, d=dst, s=src: E.dma_start(out=d, in_=s),
                      deps=deps, dma=True)

    def mm_group(bank, emitters, deps):
        dd = [d for d in deps if d is not None]
        if bank_tok[bank] is not None:
            dd.append(bank_tok[bank])
            bank_tok[bank] = None
        nmm = len(emitters)
        tok = None
        for i, em in enumerate(emitters):
            st, sp = (i == 0), (i == nmm - 1)
            if sp:
                tok = b.step("t", "t", lambda E, em=em, st=st: em(E, st, True),
                             deps=(dd if st else ()))
            else:
                b.run("t", lambda E, em=em, st=st: em(E, st, False),
                      deps=(dd if st else ()))
        return tok

    def transpose_group(bank, blocks, deps):
        ems = []
        for (srcap, ooff) in blocks:
            isbf = (srcap.dtype == BF16)
            def em(E, st, sp, s=srcap, o=ooff, bk=bank, isbf=isbf):
                pn = s.shape[0]
                if isbf:
                    base = PS[:, 512 * bk:512 * bk + 256].bitcast(BF16)
                    return E.matmul(base[0:s.shape[1], o:o + pn], s,
                                    identB[0:pn, 0:pn],
                                    is_transpose=True, start=st, stop=sp)
                return E.matmul(PS[0:s.shape[1], 512 * bk + o: 512 * bk + o + pn],
                                s, ident[0:pn, 0:pn],
                                is_transpose=True, start=st, stop=sp)
            ems.append(em)
        return mm_group(bank, ems, deps)

    t_id = load("d_c", ident[:], ident_in[:, :])
    load("d_c", ones[:], ones_in[:, :])
    t_bup = load("d_c", s_bup[:].rearrange("p (c o) -> p c o", o=1),
                 bup[:, :].rearrange("c (p o) -> p c o", o=1), slow=True)
    t_lbup = load("d_c", s_lbup[:], lbup[:, :])

    t_bqkv = load("d_c", s_bqkv[:].rearrange("p (c o) -> p c o", o=1),
                  bqkv[:, :].rearrange("c (p o) -> p c o", o=1), slow=True)

    t_idb = b.step("v", "v", lambda E: E.tensor_copy(identB[:], ident[:]), deps=[t_id])
    t_io = b.step("g", "g", lambda E: E.iota(keep01[:], pattern=[[-1, 128]], base=0,
                                             channel_multiplier=1,
                                             allow_small_or_imprecise_dtypes=True))
    t_tm0 = b.step("v", "v", lambda E: E.tensor_scalar(trimask[:], keep01[:], 0.0, NEG,
                                                       ALU.is_ge, ALU.mult), deps=[t_io])
    b.step("v", "v", lambda E: E.tensor_scalar(keep01[:], keep01[:], 0.0, 1.0,
                                               ALU.is_lt, ALU.mult), deps=[t_tm0])
    b.barrier()

    es_pw = ExitStack()
    PW32 = sbt(es_pw, "pw32", [128, 2 * 4096])
    PWH = sbt(es_pw, "pwh", [128, 4096], BF16)
    PWL = sbt(es_pw, "pwl", [128, 4096], BF16)
    pw_free = [None, None]
    pwh_free = None
    pwl_free = None

    def presplit(src_ap, hi_ap, lo_ap, nrows, ncols):
        nonlocal_state = None
        for r0 in range(0, nrows, 128):
            sl = (r0 // 128) % 2
            stg = PW32[:, sl * 4096: sl * 4096 + ncols]
            ld = load(f"d_pw{sl}", stg, src_ap[r0:r0 + 128, :], deps=[pw_free[sl]])
            hi = b.step("s", "s", lambda E, st=stg, n=ncols: E.copy(PWH[:, 0:n], st),
                        deps=[ld, b._pwh_free])
            lo = b.step("g", "g", lambda E, st=stg, n=ncols: E.tensor_tensor(
                PWL[:, 0:n], st, PWH[:, 0:n], ALU.subtract), deps=[hi, b._pwl_free])
            pw_free[sl] = lo
            b._pwh_free = store("d_pwo", hi_ap[r0:r0 + 128, :], PWH[:, 0:ncols], deps=[hi])
            b._pwl_free = store("d_pwo", lo_ap[r0:r0 + 128, :], PWL[:, 0:ncols], deps=[lo])

    b._pwh_free = None
    b._pwl_free = None
    presplit(wq, WQH, WQL, D, 1024)
    presplit(wk, WKH, WKL, D, 128)
    presplit(wv, WVH, WVL, D, 128)
    presplit(wo, WOH, WOL, 1024, D)
    presplit(wqkv, WQKVH, WQKVL, D, 3072)
    presplit(laup, LAUPH, LAUPL, D, 16)


    es_pd = ExitStack()
    A1 = sbt(es_pd, "pdA1", [128, 16384])
    EV = sbt(es_pd, "pdEV", [128, 4096])
    EVH = sbt(es_pd, "pdEVH", [128, 1024], BF16)
    EVL = sbt(es_pd, "pdEVL", [128, 1024], BF16)
    AMB = sbt(es_pd, "pdAMB", [128, 64])
    AMBG = sbt(es_pd, "pdAMBG", [128, 64])
    DQAG = sbt(es_pd, "pdDQAG", [128, 4096])
    DQTG = sbt(es_pd, "pdDQTG", [128, 4096])
    cbuf = [A1[:, 0:4096].bitcast(I32), A1[:, 4096:8192].bitcast(I32)]
    dqa = A1[:, 8192:12288]
    dqt = A1[:, 12288:16384]
    cslot_free = [None, None]
    am_free = None
    dqa_free = None
    ev_free = [None, None]

    def dequant_on(eng, acc, tmp, amb2, cd, ncols, nblk, dep_ld, dep_am, dep_free):
        tok = b.step(eng, eng, lambda E, c=cd, n=ncols, a=acc: E.tensor_scalar(
            a[:, 0:n], c, 0.0, NF4[0], ALU.is_equal, ALU.mult),
            deps=[dep_ld, dep_free])
        for k in range(1, 16):
            tk = b.step(eng, eng, lambda E, c=cd, n=ncols, t=tmp, kk=float(k), vv=NF4[k]:
                        E.tensor_scalar(t[:, 0:n], c, kk, vv, ALU.is_equal, ALU.mult),
                        deps=[tok])
            tok = b.step(eng, eng, lambda E, n=ncols, a=acc, t=tmp: E.tensor_tensor(
                a[:, 0:n], a[:, 0:n], t[:, 0:n], ALU.add), deps=[tk])
        tok = b.step(eng, eng, lambda E, n=ncols, nb=nblk, a=acc, m=amb2: E.tensor_tensor(
            a[:, 0:n].rearrange("p (blk e) -> p blk e", e=64),
            a[:, 0:n].rearrange("p (blk e) -> p blk e", e=64),
            m[:, 0:nb].unsqueeze(2).broadcast_to((128, nb, 64)), ALU.mult),
            deps=[tok, dep_am])
        return tok

    def dequant(cd, ncols, nblk, dep_ld, dep_am, dep_free):
        return dequant_on("v", dqa, dqt, AMB, cd, ncols, nblk, dep_ld, dep_am, dep_free)

    def pd_sweep(nsrc, src_cols, nblocks, blkw, dst_fn, dep_src_fn, am_src_fn):
        pass

    amg_free = None
    dqg_free = None
    for hc in range(HCH_N):
        sl = hc % 2
        use_g = False
        ldk = load(f"d_cd{sl}", cbuf[sl][:, :], cup[128 * hc:128 * (hc + 1), :],
                   deps=[cslot_free[sl]])
        if use_g:
            lda = load("d_amg", AMBG[:, 0:64], aup[128 * hc:128 * (hc + 1), :],
                       deps=[amg_free])
            dq = dequant_on("g", DQAG, DQTG, AMBG, cbuf[sl][:, :], 4096, 64,
                            ldk, lda, dqg_free)
            amg_free = dq
            dqg_free = None
        else:
            lda = load("d_am", AMB[:, 0:64], aup[128 * hc:128 * (hc + 1), :],
                       deps=[am_free])
            dq = dequant(cbuf[sl][:, :], 4096, 64, ldk, lda, dqa_free)
            am_free = dq
        cslot_free[sl] = dq
        accb = DQAG if use_g else dqa
        last_tp = None
        for g in range(8):
            bank = g % 4
            blocks = [(accb[:, (4 * g + i) * 128:(4 * g + i + 1) * 128], i * 128)
                      for i in range(4)]
            tp = transpose_group(bank, blocks, deps=[dq])
            sh = EVH[:, (g % 2) * 512:(g % 2) * 512 + 512]
            slo = EVL[:, (g % 2) * 512:(g % 2) * 512 + 512]
            ehk = b.step("s", "s", lambda E, bk=bank, st=sh: E.copy(st, psb(bk)),
                         deps=[tp, ev_free[g % 2]])
            elk = b.step("v", "v", lambda E, bk=bank, st=slo, hh=sh: E.tensor_tensor(
                st, psb(bk), hh, ALU.subtract), deps=[ehk])
            bank_tok[bank] = elk
            dsth = WUPTH.rearrange("(c p) h -> p c h", p=128)[:, 4 * g:4 * g + 4,
                                                              128 * hc:128 * (hc + 1)]
            dstl = WUPTL.rearrange("(c p) h -> p c h", p=128)[:, 4 * g:4 * g + 4,
                                                              128 * hc:128 * (hc + 1)]
            store(f"d_w{g % 2}", dsth, sh.rearrange("p (c f) -> p c f", f=128), deps=[ehk])
            ev_free[g % 2] = store(f"d_w{g % 2}", dstl,
                                   slo.rearrange("p (c f) -> p c f", f=128), deps=[elk])
            last_tp = tp
        if use_g:
            dqg_free = last_tp
        else:
            dqa_free = last_tp

    jstate = {"cd": [None, None], "am": [None, None],
              "acc": [None, None], "ev": [None, None]}

    def pd_down_job(jid, eng, bankbase=4):
        dc, half = jid // 2, jid % 2
        hlo_c = 0 if half == 0 else 11
        nhc = 11
        hw = nhc * 128
        blo = hlo_c * 2
        nblk = nhc * 2
        gi = 0 if eng == "v" else 1
        cdj = JCD[:, gi * 1408: gi * 1408 + hw]
        amj = JAM[:, gi * 22: gi * 22 + nblk]
        acc = JDA if eng == "v" else DQAG
        tmp = JDT if eng == "v" else DQTG
        ldk = load(f"d_jcd{gi}", cdj,
                   cdn[128 * dc:128 * (dc + 1), hlo_c * 128: hlo_c * 128 + hw],
                   deps=[jstate["cd"][gi]])
        lda = load(f"d_jam{gi}", amj, adn[128 * dc:128 * (dc + 1), blo:blo + nblk],
                   deps=[jstate["am"][gi]])
        dq = dequant_on(eng, acc, tmp, JAM if False else amj.unsqueeze(0).squeeze(0), hw, nblk, ldk, lda, jstate["acc"][gi]) if False else None
        tok = b.step(eng, eng, lambda E, c=cdj, n=hw, a=acc: E.tensor_scalar(
            a[:, 0:n], c, 0.0, NF4[0], ALU.is_equal, ALU.mult),
            deps=[ldk, jstate["acc"][gi]])
        for k in range(1, 16):
            tk = b.step(eng, eng, lambda E, c=cdj, n=hw, t=tmp, kk=float(k), vv=NF4[k]:
                        E.tensor_scalar(t[:, 0:n], c, kk, vv, ALU.is_equal, ALU.mult),
                        deps=[tok])
            tok = b.step(eng, eng, lambda E, n=hw, a=acc, t=tmp: E.tensor_tensor(
                a[:, 0:n], a[:, 0:n], t[:, 0:n], ALU.add), deps=[tk])
        dq = b.step(eng, eng, lambda E, n=hw, nb=nblk, a=acc, m=amj: E.tensor_tensor(
            a[:, 0:n].rearrange("p (blk e) -> p blk e", e=64),
            a[:, 0:n].rearrange("p (blk e) -> p blk e", e=64),
            m.unsqueeze(2).broadcast_to((128, nb, 64)), ALU.mult),
            deps=[tok, lda])
        jstate["cd"][gi] = dq
        jstate["am"][gi] = dq
        last_tp = None
        for g in range(3):
            bank = bankbase + (g % 2)
            ilo = 4 * g
            ihi = min(4 * g + 4, nhc)
            if ihi <= ilo:
                break
            blocks = [(acc[:, (ilo + i) * 128:(ilo + i + 1) * 128], i * 128)
                      for i in range(ihi - ilo)]
            tp = transpose_group(bank, blocks, deps=[dq])
            w = (ihi - ilo) * 128
            sh = JEVH[:, gi * 512:gi * 512 + w]
            slo2 = JEVL[:, gi * 512:gi * 512 + w]
            ehk = b.step("s", "s", lambda E, bk=bank, st=sh, w=w: E.copy(
                st, PS[:, 512 * bk:512 * bk + w]), deps=[tp, jstate["ev"][gi]])
            elk = b.step("v", "v", lambda E, bk=bank, st=slo2, hh=sh, w=w: E.tensor_tensor(
                st, PS[:, 512 * bk:512 * bk + w], hh, ALU.subtract), deps=[ehk])
            bank_tok[bank] = elk
            dsth = WDNTH.rearrange("(c p) d -> p c d", p=128)[:, hlo_c + ilo: hlo_c + ihi,
                                                              128 * dc:128 * (dc + 1)]
            dstl = WDNTL.rearrange("(c p) d -> p c d", p=128)[:, hlo_c + ilo: hlo_c + ihi,
                                                              128 * dc:128 * (dc + 1)]
            store(f"d_jw{gi}", dsth, sh.rearrange("p (c f) -> p c f", f=128), deps=[ehk])
            jstate["ev"][gi] = store(f"d_jw{gi}", dstl,
                                     slo2.rearrange("p (c f) -> p c f", f=128), deps=[elk])
            last_tp = tp
        jstate["acc"][gi] = last_tp


    for erc in range(4):
        sl = erc % 2
        src = cbuf[sl][:, 0:4096].bitcast(F32)
        ldk = load(f"d_cd{sl}", src, la1[128 * erc:128 * (erc + 1), :],
                   deps=[cslot_free[sl]])
        for g in range(8):
            bank = g % 4
            blocks = [(src[:, (4 * g + i) * 128:(4 * g + i + 1) * 128], i * 128)
                      for i in range(4)]
            tp = transpose_group(bank, blocks, deps=[ldk])
            sh = EVH[:, (g % 2) * 512:(g % 2) * 512 + 512]
            slo = EVL[:, (g % 2) * 512:(g % 2) * 512 + 512]
            ehk = b.step("s", "s", lambda E, bk=bank, st=sh: E.copy(st, psb(bk)),
                         deps=[tp, ev_free[g % 2]])
            elk = b.step("v", "v", lambda E, bk=bank, st=slo, hh=sh: E.tensor_tensor(
                st, psb(bk), hh, ALU.subtract), deps=[ehk])
            bank_tok[bank] = elk
            dsth = AATH.rearrange("(c p) e -> p c e", p=128)[:, 4 * g:4 * g + 4,
                                                             128 * erc:128 * (erc + 1)]
            dstl = AATL.rearrange("(c p) e -> p c e", p=128)[:, 4 * g:4 * g + 4,
                                                             128 * erc:128 * (erc + 1)]
            store(f"d_w{g % 2}", dsth, sh.rearrange("p (c f) -> p c f", f=128), deps=[ehk])
            ev_free[g % 2] = store(f"d_w{g % 2}", dstl,
                                   slo.rearrange("p (c f) -> p c f", f=128), deps=[elk])
            cslot_free[sl] = tp

    for e in range(8):
        sl = e % 2
        src = cbuf[sl][:, 0:2048].bitcast(F32)
        ldk = load(f"d_cd{sl}", src.rearrange("p (c r) -> p c r", r=64),
                   lb1[e * D:(e + 1) * D, :].rearrange("(c p) r -> p c r", p=128),
                   deps=[cslot_free[sl]])
        for g in range(8):
            bank = g % 4
            blocks = [(src[:, (4 * g + i) * 64:(4 * g + i + 1) * 64], i * 128)
                      for i in range(4)]
            tp = transpose_group(bank, blocks, deps=[ldk])
            stg = EV[:, (g % 2) * 2048:(g % 2) * 2048 + 512]
            evk = b.step("s", "s", lambda E, bk=bank, st=stg: E.copy(
                st[0:64, :], PS[0:64, 512 * bk:512 * bk + 512]),
                deps=[tp, ev_free[g % 2]])
            bank_tok[bank] = evk
            dst = BTS[64 * e:64 * e + 64, :].rearrange("r (c f) -> r c f", f=128)[:, 4 * g:4 * g + 4, :]
            ev_free[g % 2] = store(f"d_w{g % 2}", dst,
                                   stg[0:64, :].rearrange("p (c f) -> p c f", f=128),
                                   deps=[evk])
            cslot_free[sl] = tp

    src = cbuf[0][:, 0:4096].bitcast(F32)
    ldk = load("d_cd0", src[0:8, :], gw[:, :], deps=[cslot_free[0]])
    for g in range(8):
        bank = g % 4
        blocks = [(src[0:8, (4 * g + i) * 128:(4 * g + i + 1) * 128], i * 8)
                  for i in range(4)]
        tp = transpose_group(bank, blocks, deps=[ldk])
        ehk = b.step("s", "s", lambda E, bk=bank, g=g: E.copy(
            s_gwtH[:, 32 * g:32 * g + 32], PS[:, 512 * bk:512 * bk + 32]), deps=[tp])
        evk = b.step("v", "v", lambda E, bk=bank, g=g: E.tensor_tensor(
            s_gwtL[:, 32 * g:32 * g + 32], PS[:, 512 * bk:512 * bk + 32],
            s_gwtH[:, 32 * g:32 * g + 32], ALU.subtract), deps=[ehk])
        bank_tok[bank] = evk

    b.barrier()
    es_pd.close()
    es_pw.close()
    for i in range(8):
        bank_tok[i] = None

    if _STOP_AFTER == "pd":
        return _finish(nc, es, b, dbg_copies)

    es_kv = ExitStack()
    es_l1 = ExitStack()
    prev_sq = [None]

    def ln_subtile(XT, XSQ, dep_ld):
        t1 = b.step("v", "v", lambda E, X=XT: E.reduce_sum(sstat[:, 0:1], X[:, 0:D], AXX),
                    deps=[dep_ld])
        t2 = b.step("v", "v", lambda E: E.tensor_scalar(
            sstat[:, 1:2], sstat[:, 0:1], -1.0 / D, None, ALU.mult), deps=[t1])
        t3 = b.step("s", "s", lambda E, X=XT, Q=XSQ: E.activation(
            Q[:, 0:D], X[:, 0:D], AF.Square, bias=sstat[:, 1:2], scale=1.0,
            accum_out=sstat[:, 2:3]), deps=[t2, dep_ld, prev_sq[0]])
        prev_sq[0] = t3
        t4 = b.step("v", "v", lambda E: E.tensor_scalar(
            sstat[:, 3:4], sstat[:, 2:3], 1.0 / D, EPS, ALU.mult, ALU.add), deps=[t3])
        t5 = b.step("s", "s", lambda E: E.activation(sstat[:, 4:5], sstat[:, 3:4],
                                                     AF.Sqrt), deps=[t4])
        t6 = b.step("v", "v", lambda E: E.reciprocal(sstat[:, 5:6], sstat[:, 4:5]),
                    deps=[t5])
        t7 = b.step("v", "v", lambda E: E.tensor_tensor(
            sstat[:, 6:7], sstat[:, 1:2], sstat[:, 5:6], ALU.mult), deps=[t6])
        t8 = b.step("v", "v", lambda E, X=XT: E.tensor_scalar(
            X[:, 0:D], X[:, 0:D], sstat[:, 5:6], sstat[:, 6:7], ALU.mult, ALU.add),
            deps=[t7])
        return t8

    es_p1 = ExitStack()
    XNH = sbt(es_p1, "p1XNH", [128, 16384], BF16)
    XNL = sbt(es_p1, "p1XNL", [128, 16384], BF16)
    SLAB = sbt(es_p1, "p1SLAB", [128, 16384], BF16)
    XT = sbt(es_p1, "p1XT", [128, 4096])
    XSQ = sbt(es_p1, "p1XSQ", [128, 4096])
    XH = sbt(es_p1, "p1XH", [128, 4096], BF16)
    XL = sbt(es_p1, "p1XL", [128, 4096], BF16)
    STG = sbt(es_p1, "p1STG", [128, 2048])
    STGB = sbt(es_p1, "p1STGB", [128, 4096], BF16)
    vstg = STG[:, 0:512]
    x2stg = [STG[:, 512:1024], STG[:, 1024:1536]]
    uub = STG[0:16, 1536:2048]
    qstgh = [STGB[:, 0:512], STGB[:, 512:1024]]
    qstgl = [STGB[:, 1024:1536], STGB[:, 1536:2048]]
    KSB = sbt(es_kv, "KSBH", [128, 4096], BF16, side="right")
    KSBL = sbt(es_kv, "KSBL", [128, 4096], BF16, side="right")
    VTM = sbt(es_kv, "VTM", [128, 4096], side="right")
    slab_free = [None, None]
    stg_free = {}
    last_xt_user = [None]
    last_xh_user = [None]
    last_xnf_user = [None]
    p1_bank = [0]

    def psb16(k):
        return PS[:, 512 * k:512 * k + 256].bitcast(BF16)

    ready_h = [None]
    ready_l = [None]

    def make_xnf_tile(src_rows_fn, t0):
        ready = None
        for sub in range(4):
            ldx = load("d_xt", XT[:], src_rows_fn(t0 + sub * 128),
                       deps=[last_xt_user[0]])
            lnk = ln_subtile(XT, XSQ, ldx)
            hk = b.step("s", "s", lambda E: E.copy(XH[:], XT[:]), deps=[lnk, last_xh_user[0]])
            lk = b.step("v", "v", lambda E: E.tensor_tensor(XL[:], XT[:], XH[:],
                                                            ALU.subtract), deps=[hk])
            last_xt_user[0] = lk
            for comp in range(2):
                XSRC = XH if comp == 0 else XL
                XDST = XNH if comp == 0 else XNL
                ctok = hk if comp == 0 else lk
                for g in range(8):
                    bank = 4 + ((2 * g + comp) % 4)
                    blocks = [(XSRC[:, (4 * g + i) * 128:(4 * g + i + 1) * 128], i * 128)
                              for i in range(4)]
                    tp = transpose_group(bank, blocks, deps=[ctok, last_xnf_user[0]])
                    dst = XDST.rearrange("p (c t) -> p c t", t=512)[:, 4 * g:4 * g + 4,
                                                                   sub * 128:(sub + 1) * 128]
                    eng = "s" if comp == 0 else "v"
                    if comp == 0:
                        evk = b.step("s", "s", lambda E, bk=bank, d=dst: E.copy(
                            d, psb16(bk).rearrange("p (c f) -> p c f", f=128)), deps=[tp])
                    else:
                        evk = b.step("v", "v", lambda E, bk=bank, d=dst: E.tensor_copy(
                            d, psb16(bk).rearrange("p (c f) -> p c f", f=128)), deps=[tp])
                    bank_tok[bank] = evk
                    if comp == 0:
                        ready_h[0] = evk
                    else:
                        ready_l[0] = evk
                    last_xh_user[0] = tp
        return (ready_h[0], ready_l[0])

    def fclist(ti):
        return ([("AUP", 0)] + [("Q", j) for j in range(8)] + [("K", 0), ("V", 0)]
                + [("UP", j) for j in range(HCH_N)])

    for ti in range(NT):
        xnf_ready = make_xnf_tile(lambda r: xall[r:r + 128, :], ti * 512)
        uup_evac = None
        mm_tok = None
        for fci, (kind, idx) in enumerate(fclist(ti)):
            sl = fci % 2
            slab = SLAB[:, sl * 8192:(sl + 1) * 8192]
            if kind == "Q":
                srch = WQH[:, 128 * idx:128 * (idx + 1)]
                srcl = WQL[:, 128 * idx:128 * (idx + 1)]; w = 128
            elif kind == "K":
                srch = WKH[:, :]; srcl = WKL[:, :]; w = 128
            elif kind == "V":
                srch = WVH[:, :]; srcl = WVL[:, :]; w = 128
            elif kind == "AUP":
                srch = LAUPH[:, :]; srcl = LAUPL[:, :]; w = 16
            else:
                srch = WUPTH[:, 128 * idx:128 * (idx + 1)]
                srcl = WUPTL[:, 128 * idx:128 * (idx + 1)]; w = 128
            load(f"d_sl{sl}", slab[:, 0:32 * w].rearrange("p (c f) -> p c f", f=w),
                 srch.rearrange("(c p) f -> p c f", p=128), deps=[slab_free[sl]])
            slt = load(f"d_sl{sl}",
                       slab[:, 32 * w:64 * w].rearrange("p (c f) -> p c f", f=w),
                       srcl.rearrange("(c p) f -> p c f", p=128))
            bank = p1_bank[0]
            p1_bank[0] = (p1_bank[0] + 1) % 4
            m = 16 if kind == "AUP" else 128
            ems = []
            for dc in range(DC):
                for ph, xsrc in ((0, XNH), (1, XNL), (2, XNH)):
                    woff = (0 if ph < 2 else 32 * w) + dc * w
                    def em(E, st, sp, woff=woff, xs=xsrc, dc=dc, sl2=slab, w=w,
                           bk=bank, m=m):
                        return E.matmul(PS[0:m, 512 * bk:512 * bk + 512],
                                        sl2[:, woff:woff + w],
                                        xs[:, dc * 512:(dc + 1) * 512], start=st, stop=sp)
                    ems.append(em)
            dd = [slt] + list(xnf_ready)
            if kind == "AUP":
                dd.append(last_xnf_user[0])
            if kind == "UP":
                def emL(E, st, sp, bk=bank, idx=idx):
                    return E.matmul(PS[0:128, 512 * bk:512 * bk + 512],
                                    s_lbup[:, 128 * idx:128 * (idx + 1)],
                                    uub, start=st, stop=sp)
                ems.append(emL)
                dd += [uup_evac, t_lbup]
            mm_tok = mm_group(bank, ems, dd)
            slab_free[sl] = mm_tok
            if kind == "AUP":
                uup_evac = b.step("v", "v", lambda E, bk=bank:
                                  E.tensor_copy(uub, PS[0:16, 512 * bk:512 * bk + 512]),
                                  deps=[mm_tok])
                bank_tok[bank] = uup_evac
            elif kind == "K":
                ehk = b.step("s", "s", lambda E, bk=bank, ti=ti: E.copy(
                    KSB[:, ti * 512:(ti + 1) * 512], psb(bk)), deps=[mm_tok])
                elk = b.step("v", "v", lambda E, bk=bank, ti=ti: E.tensor_tensor(
                    KSBL[:, ti * 512:(ti + 1) * 512], psb(bk),
                    KSB[:, ti * 512:(ti + 1) * 512], ALU.subtract), deps=[ehk])
                bank_tok[bank] = elk
            elif kind == "V":
                evk = b.step("s", "s", lambda E, bk=bank: E.copy(vstg[:], psb(bk)),
                             deps=[mm_tok, stg_free.get("v")])
                bank_tok[bank] = evk
                vb = 4 + (fci % 4)
                blocks = [(vstg[:, i * 128:(i + 1) * 128], i * 128) for i in range(4)]
                tp = transpose_group(vb, blocks, deps=[evk])
                stg_free["v"] = tp
                vev = b.step("v", "v", lambda E, bk=vb, ti=ti: E.tensor_copy(
                    VTM[:, ti * 512:(ti + 1) * 512], psb(bk)), deps=[tp])
                bank_tok[vb] = vev
            elif kind == "Q":
                qh = qstgh[fci % 2]
                ql = qstgl[fci % 2]
                ehk = b.step("s", "s", lambda E, bk=bank, st=qh: E.copy(st, psb(bk)),
                             deps=[mm_tok, stg_free.get(f"qh{fci % 2}")])
                elk = b.step("v", "v", lambda E, bk=bank, sl3=ql, hh=qh: E.tensor_tensor(
                    sl3, psb(bk), hh, ALU.subtract),
                    deps=[ehk, stg_free.get(f"ql{fci % 2}")])
                bank_tok[bank] = elk
                stg_free[f"qh{fci % 2}"] = store(
                    "d_qs", QSH[128 * idx:128 * (idx + 1), ti * 512:(ti + 1) * 512],
                    qh, deps=[ehk])
                stg_free[f"ql{fci % 2}"] = store(
                    "d_qs", QSL[128 * idx:128 * (idx + 1), ti * 512:(ti + 1) * 512],
                    ql, deps=[elk])
            else:
                xs = x2stg[fci % 2]
                xh2 = STGB[:, 2048 + (fci % 2) * 512: 2048 + (fci % 2) * 512 + 512]
                xl2 = STGB[:, 3072 + (fci % 2) * 512: 3072 + (fci % 2) * 512 + 512]
                r32 = b.step("s", "s", lambda E, bk=bank, st=xs, idx=idx: E.activation(
                    st, psb(bk), AF.Relu, bias=s_bup[:, idx:idx + 1], scale=1.0),
                    deps=[mm_tok, t_bup, stg_free.get(f"x{fci % 2}")])
                hk = b.step("s", "s", lambda E, st=xs, d=xh2: E.copy(d, st),
                            deps=[r32, stg_free.get(f"xh{fci % 2}")])
                lk = b.step("v", "v", lambda E, st=xs, d=xl2, hh=xh2: E.tensor_tensor(
                    d, st, hh, ALU.subtract), deps=[hk, stg_free.get(f"xl{fci % 2}")])
                bank_tok[bank] = r32
                stg_free[f"x{fci % 2}"] = lk
                stg_free[f"xh{fci % 2}"] = store(
                    "d_x2", X2SH[128 * idx:128 * (idx + 1), ti * 512:(ti + 1) * 512],
                    xh2, deps=[hk])
                stg_free[f"xl{fci % 2}"] = store(
                    "d_x2", X2SL[128 * idx:128 * (idx + 1), ti * 512:(ti + 1) * 512],
                    xl2, deps=[lk])
        last_xnf_user[0] = mm_tok

    b.barrier()
    es_p1.close()
    for i in range(8):
        bank_tok[i] = None

    if _STOP_AFTER == "p1":
        return _finish(nc, es, b, dbg_copies)

    es_p2 = ExitStack()
    A1 = sbt(es_p2, "p2A1", [128, 16384])
    A2 = sbt(es_p2, "p2A2", [128, 16384])
    SM = sbt(es_p2, "p2SM", [128, 1024])
    SMB = sbt(es_p2, "p2SMB", [128, 2048], BF16)
    sc = A1[:, 0:16384]
    at = A2[:, 0:16384]
    qtih = [SMB[:, 0:512], SMB[:, 1024:1536]]
    qtil = [SMB[:, 512:1024], SMB[:, 1536:2048]]
    SMB2 = sbt(es_p2, "p2SMB2", [128, 1024], BF16)
    aostgh = SMB2[:, 0:512]
    aostgl = SMB2[:, 512:1024]

    AT2 = sbt(es_p2, "p2AT2", [128, 1024])
    qld_free = [None, None]
    ao_free = None
    sc_free = None
    pr_free = None
    bank6_free = None
    for qb in range(8):
        kb0 = 0 if qb == 7 else qb
        nkb = 8 - kb0
        nkc = 4 * nkb
        kmax = 512 * nkb
        for h in range(8):
            qsl2 = h % 2
            load(f"d_q{qsl2}", qtih[qsl2][:], QSH[128 * h:128 * (h + 1), qb * 512:(qb + 1) * 512],
                 deps=[qld_free[qsl2]])
            qld = load(f"d_q{qsl2}", qtil[qsl2][:], QSL[128 * h:128 * (h + 1), qb * 512:(qb + 1) * 512])
            norm_toks = []
            last_mmt = None
            for sub in range(4):
                scs = sc[:, sub * 4096: sub * 4096 + kmax]
                prs = at[:, sub * 4096: sub * 4096 + kmax]
                ev_last = None
                for kb in range(nkb):
                    bank = sub % 2
                    ems0 = []
                    for qa, ka in ((qtih[qsl2], KSB), (qtih[qsl2], KSBL), (qtil[qsl2], KSB)):
                        def em(E, st, sp, kb=kb, sub=sub, bk=bank, kb0=kb0, qa=qa, ka=ka):
                            return E.matmul(psb(bk), qa[:, sub * 128:(sub + 1) * 128],
                                            ka[:, (kb0 + kb) * 512:(kb0 + kb + 1) * 512],
                                            start=st, stop=sp)
                        ems0.append(em)
                    mmt = mm_group(bank, ems0, [qld, sc_free])
                    last_mmt = mmt
                    ev_last = b.step("s", "s",
                                     lambda E, bk=bank, d=scs[:, kb * 512:(kb + 1) * 512]:
                                     E.copy(d, psb(bk)), deps=[mmt])
                    bank_tok[bank] = ev_last
                dcol = qb * 512 + sub * 128 - 512 * kb0
                tok = ev_last
                if dcol > 0:
                    tok = b.step("v", "v", lambda E, d=scs[:, 0:dcol]:
                                 E.memset(d, NEG), deps=[tok])
                tok = b.step("v", "v", lambda E, d=scs[:, dcol:dcol + 128]:
                             E.tensor_tensor(d, d, keep01[:], ALU.mult), deps=[tok])
                tok = b.step("v", "v", lambda E, d=scs[:, dcol:dcol + 128]:
                             E.tensor_tensor(d, d, trimask[:], ALU.add), deps=[tok])
                mx = b.step("v", "v", lambda E, s=scs, km=kmax: E.reduce_max(
                    sstat[:, 8:9], s[:, 0:km], AXX), deps=[tok])
                nb2 = b.step("v", "v", lambda E: E.tensor_scalar(
                    sstat[:, 9:10], sstat[:, 8:9], -SC0, None, ALU.mult), deps=[mx])
                ex = b.step("s", "s", lambda E, s=scs, km=kmax: E.activation(
                    s[:, 0:km], s[:, 0:km], AF.Exp, bias=sstat[:, 9:10], scale=SC0,
                    accum_out=sstat[:, 10:11]), deps=[nb2])
                rc = b.step("v", "v", lambda E: E.reciprocal(
                    sstat[:, 11:12], sstat[:, 10:11]), deps=[ex])
                nm = b.step("v", "v", lambda E, s=scs, p=prs, km=kmax: E.tensor_scalar(
                    p[:, 0:km], s[:, 0:km], sstat[:, 11:12], None, ALU.mult),
                    deps=[rc, pr_free])
                norm_toks.append(nm)
            qld_free[qsl2] = last_mmt
            sc_free = norm_toks[-1]
            last_tp = None
            av_tok = [None, None]
            evks = [None] * nkc

            def emit_av(kc):
                sl = kc % 2
                av_tok[sl] = b.step("t", "t", lambda E, kc=kc, kb0=kb0, sl=sl, last=(kc == nkc - 1):
                                    E.matmul(psb(6),
                                             VTM[:, (4 * kb0 + kc) * 128:(4 * kb0 + kc + 1) * 128],
                                             AT2[:, sl * 512:(sl + 1) * 512],
                                             start=(kc == 0), stop=last),
                                    deps=([evks[kc], bank6_free] if kc == 0 else [evks[kc]]))

            for kc in range(nkc):
                bank = 2 + (kc % 2)
                blocks = [(at[:, sub * 4096 + kc * 128: sub * 4096 + (kc + 1) * 128],
                           sub * 128) for sub in range(4)]
                tp = transpose_group(bank, blocks, deps=norm_toks)
                sl = kc % 2
                evks[kc] = b.step("v", "v", lambda E, bk=bank, sl=sl:
                                  E.tensor_copy(AT2[:, sl * 512:(sl + 1) * 512], psb(bk)),
                                  deps=[tp, av_tok[sl]])
                bank_tok[bank] = evks[kc]
                if kc > 0:
                    emit_av(kc - 1)
                last_tp = tp
            emit_av(nkc - 1)
            pr_free = last_tp
            avt = av_tok[(nkc - 1) % 2]
            ehk = b.step("s", "s", lambda E: E.copy(aostgh[:], psb(6)),
                         deps=[avt, ao_free])
            elk = b.step("v", "v", lambda E: E.tensor_tensor(
                aostgl[:], psb(6), aostgh[:], ALU.subtract), deps=[ehk])
            bank6_free = elk
            store("d_ao", AOSH[128 * h:128 * (h + 1), qb * 512:(qb + 1) * 512],
                  aostgh[:], deps=[ehk])
            ao_free = store("d_ao", AOSL[128 * h:128 * (h + 1), qb * 512:(qb + 1) * 512],
                            aostgl[:], deps=[elk])

    b.barrier()
    es_p2.close()
    es_kv.close()
    for i in range(8):
        bank_tok[i] = None

    if _STOP_AFTER == "p2":
        return _finish(nc, es, b, dbg_copies)

    es_dq = ExitStack()
    JCD = sbt(es_dq, "pdJCD", [128, 2816], I32)
    JAM = sbt(es_dq, "pdJAM", [128, 44])
    JEVH = sbt(es_dq, "pdJEVH", [128, 1024], BF16)
    JEVL = sbt(es_dq, "pdJEVL", [128, 1024], BF16)
    JDA = sbt(es_dq, "pdJDA", [128, 1408])
    JDT = sbt(es_dq, "pdJDT", [128, 1408])
    es_p3 = ExitStack()
    A1 = sbt(es_p3, "p3A1", [128, 32768], BF16)
    AOT = sbt(es_p3, "p3AOT", [128, 4096], BF16)
    EV3 = sbt(es_p3, "p3EV", [128, 4096])
    slab_tok = None
    slab_users3 = None
    ao_free3 = [None, None]
    ev_free3 = [None, None]
    for dh in range(2):
        load("d_wo", A1[:, 0:16384].rearrange("p (c f) -> p c f", f=2048),
             WOH.rearrange("(c p) d -> p c d", p=128)[:, :, dh * 2048:(dh + 1) * 2048],
             deps=[slab_tok, slab_users3])
        slab_tok = load("d_wo", A1[:, 16384:32768].rearrange("p (c f) -> p c f", f=2048),
                        WOL.rearrange("(c p) d -> p c d", p=128)[:, :, dh * 2048:(dh + 1) * 2048])
        for tc in range(32):
            sl = tc % 2
            aoth = AOT[:, sl * 1024:(sl + 1) * 1024]
            aotl = AOT[:, 2048 + sl * 1024: 2048 + (sl + 1) * 1024]
            load(f"d_aot{sl}", aoth.rearrange("p (c f) -> p c f", f=128),
                 AOSH.rearrange("(c p) t -> p c t", p=128)[:, :, tc * 128:(tc + 1) * 128],
                 deps=[ao_free3[sl]])
            aold = load(f"d_aot{sl}", aotl.rearrange("p (c f) -> p c f", f=128),
                        AOSL.rearrange("(c p) t -> p c t", p=128)[:, :, tc * 128:(tc + 1) * 128])
            b0 = 4 * (tc % 2)
            gtoks = []
            for nb in range(4):
                ems = []
                for j in range(8):
                    for aop, woff in ((aoth, 0), (aoth, 16384), (aotl, 0)):
                        def em(E, st, sp, nb=nb, j=j, b0=b0, a=aop, wo2=woff):
                            return E.matmul(PS[:, (b0 + nb) * 512:(b0 + nb) * 512 + 512],
                                            a[:, j * 128:(j + 1) * 128],
                                            A1[:, wo2 + j * 2048 + nb * 512: wo2 + j * 2048 + nb * 512 + 512],
                                            start=st, stop=sp)
                        ems.append(em)
                gt = mm_group(b0 + nb, ems, [aold, slab_tok])
                gtoks.append(gt)
            ao_free3[sl] = gtoks[-1]
            stg = EV3[:, sl * 2048:(sl + 1) * 2048]
            evk = b.step("v", "v", lambda E, b0=b0, st=stg: E.tensor_copy(
                st, PS[:, b0 * 512:b0 * 512 + 2048]), deps=gtoks + [ev_free3[sl]])
            for nb in range(4):
                bank_tok[b0 + nb] = evk
            ev_free3[sl] = store("d_aps", APS[tc * 128:(tc + 1) * 128,
                                              dh * 2048:(dh + 1) * 2048], stg, deps=[evk])
            slab_users3 = gtoks[-1]
            pd_down_job(dh * 32 + tc, "v", bankbase=b0)
    b.barrier()
    es_p3.close()
    es_dq.close()
    for i in range(8):
        bank_tok[i] = None

    if _STOP_AFTER == "p3":
        return _finish(nc, es, b, dbg_copies)

    es_p5 = ExitStack()
    WSL = sbt(es_p5, "p5WSL", [128, 2 * HCH_N * 1024], BF16)
    X2T = sbt(es_p5, "p5X2T", [128, 2 * HCH_N * 512], BF16)
    ATL = sbt(es_p5, "p5ATL", [128, 2048])
    EV5 = sbt(es_p5, "p5EV", [128, 2048])
    UDB = sbt(es_p5, "p5UDB", [128, 512])
    s_ldna32 = sbt(es_p5, "p5ldna32", [128, HCH_N * 16])
    s_ldnaH = sbt(es_p5, "p5ldnaH", [128, HCH_N * 16], BF16)
    s_ldnaL = sbt(es_p5, "p5ldnaL", [128, HCH_N * 16], BF16)
    s_ldnb = sbt(es_p5, "p5ldnb", [16, D])
    s_bdn = sbt(es_p5, "p5bdn", [1, D])
    t_ldnb = load("d_c", s_ldnb[:], ldnb[:, :])
    t_bdn = load("d_c", s_bdn[:], bdn8[:, :])
    t_la32 = load("d_c", s_ldna32[:].rearrange("p (c r) -> p c r", r=16),
                  ldna[:, :].rearrange("(c p) r -> p c r", p=128))
    t_lah = b.step("s", "s", lambda E: E.copy(s_ldnaH[:], s_ldna32[:]), deps=[t_la32])
    t_lal = b.step("v", "v", lambda E: E.tensor_tensor(
        s_ldnaL[:], s_ldna32[:], s_ldnaH[:], ALU.subtract), deps=[t_lah])
    NH2 = HCH_N * 1024
    NX2 = HCH_N * 512
    wsl_tok = None
    x2_free = None
    atl_free = [None, None]
    ev_free5 = [None, None]
    ud_free = [None]
    cc_toks = []
    for dq4 in range(4):
        load("d_wsl", WSL[:, 0:NH2].rearrange("p (c f) -> p c f", f=1024),
             WDNTH.rearrange("(c p) d -> p c d", p=128)[:, :, dq4 * 1024:(dq4 + 1) * 1024],
             deps=[wsl_tok, x2_free])
        wsl_tok = load("d_wsl", WSL[:, NH2:2 * NH2].rearrange("p (c f) -> p c f", f=1024),
                       WDNTL.rearrange("(c p) d -> p c d", p=128)[:, :, dq4 * 1024:(dq4 + 1) * 1024])
        for ti in range(NT):
            load("d_x2t", X2T[:, 0:NX2].rearrange("p (c f) -> p c f", f=512),
                 X2SH.rearrange("(c p) t -> p c t", p=128)[:, :, ti * 512:(ti + 1) * 512],
                 deps=[x2_free])
            x2ld = load("d_x2t", X2T[:, NX2:2 * NX2].rearrange("p (c f) -> p c f", f=512),
                        X2SL.rearrange("(c p) t -> p c t", p=128)[:, :, ti * 512:(ti + 1) * 512])
            ems = []
            for hc in range(HCH_N):
                for aop, xoff in ((s_ldnaH, 0), (s_ldnaH, NX2), (s_ldnaL, 0)):
                    def em(E, st, sp, hc=hc, a=aop, xo=xoff):
                        return E.matmul(PS[0:16, 7 * 512:7 * 512 + 512],
                                        a[:, hc * 16:(hc + 1) * 16],
                                        X2T[:, xo + hc * 512: xo + (hc + 1) * 512],
                                        start=st, stop=sp)
                    ems.append(em)
            udt = mm_group(7, ems, [x2ld, t_lah, t_lal, ud_free[0]])
            ud_done_t = b.step("v", "v", lambda E: E.tensor_copy(
                UDB[0:16, :], PS[0:16, 7 * 512:7 * 512 + 512]), deps=[udt])
            bank_tok[7] = ud_done_t
            last_g = None
            for sub in range(4):
                b0 = 0 if (sub % 2 == 0) else 2
                gtoks = []
                for nb in range(2):
                    bk = b0 + nb
                    ems = []
                    for hc in range(HCH_N):
                        for xoff, woff in ((0, 0), (NX2, 0), (0, NH2)):
                            def em(E, st, sp, hc=hc, nb=nb, sub=sub, bk=bk, xo=xoff, wo2=woff):
                                return E.matmul(psb(bk),
                                                X2T[:, xo + hc * 512 + sub * 128: xo + hc * 512 + sub * 128 + 128],
                                                WSL[:, wo2 + hc * 1024 + nb * 512: wo2 + hc * 1024 + nb * 512 + 512],
                                                start=st, stop=sp)
                            ems.append(em)
                    def emlora(E, st, sp, nb=nb, sub=sub, dq4=dq4, bk=bk):
                        return E.matmul(psb(bk),
                                        UDB[0:16, sub * 128: sub * 128 + 128],
                                        s_ldnb[:, dq4 * 1024 + nb * 512: dq4 * 1024 + nb * 512 + 512],
                                        start=st, stop=sp)
                    ems.append(emlora)
                    def embd(E, st, sp, nb=nb, dq4=dq4, bk=bk):
                        return E.matmul(psb(bk), ones[0:1, 0:128],
                                        s_bdn[0:1, dq4 * 1024 + nb * 512: dq4 * 1024 + nb * 512 + 512],
                                        start=st, stop=sp)
                    ems.append(embd)
                    gt = mm_group(bk, ems, [x2ld, wsl_tok, ud_done_t, t_ldnb, t_bdn])
                    gtoks.append(gt)
                slq = sub % 2
                stg = EV5[:, slq * 1024:(slq + 1) * 1024]
                trow = ti * 512 + sub * 128
                dd = gtoks + [ev_free5[slq]]
                atl = ATL[:, slq * 1024:(slq + 1) * 1024]
                atld = load(f"d_atl{slq}", atl, APS[trow:trow + 128,
                                                    dq4 * 1024:(dq4 + 1) * 1024],
                            deps=[atl_free[slq]])
                e1 = b.step("v", "v", lambda E, st=stg, b0=b0, a=atl: E.tensor_tensor(
                    st, PS[:, b0 * 512:b0 * 512 + 1024], a, ALU.add),
                    deps=dd + [atld])
                atl_free[slq] = e1
                for bk in (b0, b0 + 1):
                    bank_tok[bk] = e1
                ev_free5[slq] = store("d_part", PARTQ[dq4][trow:trow + 128, :],
                                      stg, deps=[e1])
                last_g = gtoks[-1]
            x2_free = last_g
            ud_free[0] = last_g
        b.wait("g", ("d_part", b.cnt["d_part"]))
        cc_toks.append(b.step("g", "cc", lambda E, q=dq4: E.collective_compute(
            "ReduceScatter", ALU.add, replica_groups=RS1_GROUPS,
            ins=[PARTQ[q][:, :]], outs=[R2OUTQ[q][:, :]])))

    b.barrier()
    es_p5.close()
    for i in range(8):
        bank_tok[i] = None

    if _STOP_AFTER == "p5":
        return _finish(nc, es, b, dbg_copies)

    b.barrier()

    if _STOP_AFTER == "l0":
        return _finish(nc, es, b, dbg_copies)

    es_p8 = ExitStack()
    XNH8 = sbt(es_p8, "p8XNH", [128, 16384], BF16)
    XNL8 = sbt(es_p8, "p8XNL", [128, 16384], BF16)
    SLAB8 = sbt(es_p8, "p8SLAB", [128, 16384], BF16)
    XT = sbt(es_p8, "p8XT", [128, 4096])
    XSQ = sbt(es_p8, "p8XSQ", [128, 4096])
    XH8 = sbt(es_p8, "p8XH", [128, 4096], BF16)
    XL8 = sbt(es_p8, "p8XL", [128, 4096], BF16)
    STG8 = sbt(es_p8, "p8STG", [128, 3072])
    uall = sbt(es_l1, "uall", [128, 4096], side="right")
    glog = sbt(es_l1, "glog", [8, 1024], side="right")
    vstg = STG8[:, 0:512]
    qstg8 = [STG8[:, 512:1024], STG8[:, 1024:1536]]
    kstg8 = [STG8[:, 1536:2048], STG8[:, 2048:2560]]
    vstg2 = STG8[:, 2560:3072]
    slab_free = [None, None]
    stg_free = {}
    last_xt_user[0] = None
    last_xh_user[0] = None
    last_xnf_user[0] = None

    def x1_sub(trow):
        ldo = load("d_xt", XT[:], xown[trow:trow + 128, :], deps=[last_xt_user[0]])
        ldr = None
        for q in range(4):
            ldr = load("d_xr", XSQ[:, q * 1024:(q + 1) * 1024],
                       R2OUTQ[q][trow:trow + 128, :],
                       deps=([prev_sq[0]] if q == 0 else ()))
        ad = b.step("v", "v", lambda E: E.tensor_tensor(XT[:], XT[:], XSQ[:], ALU.add),
                    deps=[ldo, ldr])
        st = store("d_x1", X1S[trow:trow + 128, :], XT[:], deps=[ad])
        return ln_subtile(XT, XSQ, st)

    ready8 = [None, None]
    for ti1 in range(2):
        for sub in range(4):
            trow = ti1 * 512 + sub * 128
            lnk = x1_sub(trow)
            hk = b.step("s", "s", lambda E: E.copy(XH8[:], XT[:]),
                        deps=[lnk, last_xh_user[0]])
            lk = b.step("v", "v", lambda E: E.tensor_tensor(XL8[:], XT[:], XH8[:],
                                                            ALU.subtract), deps=[hk])
            last_xt_user[0] = lk
            for comp in range(2):
                XSRC = XH8 if comp == 0 else XL8
                XDST = XNH8 if comp == 0 else XNL8
                ctok = hk if comp == 0 else lk
                for g in range(8):
                    bank = 4 + ((2 * g + comp) % 4)
                    blocks = [(XSRC[:, (4 * g + i) * 128:(4 * g + i + 1) * 128], i * 128)
                              for i in range(4)]
                    tp = transpose_group(bank, blocks, deps=[ctok, last_xnf_user[0]])
                    dst = XDST.rearrange("p (c t) -> p c t", t=512)[:, 4 * g:4 * g + 4,
                                                                   sub * 128:(sub + 1) * 128]
                    eng = "s" if comp == 0 else "v"
                    if comp == 0:
                        evk = b.step("s", "s", lambda E, bk=bank, d=dst: E.copy(
                            d, psb16(bk).rearrange("p (c f) -> p c f", f=128)), deps=[tp])
                    else:
                        evk = b.step("v", "v", lambda E, bk=bank, d=dst: E.tensor_copy(
                            d, psb16(bk).rearrange("p (c f) -> p c f", f=128)), deps=[tp])
                    bank_tok[bank] = evk
                    ready8[comp] = evk
                    last_xh_user[0] = tp
        fcs = [("QKV", j) for j in range(24)] + [("G", 0)] + [("MA", j) for j in range(4)]
        mm_tok = None
        for fci, (kind, idx) in enumerate(fcs):
            bank = fci % 4
            m = 8 if kind == "G" else 128
            if kind in ("QKV", "MA"):
                sl = fci % 2
                slab = SLAB8[:, sl * 8192:(sl + 1) * 8192]
                if kind == "QKV":
                    sh2 = WQKVH[:, 128 * idx:128 * (idx + 1)]
                    sl2 = WQKVL[:, 128 * idx:128 * (idx + 1)]
                else:
                    sh2 = AATH[:, 128 * idx:128 * (idx + 1)]
                    sl2 = AATL[:, 128 * idx:128 * (idx + 1)]
                load(f"d_sl{sl}", slab[:, 0:4096].rearrange("p (c f) -> p c f", f=128),
                     sh2.rearrange("(c p) f -> p c f", p=128), deps=[slab_free[sl]])
                slt = load(f"d_sl{sl}",
                           slab[:, 4096:8192].rearrange("p (c f) -> p c f", f=128),
                           sl2.rearrange("(c p) f -> p c f", p=128))
                def lhs_fn(dc, ph, slab=slab):
                    return slab[:, (0 if ph < 2 else 4096) + dc * 128:
                                (0 if ph < 2 else 4096) + dc * 128 + 128]
                dd = [slt] + ready8
            else:
                def lhs_fn(dc, ph):
                    g2 = s_gwtH if ph < 2 else s_gwtL
                    return g2[:, 8 * dc:8 * dc + 8]
                dd = list(ready8)
                slt = None
            ems = []
            for dc in range(DC):
                for ph, xsrc in ((0, XNH8), (1, XNL8), (2, XNH8)):
                    def em(E, st, sp, dc=dc, ph=ph, f=lhs_fn, bk=bank, m=m, xs=xsrc):
                        return E.matmul(PS[0:m, 512 * bk:512 * bk + 512], f(dc, ph),
                                        xs[:, dc * 512:(dc + 1) * 512], start=st, stop=sp)
                    ems.append(em)
            mm_tok = mm_group(bank, ems, dd)
            if kind in ("QKV", "MA"):
                slab_free[sl] = mm_tok
            if kind == "QKV" and idx < 8:
                qs = qstg8[fci % 2]
                evk = b.step("v", "v", lambda E, bk=bank, st=qs, idx=idx: E.tensor_scalar(
                    st, psb(bk), s_bqkv[:, idx:idx + 1], None, ALU.add),
                    deps=[mm_tok, t_bqkv, stg_free.get(f"q8{fci % 2}")])
                bank_tok[bank] = evk
                stg_free[f"q8{fci % 2}"] = store(
                    "d_qs1", QS1[128 * idx:128 * (idx + 1), ti1 * 512:(ti1 + 1) * 512],
                    qs, deps=[evk])
            elif kind == "QKV" and idx < 16:
                ks = kstg8[fci % 2]
                evk = b.step("v", "v", lambda E, bk=bank, st=ks, idx=idx: E.tensor_scalar(
                    st, psb(bk), s_bqkv[:, idx:idx + 1], None, ALU.add),
                    deps=[mm_tok, t_bqkv, stg_free.get(f"k8{fci % 2}")])
                bank_tok[bank] = evk
                stg_free[f"k8{fci % 2}"] = store(
                    "d_ag", AGIN[128 * (idx - 8):128 * (idx - 7),
                                 ti1 * 512:(ti1 + 1) * 512], ks, deps=[evk])
            elif kind == "QKV":
                evk = b.step("v", "v", lambda E, bk=bank, idx=idx: E.tensor_scalar(
                    vstg[:], psb(bk), s_bqkv[:, idx:idx + 1], None, ALU.add),
                    deps=[mm_tok, t_bqkv, stg_free.get("v8")])
                bank_tok[bank] = evk
                vb = 4 + (fci % 4)
                blocks = [(vstg[:, i * 128:(i + 1) * 128], i * 128) for i in range(4)]
                tp = transpose_group(vb, blocks, deps=[evk])
                stg_free["v8"] = tp
                vev = b.step("v", "v", lambda E, bk=vb, sv=vstg2: E.tensor_copy(
                    sv, psb(bk)), deps=[tp, stg_free.get("v8st")])
                bank_tok[vb] = vev
                dst = AGIN[1024 + ti1 * 512: 1024 + (ti1 + 1) * 512,
                           128 * (idx - 16):128 * (idx - 15)]
                stg_free["v8st"] = store(
                    "d_ag", dst.rearrange("(c p) f -> p c f", p=128),
                    vstg2.rearrange("p (c f) -> p c f", f=128), deps=[vev])
            elif kind == "G":
                evk = b.step("v", "v", lambda E, bk=bank, ti1=ti1: E.tensor_copy(
                    glog[0:8, ti1 * 512:(ti1 + 1) * 512], PS[0:8, 512 * bk:512 * bk + 512]),
                    deps=[mm_tok])
                bank_tok[bank] = evk
            else:
                evk = b.step("v", "v", lambda E, bk=bank, idx=idx, ti1=ti1: E.tensor_copy(
                    uall[:, idx * 1024 + ti1 * 512: idx * 1024 + (ti1 + 1) * 512],
                    psb(bk)), deps=[mm_tok])
                bank_tok[bank] = evk
        last_xnf_user[0] = mm_tok

    b.barrier()
    es_p8.close()
    for i in range(8):
        bank_tok[i] = None

    cc3 = b.step("g", "cc", lambda E: E.collective_compute(
        "AllGather", ALU.bypass, replica_groups=AG_GROUPS,
        ins=[AGIN[:, :]], outs=[AGOUT[:, :]]))
    b.barrier()

    if _STOP_AFTER == "p9":
        return _finish(nc, es, b, dbg_copies)

    es_p10 = ExitStack()
    SC1B = sbt(es_p10, "pASC", [128, 8192])
    AT1 = sbt(es_p10, "pAAT", [128, 8192])
    QT1 = sbt(es_p10, "pAQT", [128, 2048])
    KB1 = sbt(es_p10, "pAKB", [128, 4096])
    VB1 = sbt(es_p10, "pAVB", [128, 4096])
    af = sbt(es_l1, "af", [128, 8192], side="right")
    qfree = None
    kfree = None
    vfree = None
    scfree = None
    atfree = None
    for qb1 in range(4):
        qld = load("d_q1", QT1[:, :].rearrange("p (c f) -> p c f", f=256),
                   QS1.rearrange("(c p) t -> p c t", p=128)[:, :, qb1 * 256:(qb1 + 1) * 256],
                   deps=[qfree])
        exp_toks = []
        ev_last = None
        for kb in range(8):
            r, hf = kb // 2, kb % 2
            kld = load("d_k1", KB1[:, :].rearrange("p (c f) -> p c f", f=512),
                       AGOUT[r * 2048: r * 2048 + 1024, :].rearrange(
                           "(c p) t -> p c t", p=128)[:, :, hf * 512:(hf + 1) * 512],
                       deps=[kfree])
            for sub in range(2):
                bank = sub
                ems = []
                for dkc in range(8):
                    def em(E, st, sp, dkc=dkc, sub=sub, bk=bank):
                        return E.matmul(psb(bk),
                                        QT1[:, dkc * 256 + sub * 128: dkc * 256 + sub * 128 + 128],
                                        KB1[:, dkc * 512:(dkc + 1) * 512], start=st, stop=sp)
                    ems.append(em)
                mmt = mm_group(bank, ems, [qld, kld, scfree])
                ev_last = b.step("v", "v", lambda E, bk=bank,
                                 d=SC1B[:, sub * 4096 + kb * 512: sub * 4096 + (kb + 1) * 512]:
                                 E.tensor_copy(d, psb(bk)), deps=[mmt])
                bank_tok[bank] = ev_last
            kfree = mmt
        for sub in range(2):
            scs = SC1B[:, sub * 4096: sub * 4096 + 4096]
            mx = b.step("v", "v", lambda E, s=scs: E.reduce_max(
                sstat[:, 8:9], s, AXX), deps=[ev_last])
            nb2 = b.step("v", "v", lambda E: E.tensor_scalar(
                sstat[:, 9:10], sstat[:, 8:9], -SC1, None, ALU.mult), deps=[mx])
            ex = b.step("s", "s", lambda E, s=scs: E.activation(
                s, s, AF.Exp, bias=sstat[:, 9:10], scale=SC1,
                accum_out=sstat[:, 10:11]), deps=[nb2])
            rc = b.step("v", "v", lambda E: E.reciprocal(
                sstat[:, 11:12], sstat[:, 10:11]), deps=[ex])
            nm = b.step("v", "v", lambda E, s=scs: E.tensor_scalar(
                s, s, sstat[:, 11:12], None, ALU.mult), deps=[rc])
            exp_toks.append(nm)
        at_ready = None
        last_tp = None
        for kc in range(32):
            bank = 2 + (kc % 2)
            blocks = [(SC1B[:, sub * 4096 + kc * 128: sub * 4096 + (kc + 1) * 128],
                       sub * 128) for sub in range(2)]
            tp = transpose_group(bank, blocks, deps=exp_toks + [atfree])
            evk = b.step("v", "v", lambda E, bk=bank,
                         d=AT1[:, kc * 256:(kc + 1) * 256]:
                         E.tensor_copy(d, PS[:, 512 * bk:512 * bk + 256]), deps=[tp])
            bank_tok[bank] = evk
            at_ready = evk
            last_tp = tp
        scfree = last_tp
        first = [True] * 8
        avtoks = [None] * 8
        for kb in range(8):
            r, hf = kb // 2, kb % 2
            vld = load("d_v1", VB1[:, :].rearrange("p (c f) -> p c f", f=1024),
                       AGOUT[r * 2048 + 1024 + hf * 512: r * 2048 + 1024 + (hf + 1) * 512, :]
                       .rearrange("(c p) f -> p c f", p=128),
                       deps=[vfree])
            for dvc in range(8):
                for kcl in range(4):
                    st = first[dvc]
                    first[dvc] = False
                    sp = (kb == 7 and kcl == 3)
                    last_of_kb = (dvc == 7 and kcl == 3)
                    dd = [vld, at_ready]
                    if st and bank_tok[dvc] is not None:
                        dd.append(bank_tok[dvc])
                        bank_tok[dvc] = None
                    def em(E, dvc=dvc, kcl=kcl, kb=kb, st=st, sp=sp):
                        return E.matmul(PS[:, dvc * 512:dvc * 512 + 256],
                                        VB1[:, kcl * 1024 + dvc * 128: kcl * 1024 + dvc * 128 + 128],
                                        AT1[:, (4 * kb + kcl) * 256:(4 * kb + kcl + 1) * 256],
                                        start=st, stop=sp)
                    if sp or last_of_kb:
                        tokkb = b.step("t", "t", em, deps=dd)
                        if sp:
                            avtoks[dvc] = tokkb
                        vfree = tokkb
                    else:
                        b.run("t", em, deps=dd)
        for dvc in range(8):
            if avtoks[dvc] is None:
                avtoks[dvc] = vfree
        atfree = vfree
        qfree = vfree
        for dvc in range(8):
            evk = b.step("v", "v", lambda E, dvc=dvc, qb1=qb1: E.tensor_copy(
                af[:, dvc * 1024 + qb1 * 256: dvc * 1024 + (qb1 + 1) * 256],
                PS[:, dvc * 512:dvc * 512 + 256]), deps=[avtoks[dvc]])
            bank_tok[dvc] = evk

    b.barrier()
    es_p10.close()
    for i in range(8):
        bank_tok[i] = None

    if _STOP_AFTER == "p10":
        return _finish(nc, es, b, dbg_copies)

    es_p11 = ExitStack()
    rmat = sbt(es_p11, "rmat", [8, 512])
    t_rm = load("d_c", rmat[:], rmat_in[:, :])
    wtm = sbt(es_p11, "wtm", [128, 64])
    wtm2 = sbt(es_p11, "wtm2", [128, 64])
    PB = sbt(es_p11, "p11PB", [128, 40])
    wfm = sbt(es_p11, "wfm", [8, 1024])
    wexp = sbt(es_l1, "wexp", [128, 4096], side="right")
    evk = None
    for g2 in range(2):
        bank = g2 % 2
        blocks = [(glog[0:8, (4 * g2 + i) * 128:(4 * g2 + i + 1) * 128], i * 8)
                  for i in range(4)]
        tp = transpose_group(bank, blocks, deps=[])
        evk = b.step("v", "v", lambda E, bk=bank, g2=g2: E.tensor_copy(
            wtm[:, 32 * g2:32 * g2 + 32], PS[:, 512 * bk:512 * bk + 32]), deps=[tp])
        bank_tok[bank] = evk
    wlast = None
    for tc in range(8):
        lg = wtm[:, 8 * tc:8 * tc + 8]
        t1 = b.step("v", "v", lambda E, lg=lg: E.reduce_max(sstat[:, 8:9], lg, AXX),
                    deps=[evk])
        t2 = b.step("v", "v", lambda E, lg=lg: E.tensor_scalar(
            PB[:, 0:8], lg, sstat[:, 8:9], None, ALU.is_equal), deps=[t1])
        t3 = b.step("v", "v", lambda E, lg=lg: E.scalar_tensor_tensor(
            PB[:, 8:16], PB[:, 0:8], -1e30, lg, ALU.mult, ALU.add), deps=[t2])
        t4 = b.step("v", "v", lambda E: E.reduce_max(sstat[:, 9:10], PB[:, 8:16], AXX),
                    deps=[t3])
        t5 = b.step("v", "v", lambda E: E.tensor_scalar(
            sstat[:, 10:11], sstat[:, 8:9], -1.0, None, ALU.mult), deps=[t4])
        t6 = b.step("s", "s", lambda E, lg=lg: E.activation(
            PB[:, 16:24], lg, AF.Exp, bias=sstat[:, 10:11], scale=1.0), deps=[t5])
        t7 = b.step("s", "s", lambda E: E.activation(
            sstat[:, 11:12], sstat[:, 9:10], AF.Exp, bias=sstat[:, 10:11], scale=1.0),
            deps=[t6])
        t8 = b.step("v", "v", lambda E: E.tensor_scalar(
            sstat[:, 12:13], sstat[:, 11:12], 1.0, None, ALU.add), deps=[t7])
        t9 = b.step("v", "v", lambda E: E.reciprocal(sstat[:, 13:14], sstat[:, 12:13]),
                    deps=[t8])
        ta = b.step("v", "v", lambda E, lg=lg: E.tensor_scalar(
            PB[:, 24:32], lg, sstat[:, 9:10], None, ALU.is_ge), deps=[t9])
        tb = b.step("v", "v", lambda E: E.tensor_tensor(
            PB[:, 32:40], PB[:, 16:24], PB[:, 24:32], ALU.mult), deps=[ta, t6])
        wlast = b.step("v", "v", lambda E, tc=tc: E.tensor_scalar(
            wtm2[:, 8 * tc:8 * tc + 8], PB[:, 32:40], sstat[:, 13:14], None, ALU.mult),
            deps=[tb])
    for g2 in range(2):
        bank = 2 + g2
        blocks = [(wtm2[:, (4 * g2 + i) * 8:(4 * g2 + i + 1) * 8], i * 128)
                  for i in range(4)]
        tp = transpose_group(bank, blocks, deps=[wlast])
        evk = b.step("v", "v", lambda E, bk=bank, g2=g2: E.tensor_copy(
            wfm[0:8, 512 * g2:512 * g2 + 512], PS[0:8, 512 * bk:512 * bk + 512]),
            deps=[tp])
        bank_tok[bank] = evk
    wu_toks = []
    for erc in range(4):
        for ti1 in range(2):
            bank = 4 + ((2 * erc + ti1) % 4)
            def em(E, st, sp, erc=erc, ti1=ti1, bk=bank):
                return E.matmul(psb(bk), rmat[0:8, erc * 128:(erc + 1) * 128],
                                wfm[0:8, ti1 * 512:(ti1 + 1) * 512], start=st, stop=sp)
            mmt = mm_group(bank, [em], [evk])
            wek = b.step("v", "v", lambda E, bk=bank, erc=erc, ti1=ti1: E.tensor_copy(
                wexp[:, erc * 1024 + ti1 * 512: erc * 1024 + (ti1 + 1) * 512], psb(bk)),
                deps=[mmt])
            bank_tok[bank] = wek
            wu_toks.append(b.step("v", "v", lambda E, erc=erc, ti1=ti1: E.tensor_tensor(
                uall[:, erc * 1024 + ti1 * 512: erc * 1024 + (ti1 + 1) * 512],
                uall[:, erc * 1024 + ti1 * 512: erc * 1024 + (ti1 + 1) * 512],
                wexp[:, erc * 1024 + ti1 * 512: erc * 1024 + (ti1 + 1) * 512],
                ALU.mult), deps=[wek]))

    b.barrier()
    es_p11.close()
    for i in range(8):
        bank_tok[i] = None

    if _STOP_AFTER == "p11":
        return _finish(nc, es, b, dbg_copies)

    es_p12 = ExitStack()
    W1S = sbt(es_p12, "p12W1", [128, 8192])
    BTSL = sbt(es_p12, "p12BT", [128, 4096])
    X1T = sbt(es_p12, "p12X1", [128, 1024])
    EVF = sbt(es_p12, "p12EV", [128, 2048])
    s_bo = sbt(es_p12, "p12bo", [1, D])
    t_bo = load("d_c", s_bo[:], bo8[:, :])
    w1_tok = None
    bt_tok = None
    x1_free = [None]
    evf_free = [None, None]
    out_toks = []
    p12_last = [None]
    for dq in range(4):
        w1_tok = load("d_w1s", W1S[:, :].rearrange("p (c f) -> p c f", f=1024),
                      wo1.rearrange("(c p) d -> p c d", p=128)[:, :, dq * 1024:(dq + 1) * 1024],
                      deps=[w1_tok, p12_last[0]])
        bt_tok = load("d_bts", BTSL[:, :].rearrange("p (c f) -> p c f", f=1024),
                      BTS.rearrange("(c p) d -> p c d", p=128)[:, :, dq * 1024:(dq + 1) * 1024],
                      deps=[bt_tok, p12_last[0]])
        for tc in range(8):
            x1t = X1T[:, 0:1024]
            x1ld = load("d_x1t", x1t, X1S[tc * 128:(tc + 1) * 128,
                                          dq * 1024:(dq + 1) * 1024],
                        deps=[x1_free[0]])
            b0 = 2 * (tc % 4)
            gtoks = []
            for nb in range(2):
                bk = b0 + nb
                ems = []
                if _P12_PARTS & 1:
                    for dvc in range(8):
                        def em(E, st, sp, dvc=dvc, nb=nb, tc=tc, bk=bk):
                            return E.matmul(psb(bk),
                                            af[:, dvc * 1024 + tc * 128: dvc * 1024 + tc * 128 + 128],
                                            W1S[:, dvc * 1024 + nb * 512: dvc * 1024 + nb * 512 + 512],
                                            start=st, stop=sp)
                        ems.append(em)
                if _P12_PARTS & 2:
                    for erc in range(4):
                        def em2(E, st, sp, erc=erc, nb=nb, tc=tc, bk=bk):
                            return E.matmul(psb(bk),
                                            uall[:, erc * 1024 + tc * 128: erc * 1024 + tc * 128 + 128],
                                            BTSL[:, erc * 1024 + nb * 512: erc * 1024 + nb * 512 + 512],
                                            start=st, stop=sp)
                        ems.append(em2)
                if _P12_PARTS & 4:
                    def em3(E, st, sp, nb=nb, dq=dq, bk=bk):
                        return E.matmul(psb(bk), ones[0:1, 0:128],
                                        s_bo[0:1, dq * 1024 + nb * 512: dq * 1024 + nb * 512 + 512],
                                        start=st, stop=sp)
                    ems.append(em3)
                gt = mm_group(bk, ems, [w1_tok, bt_tok, t_bo] + wu_toks)
                gtoks.append(gt)
            sl = tc % 2
            stg = EVF[:, sl * 1024:(sl + 1) * 1024]
            evk = b.step("v", "v", lambda E, st=stg, b0=b0, xx=x1t: E.tensor_tensor(
                st, PS[:, b0 * 512: b0 * 512 + 1024], xx, ALU.add),
                deps=gtoks + [x1ld, evf_free[sl]])
            for nb in range(2):
                bank_tok[b0 + nb] = evk
            x1_free[0] = evk
            out_toks.append(store("d_out", yout[tc * 128:(tc + 1) * 128,
                                                dq * 1024:(dq + 1) * 1024], stg,
                                  deps=[evk]))
            evf_free[sl] = out_toks[-1]
            p12_last[0] = gtoks[-1]
    for t in out_toks:
        b.wait("y", t)
        b.wait("g", t)
    b.barrier()
    es_p12.close()
    es_l1.close()
    return _finish(nc, es, b, dbg_copies)


def _finish(nc, es, b, dbg_copies=()):
    b.barrier()
    for (src_ap, dst_ap, shape) in dbg_copies:
        nrow = shape[0]
        step = max(1, min(nrow, (4 << 20) // max(1, int(np.prod(shape[1:])) * 4)))
        for r0 in range(0, nrow, step):
            r1 = min(nrow, r0 + step)
            b.step("g", "d_dbg", lambda E, s=src_ap[r0:r1], d=dst_ap[r0:r1]:
                   E.dma_start(out=d, in_=s), dma=True)
    b.barrier()
    with nc.Block() as block:
        @block.tensor
        def _(E):
            for f in b.ops["t"]:
                f(E)

        @block.vector
        def _(E):
            for f in b.ops["v"]:
                f(E)

        @block.scalar
        def _(E):
            for f in b.ops["s"]:
                f(E)

        @block.gpsimd
        def _(E):
            for f in b.ops["g"]:
                f(E)

        @block.sync
        def _(E):
            for f in b.ops["y"]:
                f(E)
    es.close()
    return nc


def kernel(**inputs):
    x = np.ascontiguousarray(np.asarray(inputs["L0_attn_x"], dtype=np.float32)).reshape(TT, D)
    xs = [x[0:S], x[S:TT]]
    xall_per_seq = [np.ascontiguousarray(np.concatenate([xs[b_], xs[1 - b_]], axis=0))
                    for b_ in range(2)]
    wq_full = np.asarray(inputs["L0_attn_Wq"], dtype=np.float32)
    wk_full = np.asarray(inputs["L0_attn_Wk"], dtype=np.float32)
    wv_full = np.asarray(inputs["L0_attn_Wv"], dtype=np.float32)
    wo_full = np.asarray(inputs["L0_attn_Wo"], dtype=np.float32)
    cu = np.asarray(inputs["L0_ffn_w_up_codes"], dtype=np.int32).reshape(11008, D)
    au = np.asarray(inputs["L0_ffn_w_up_absmax"], dtype=np.float32).reshape(11008, 64)
    bu = np.asarray(inputs["L0_ffn_b_up"], dtype=np.float32)
    lau = np.asarray(inputs["L0_ffn_w_up_lora_a"], dtype=np.float32)
    lbu = np.asarray(inputs["L0_ffn_w_up_lora_b"], dtype=np.float32)
    cd = np.asarray(inputs["L0_ffn_w_down_codes"], dtype=np.int32).reshape(D, 11008)
    ad = np.asarray(inputs["L0_ffn_w_down_absmax"], dtype=np.float32).reshape(D, 172)
    bd = np.asarray(inputs["L0_ffn_b_down"], dtype=np.float32)
    lda_ = np.asarray(inputs["L0_ffn_w_down_lora_a"], dtype=np.float32)
    ldb_ = np.asarray(inputs["L0_ffn_w_down_lora_b"], dtype=np.float32)
    wqkv_ = np.asarray(inputs["L1_attn_Wqkv"], dtype=np.float32)
    bqkv_ = np.asarray(inputs["L1_attn_bqkv"], dtype=np.float32).reshape(24, 128)
    wo1_ = np.asarray(inputs["L1_attn_Wo"], dtype=np.float32)
    bo_ = np.asarray(inputs["L1_attn_bo"], dtype=np.float32).reshape(1, 4096)
    gw_ = np.asarray(inputs["L1_ffn_gate_w"], dtype=np.float32)
    la1_ = np.asarray(inputs["L1_ffn_lora_A"], dtype=np.float32).reshape(512, D)
    lb1_ = np.asarray(inputs["L1_ffn_lora_B"], dtype=np.float32).reshape(8 * D, 64)

    ident = np.eye(128, dtype=np.float32)
    ones_ = np.ones((1, 128), dtype=np.float32)
    rmat_ = np.zeros((8, 512), dtype=np.float32)
    for e in range(8):
        rmat_[e, 64 * e:64 * (e + 1)] = 1.0

    chunks = []
    for c in range(8):
        g = c % 4
        if g < 3:
            chunks.append(list(range(22 * g, 22 * g + 22)))
        else:
            chunks.append(list(range(66, 86)) + [None, None])

    def rows_blk(arr, hlist, blk=128):
        parts = []
        for h in hlist:
            if h is None:
                parts.append(np.zeros((blk,) + arr.shape[1:], dtype=arr.dtype))
            else:
                parts.append(arr[blk * h: blk * (h + 1)])
        return np.ascontiguousarray(np.concatenate(parts, axis=0))

    def cols_blk(arr, hlist, blk):
        parts = []
        for h in hlist:
            if h is None:
                parts.append(np.zeros(arr.shape[:1] + (blk,), dtype=arr.dtype))
            else:
                parts.append(arr[:, blk * h: blk * (h + 1)])
        return np.ascontiguousarray(np.concatenate(parts, axis=1))

    in_maps = []
    for c in range(8):
        kvh = c % 4
        hl = chunks[c]
        bup_rows = []
        for h in hl:
            bup_rows.append(np.zeros(128, dtype=np.float32) if h is None
                            else bu[128 * h:128 * (h + 1)])
        bup_c = np.ascontiguousarray(np.stack(bup_rows, axis=0))
        m = {
            "xall": np.ascontiguousarray(xs[MYSEQ[c]]),
            "xown": np.ascontiguousarray(x[OWN_START[c]:OWN_START[c] + 1024]),
            "wq": np.ascontiguousarray(wq_full[:, 1024 * kvh:1024 * (kvh + 1)]),
            "wk": np.ascontiguousarray(wk_full[:, 128 * kvh:128 * (kvh + 1)]),
            "wv": np.ascontiguousarray(wv_full[:, 128 * kvh:128 * (kvh + 1)]),
            "wo": np.ascontiguousarray(wo_full[1024 * kvh:1024 * (kvh + 1), :]),
            "cup": rows_blk(cu, hl),
            "aup": rows_blk(au, hl),
            "bup": bup_c,
            "laup": lau,
            "lbup": cols_blk(lbu, hl, 128),
            "cdn": cols_blk(cd, hl, 128),
            "adn": cols_blk(ad, hl, 2),
            "ldna": rows_blk(lda_, hl),
            "ldnb": ldb_,
            "bdn8": (bd.reshape(1, 4096) if c % 4 == 0
                     else np.zeros((1, 4096), dtype=np.float32)),
            "wqkv": wqkv_,
            "bqkv": bqkv_,
            "wo1": wo1_,
            "bo8": bo_,
            "gw": gw_,
            "la1": la1_,
            "lb1": lb1_,
            "identI": ident,
            "onesI": ones_,
            "rmatI": rmat_,
        }
        in_maps.append(m)

    nc = build_program()
    res = run_bass_kernel_spmd(nc, in_maps, core_ids=list(range(8)))
    global _LAST_RESULTS
    _LAST_RESULTS = res.results
    out = np.empty((TT, D), dtype=np.float32)
    for c in range(8):
        out[OWN_START[c]:OWN_START[c] + 1024] = np.asarray(res.results[c]["y"],
                                                           dtype=np.float32)
    return out.reshape(2, S, D)
